# version 08
# baseline (speedup 1.0000x reference)
import numpy as np
import ml_dtypes
from contextlib import ExitStack

import concourse.bass as bass
import concourse.mybir as mybir
from concourse.bass_utils import run_bass_kernel_spmd

F32 = mybir.dt.float32
BF16 = mybir.dt.bfloat16
I32 = mybir.dt.int32
AF = mybir.ActivationFunctionType
ALU = mybir.AluOpType
BF16NP = ml_dtypes.bfloat16

NCORE = 8
B, S, DM = 2, 8192, 2048
T = B * S
TLOC = T // NCORE
DI, DST = 4096, 16
DLOC = DI // NCORE
NPC = DLOC // 128
NKC = DM // 128


def mknc():
    return bass.Bass("TRN2", target_bir_lowering=False, debug=False)


class Sem:
    def __init__(self, ctx, nc, name):
        self.h = ctx.enter_context(nc.semaphore(name))
        self.n = 0

    def inc(self, inst, k=1):
        inst.then_inc(self.h, k)
        self.n += k
        return self.n


def build_cast(nc, pairs):
    CW = 8192
    with ExitStack() as ctx:
        stg = [ctx.enter_context(nc.sbuf_tensor(f"cstg{i}", [128, CW], F32)) for i in range(2)]
        out = [ctx.enter_context(nc.sbuf_tensor(f"cout{i}", [128, CW], BF16)) for i in range(2)]
        ld = Sem(ctx, nc, "c_ld")
        cv = Sem(ctx, nc, "c_cv")
        st = Sem(ctx, nc, "c_st")
        jobs = []
        for (src, dst, rows, cols) in pairs:
            for r0 in range(0, rows, 128):
                for c0 in range(0, cols, CW):
                    jobs.append((src, dst, r0, c0, min(CW, cols - c0)))
        with nc.Block() as blk:
            @blk.sync
            def _(sync):
                for i, (src, dst, r0, c0, w) in enumerate(jobs):
                    bi = i % 2
                    if i >= 2:
                        sync.wait_ge(cv.h, i - 1)
                    sync.dma_start(out=stg[bi][:, 0:w], in_=src[r0:r0+128, c0:c0+w]).then_inc(ld.h, 16)
                    ld.n += 16

            @blk.vector
            def _(vector):
                for i, (src, dst, r0, c0, w) in enumerate(jobs):
                    bi = i % 2
                    vector.wait_ge(ld.h, 16 * (i + 1))
                    if i >= 2:
                        vector.wait_ge(st.h, 16 * (i - 1))
                    cv.inc(vector.tensor_copy(out=out[bi][:, 0:w], in_=stg[bi][:, 0:w]))

            @blk.gpsimd
            def _(gp):
                for i, (src, dst, r0, c0, w) in enumerate(jobs):
                    bi = i % 2
                    gp.wait_ge(cv.h, i + 1)
                    gp.dma_start(out=dst[r0:r0+128, c0:c0+w], in_=out[bi][:, 0:w]).then_inc(st.h, 16)
                    st.n += 16
                gp.wait_ge(st.h, st.n)


def build_a1(nc, io):
    TC = 512
    nch = T // TC
    with ExitStack() as ctx:
        sb = lambda *a: ctx.enter_context(nc.sbuf_tensor(*a))
        ps = lambda *a: ctx.enter_context(nc.psum_tensor(*a))
        w_in_bf = sb("w_in_bf", [128, NKC * 1024], BF16)
        wx_bf = sb("wx_bf", [128, NPC * 80], BF16)
        conv_w = sb("conv_wT", [128, NPC * 4], F32)
        conv_b = sb("conv_bT", [128, NPC], F32)
        stg = sb("stgT", [128, NKC * 1024], F32)
        rhs = [sb(f"rhsT{i}", [128, NKC * TC], BF16) for i in range(2)]
        xs = sb("xsT", [128, NPC * (TC + 3)], F32)
        cacc = sb("caccT", [128, NPC * TC], F32)
        xcv = sb("xcvT", [128, NPC * TC], BF16)
        szt = sb("sztT", [128, NPC * TC], BF16)
        p_sb = sb("p_sbT", [80, TC], F32)
        pm = [ps(f"pmT{i}", [128, TC], F32) for i in range(4)]
        pp = ps("ppT", [80, TC], F32)

        ld = Sem(ctx, nc, "a_ld")
        mm = Sem(ctx, nc, "a_mm")
        convd = Sem(ctx, nc, "a_convd")
        silud = Sem(ctx, nc, "a_silud")
        szd = Sem(ctx, nc, "a_szd")
        pevd = Sem(ctx, nc, "a_pevd")
        st = Sem(ctx, nc, "a_st")

        with nc.Block() as blk:
            @blk.sync
            def _(sync):
                for kc in range(NKC):
                    sync.dma_start(out=stg[:, kc*1024:(kc+1)*1024],
                                   in_=io["w_in_T"][kc*128:(kc+1)*128, :]).then_inc(ld.h, 16)
                for pc in range(NPC):
                    sync.dma_start(out=conv_w[:, pc*4:(pc+1)*4],
                                   in_=io["conv_w"][pc*128:(pc+1)*128, :]).then_inc(ld.h, 16)
                    sync.dma_start(out=conv_b[:, pc:pc+1],
                                   in_=io["conv_b"][pc*128:(pc+1)*128, :]).then_inc(ld.h, 16)
                ld.n += (NKC + 2 * NPC) * 16

            @blk.vector
            def _(vector):
                vector.wait_ge(ld.h, (NKC + 2 * NPC) * 16)
                vector.tensor_copy(out=w_in_bf[:], in_=stg[:])
                vector.memset(xs[:], 0.0)
                vector.memset(wx_bf[:], 0.0)

        with nc.Block() as blk:
            @blk.sync
            def _(sync):
                for pc in range(NPC):
                    sync.dma_start(out=stg[:, pc*33:(pc+1)*33], in_=io["wx_T"][pc*128:(pc+1)*128, :]).then_inc(ld.h, 16)
                ld.n += NPC * 16
                for kc in range(NKC):
                    sync.dma_start(out=rhs[0][:, kc*TC:(kc+1)*TC], in_=io["xT_bf"][kc, :, 0:TC]).then_inc(ld.h, 16)
                ld.n += NKC * 16

            @blk.vector
            def _(vector):
                vector.wait_ge(ld.h, (NKC + 2 * NPC + NPC) * 16)
                for pc in range(NPC):
                    vector.tensor_copy(out=wx_bf[:, pc*80+0:pc*80+16], in_=stg[:, pc*33:pc*33+16])
                    vector.tensor_copy(out=wx_bf[:, pc*80+32:pc*80+48], in_=stg[:, pc*33+16:pc*33+32])
                    vector.tensor_copy(out=wx_bf[:, pc*80+64:pc*80+65], in_=stg[:, pc*33+32:pc*33+33])

        base_ld = ld.n
        LDC = NKC * 16
        for ci in range(nch):
            first_in_b = (ci * TC) % S == 0
            cur = rhs[ci % 2]
            nxt = rhs[(ci + 1) % 2]
            t0 = ci * TC
            with nc.Block() as blk:
                @blk.sync
                def _(sync, ci=ci, nxt=nxt, t0=t0):
                    if ci + 1 < nch:
                        for kc in range(NKC):
                            sync.dma_start(out=nxt[:, kc*TC:(kc+1)*TC],
                                           in_=io["xT_bf"][kc, :, t0+TC:t0+2*TC]).then_inc(ld.h, 16)
                        ld.n += LDC

                @blk.tensor
                def _(tensor, ci=ci, cur=cur):
                    tensor.wait_ge(ld.h, base_ld + LDC * ci)
                    for mt in range(4):
                        for kc in range(NKC):
                            inst = tensor.matmul(
                                out=pm[mt][:],
                                lhsT=w_in_bf[:, kc*1024 + mt*128: kc*1024 + (mt+1)*128],
                                rhs=cur[:, kc*TC:(kc+1)*TC],
                                start=(kc == 0), stop=(kc == NKC - 1))
                        mm.inc(inst)
                    tensor.wait_ge(convd.h, ci + 1)
                    for mt in range(4):
                        for kc in range(NKC):
                            inst = tensor.matmul(
                                out=pm[mt][:],
                                lhsT=w_in_bf[:, kc*1024 + (4+mt)*128: kc*1024 + (5+mt)*128],
                                rhs=cur[:, kc*TC:(kc+1)*TC],
                                start=(kc == 0), stop=(kc == NKC - 1))
                        mm.inc(inst)
                    tensor.wait_ge(silud.h, ci + 1)
                    tensor.wait_ge(pevd.h, ci)
                    for pc in range(NPC):
                        inst = tensor.matmul(out=pp[:], lhsT=wx_bf[:, pc*80:(pc+1)*80],
                                             rhs=xcv[:, pc*TC:(pc+1)*TC],
                                             start=(pc == 0), stop=(pc == NPC - 1))
                    mm.inc(inst)

                @blk.vector
                def _(vector, ci=ci, first_in_b=first_in_b):
                    mm0 = 9 * ci
                    for pc in range(NPC):
                        vector.wait_ge(mm.h, mm0 + pc + 1)
                        o = pc * (TC + 3)
                        if first_in_b:
                            vector.memset(xs[:, o:o+3], 0.0)
                        vector.tensor_copy(out=xs[:, o+3:o+3+TC], in_=pm[pc][:])
                        vector.tensor_scalar(out=cacc[:, pc*TC:(pc+1)*TC], in0=xs[:, o:o+TC],
                                             scalar1=conv_w[:, pc*4:pc*4+1], scalar2=None, op0=ALU.mult)
                        for j in (1, 2, 3):
                            vector.scalar_tensor_tensor(
                                out=cacc[:, pc*TC:(pc+1)*TC], in0=xs[:, o+j:o+j+TC],
                                scalar=conv_w[:, pc*4+j:pc*4+j+1], in1=cacc[:, pc*TC:(pc+1)*TC],
                                op0=ALU.mult, op1=ALU.add)
                        inst = vector.tensor_copy(out=xs[:, o:o+3], in_=xs[:, o+TC:o+TC+3])
                    convd.inc(inst)
                    vector.wait_ge(mm.h, mm0 + 9)
                    pevd.inc(vector.tensor_copy(out=p_sb[:], in_=pp[:]))

                @blk.scalar
                def _(scalar, ci=ci):
                    scalar.wait_ge(convd.h, ci + 1)
                    scalar.wait_ge(st.h, 16 * 9 * ci)
                    for pc in range(NPC):
                        inst = scalar.activation(out=xcv[:, pc*TC:(pc+1)*TC], in_=cacc[:, pc*TC:(pc+1)*TC],
                                                 func=AF.Silu, bias=conv_b[:, pc:pc+1])
                    silud.inc(inst)
                    scalar.wait_ge(mm.h, 9 * ci + 8)
                    for pc in range(NPC):
                        inst = scalar.activation(out=szt[:, pc*TC:(pc+1)*TC], in_=pm[pc][:], func=AF.Silu)
                    szd.inc(inst)

                @blk.gpsimd
                def _(gp, ci=ci, t0=t0):
                    gp.wait_ge(silud.h, ci + 1)
                    for pc in range(NPC):
                        gp.dma_start(out=io["x_conv"][pc, :, t0:t0+TC], in_=xcv[:, pc*TC:(pc+1)*TC]).then_inc(st.h, 16)
                    gp.wait_ge(szd.h, ci + 1)
                    for pc in range(NPC):
                        gp.dma_start(out=io["sz"][pc, :, t0:t0+TC], in_=szt[:, pc*TC:(pc+1)*TC]).then_inc(st.h, 16)
                    gp.wait_ge(pevd.h, ci + 1)
                    gp.dma_start(out=io["p_part"][:, t0:t0+TC], in_=p_sb[:]).then_inc(st.h, 16)
                    st.n += 9 * 16

        with nc.Block() as blk:
            @blk.gpsimd
            def _(gp):
                gp.wait_ge(st.h, st.n)

def _f32(a):
    return np.ascontiguousarray(a, dtype=np.float32)


def host_xT(x):
    return _f32(x.transpose(2, 0, 1).reshape(DM, T))


def kernel(**inputs):
    raise NotImplementedError("assembled later")


def build_a3(nc, io, nch_limit=None, dbg=None):
    L = 512
    nch = nch_limit or (T // L)
    NS = DST
    with ExitStack() as ctx:
        sb = lambda *a: ctx.enter_context(nc.sbuf_tensor(*a))
        ps = lambda *a: ctx.enter_context(nc.psum_tensor(*a))
        A_neg = sb("A_neg", [128, NPC * NS], F32)
        Dv = sb("DvS", [128, NPC], F32)
        mdt = sb("mdtS", [1, 1], F32)
        ones_bf = sb("ones_bfS", [1, 128], BF16)
        ones_col = sb("ones_colS", [128, 1], BF16)
        p_stage = sb("p_stageS", [80, 8 * L], F32)
        sel_i = sb("sel_iS", [16, 16 * 128], I32)
        sel = sb("selS", [16, 16 * 128], BF16)
        p8 = sb("p8S", [80, L], F32)
        sp1 = sb("sp1S", [1, L], F32)
        sp2 = sb("sp2S", [1, L], F32)
        dt_row = sb("dt_rowS", [1, L], F32)
        dt_bf = sb("dt_bfS", [1, L], BF16)
        dt16_sb = sb("dt16_sbS", [16, L], F32)
        w16bf = sb("w16bfS", [16, L], BF16)
        c16bf = sb("c16bfS", [16, L], BF16)
        B128 = [sb(f"B128_{q}", [128, NS * L], BF16) for q in range(2)]
        C128 = [sb(f"C128_{q}", [128, NS * L], BF16) for q in range(2)]
        dt128 = [sb(f"dt128_{q}", [128, L], F32) for q in range(2)]
        xcv = [sb(f"xcv3{q}", [128, NPC * L], BF16) for q in range(2)]
        szt = [sb(f"szt3{q}", [128, NPC * L], BF16) for q in range(2)]
        xd = sb("xdS", [128, NPC * L], F32)
        ab = sb("abS", [128, 4 * L], F32)
        us = sb("usS", [128, L], F32)
        hcs = sb("hcsS", [128, L], F32)
        h_pc = sb("h_pcS", [128, NS * L], F32)
        hcar = [sb(f"hcar{q}", [128, NPC * NS], F32) for q in range(2)]
        y_acc = sb("y_accS", [128, NPC * L], F32)
        ypbf = [sb(f"ypbf{q}", [128, NPC * L], BF16) for q in range(2)]
        pmb = [ps(f"pmbS{i}", [128, 512], F32) for i in range(4)]
        pm16 = ps("pm16S", [16, 512], F32)

        ld = Sem(ctx, nc, "s_ld")
        pltree = Sem(ctx, nc, "s_pltree")
        plc = Sem(ctx, nc, "s_plc")
        plw = Sem(ctx, nc, "s_plw")
        pldt = Sem(ctx, nc, "s_pldt")
        s4 = Sem(ctx, nc, "s_s4")
        sevdt = Sem(ctx, nc, "s_sevdt")
        pm16d = Sem(ctx, nc, "s_pm16d")
        pmm = Sem(ctx, nc, "s_pmm")
        sev = Sem(ctx, nc, "s_sev")
        sab = Sem(ctx, nc, "s_sab")
        dvs = Sem(ctx, nc, "s_dvs")
        dvy = Sem(ctx, nc, "s_dvy")
        st = Sem(ctx, nc, "s_st")

        with nc.Block() as blk:
            @blk.sync
            def _(sync):
                for pc in range(NPC):
                    sync.dma_start(out=A_neg[:, pc*NS:(pc+1)*NS],
                                   in_=io["A_log"][pc*128:(pc+1)*128, :]).then_inc(ld.h, 16)
                    sync.dma_start(out=Dv[:, pc:pc+1],
                                   in_=io["Dvec"][pc*128:(pc+1)*128, :]).then_inc(ld.h, 16)
                sync.dma_start(out=p_stage[0:1, 0:DI], in_=io["dt_bias"][:]).then_inc(ld.h, 16)
                ld.n += (2 * NPC + 1) * 16

            @blk.scalar
            def _(scalar):
                scalar.wait_ge(ld.h, ld.n)
                scalar.activation(out=A_neg[:], in_=A_neg[:], func=AF.Exp)

            @blk.vector
            def _(vector):
                vector.wait_ge(ld.h, ld.n)
                vector.memset(ones_bf[:], 1.0)
                vector.memset(ones_col[:], 1.0)
                vector.tensor_scalar(out=p_stage[32:33, 0:DI], in0=p_stage[0:1, 0:DI],
                                     scalar1=1.0 / 4096.0, scalar2=0.0, op0=ALU.mult, op1=ALU.add,
                                     accum_out=mdt[:])

            @blk.gpsimd
            def _(gp):
                gp.iota(sel_i[:], pattern=[[1, 16], [0, 128]], base=0, channel_multiplier=-1)

        with nc.Block() as blk:
            @blk.vector
            def _(vector):
                vector.tensor_scalar(out=A_neg[:], in0=A_neg[:], scalar1=-1.0, scalar2=None, op0=ALU.mult)
                vector.tensor_single_scalar(out=sel[:], in_=sel_i[:], scalar=0, op=ALU.is_equal)



        def meta_loads(sync, m):
            q = m % 2
            t0 = m * L
            sync.dma_start(out=p_stage[0:16, :],
                           in_=io["p_all"][:, 0:16, t0:t0+L].transpose([1, 0, 2])).then_inc(ld.h, 16)
            sync.dma_start(out=p_stage[32:48, :],
                           in_=io["p_all"][:, 32:48, t0:t0+L].transpose([1, 0, 2])).then_inc(ld.h, 16)
            sync.dma_start(out=p_stage[64:65, :],
                           in_=io["p_all"][:, 64:65, t0:t0+L].transpose([1, 0, 2])).then_inc(ld.h, 16)
            for pc in range(NPC):
                sync.dma_start(out=xcv[q][:, pc*L:(pc+1)*L], in_=io["x_conv"][pc, :, t0:t0+L]).then_inc(ld.h, 16)
                sync.dma_start(out=szt[q][:, pc*L:(pc+1)*L], in_=io["sz"][pc, :, t0:t0+L]).then_inc(ld.h, 16)
            ld.n += 11 * 16

        def meta_pool(gp, m, ld_base):
            q = m % 2
            gp.wait_ge(ld.h, ld_base + 3 * 16)
            gp.tensor_tensor(out=p_stage[:, 0:4*L], in0=p_stage[:, 0:4*L],
                             in1=p_stage[:, 4*L:8*L], op=ALU.add)
            gp.tensor_tensor(out=p_stage[:, 0:2*L], in0=p_stage[:, 0:2*L],
                             in1=p_stage[:, 2*L:4*L], op=ALU.add)
            pltree.inc(gp.tensor_tensor(out=p8[:], in0=p_stage[:, 0:L],
                                        in1=p_stage[:, L:2*L], op=ALU.add))
            plc.inc(gp.tensor_copy(out=c16bf[:], in_=p8[32:48, :]))
            gp.wait_ge(s4.h, m + 1)
            gp.tensor_tensor(out=dt_row[:], in0=sp1[:], in1=sp2[:], op=ALU.add)
            pldt.inc(gp.tensor_copy(out=dt_bf[:], in_=dt_row[:]))
            gp.wait_ge(sevdt.h, m + 1)
            plw.inc(gp.tensor_tensor(out=w16bf[:], in0=p8[0:16, :], in1=dt16_sb[:], op=ALU.mult))

        def meta_scalar_dt(scalar, m):
            scalar.wait_ge(pltree.h, m + 1)
            scalar.activation(out=sp1[:], in_=p8[64:65, :], func=AF.Abs, bias=mdt[0:1, 0:1])
            scalar.activation(out=sp1[:], in_=sp1[:], func=AF.Exp, scale=-1.0)
            scalar.activation(out=sp1[:], in_=sp1[:], func=AF.Ln, bias=1.0)
            s4.inc(scalar.activation(out=sp2[:], in_=p8[64:65, :], func=AF.Relu, bias=mdt[0:1, 0:1]))

        def meta_pe(tensor, m):
            q = m % 2
            tensor.wait_ge(pldt.h, m + 1)
            pm16d.inc(tensor.matmul(out=pm16[:, 0:L], lhsT=ones_bf[:, 0:16], rhs=dt_bf[:],
                                    start=True, stop=True))
            r = 0
            if 33 * m + r >= 4:
                tensor.wait_ge(sev.h, 33 * m + r - 3)
            pmm.inc(tensor.matmul(out=pmb[r % 4][:], lhsT=ones_bf[:, 0:128], rhs=dt_bf[:],
                                  start=True, stop=True))
            r += 1
            for rowsrc, gate in ((w16bf, plw), (c16bf, plc)):
                tensor.wait_ge(gate.h, m + 1)
                for i in range(NS):
                    if 33 * m + r >= 4:
                        tensor.wait_ge(sev.h, 33 * m + r - 3)
                    pmm.inc(tensor.matmul(out=pmb[r % 4][:], lhsT=sel[:, i*128:(i+1)*128],
                                          rhs=rowsrc[:], start=True, stop=True))
                    r += 1

        _EVAC_RANGES = {1: range(0, 11), 2: range(11, 22), 3: range(22, 32)}

        def meta_scalar_evac(scalar, m, phase):
            q = m % 2
            if phase == 0:
                scalar.wait_ge(pm16d.h, m + 1)
                sevdt.inc(scalar.copy(out=dt16_sb[:], in_=pm16[:, 0:L]))
                scalar.wait_ge(pmm.h, 33 * m + 1)
                sev.inc(scalar.copy(out=dt128[q][:], in_=pmb[0][:]))
            else:
                for j in _EVAC_RANGES[phase]:
                    r = 1 + j
                    dst = B128[q][:, j*L:(j+1)*L] if j < NS else C128[q][:, (j-NS)*L:(j-NS+1)*L]
                    scalar.wait_ge(pmm.h, 33 * m + r + 1)
                    sev.inc(scalar.copy(out=dst, in_=pmb[r % 4][:]))

        ld_base0 = ld.n
        with nc.Block() as blk:
            @blk.sync
            def _(sync):
                meta_loads(sync, 0)

            @blk.gpsimd
            def _(gp):
                meta_pool(gp, 0, ld_base0)

            @blk.scalar
            def _(scalar):
                meta_scalar_dt(scalar, 0)
                for phase in range(4):
                    meta_scalar_evac(scalar, 0, phase)


            @blk.tensor
            def _(tensor):
                meta_pe(tensor, 0)

        for ci in range(nch):
            q = ci % 2
            first_in_b = (ci * L) % S == 0
            m = ci + 1
            ld_base = ld.n
            with nc.Block() as blk:
                @blk.sync
                def _(sync, ci=ci, m=m, ld_base=ld_base):
                    if m < nch:
                        meta_loads(sync, m)
                    t0 = ci * L
                    sync.wait_ge(dvy.h, 4 * (ci + 1))
                    for pc in range(NPC):
                        sync.dma_start(out=io["y_pre"][pc, :, t0:t0+L],
                                       in_=ypbf[ci % 2][:, pc*L:(pc+1)*L]).then_inc(st.h, 16)
                    st.n += 64

                @blk.gpsimd
                def _(gp, m=m, ld_base=ld_base):
                    if m < nch:
                        meta_pool(gp, m, ld_base)

                @blk.scalar
                def _(scalar, ci=ci, q=q, m=m):
                    for pc in range(NPC):
                        for s in range(NS):
                            j = pc * NS + s
                            g = 64 * ci + j
                            if g >= 4:
                                scalar.wait_ge(dvs.h, g - 3)
                            sab.inc(scalar.activation(out=ab[:, (j % 4)*L:(j % 4 + 1)*L],
                                                      in_=dt128[q][:], func=AF.Exp,
                                                      scale=A_neg[:, j:j+1]))
                        if m < nch:
                            if pc == 0:
                                meta_scalar_dt(scalar, m)
                                meta_scalar_evac(scalar, m, 0)
                            else:
                                meta_scalar_evac(scalar, m, pc)

                @blk.tensor
                def _(tensor, m=m):
                    if m < nch:
                        meta_pe(tensor, m)

                @blk.vector
                def _(vector, ci=ci, q=q, first_in_b=first_in_b):
                    if ci >= 2:
                        vector.wait_ge(st.h, 64 * (ci - 1))
                    for pc in range(NPC):
                        for s in range(NS):
                            j = pc * NS + s
                            g = 64 * ci + j
                            vector.wait_ge(sab.h, g + 1)
                            vector.tensor_tensor(out=us[:], in0=xcv[q][:, pc*L:(pc+1)*L],
                                                 in1=B128[q][:, s*L:(s+1)*L], op=ALU.mult)
                            init = 0.0 if first_in_b else hcar[1 - q][:, pc*NS+s:pc*NS+s+1]
                            dvs.inc(vector.tensor_tensor_scan(
                                out=h_pc[:, s*L:(s+1)*L], data0=ab[:, (j % 4)*L:(j % 4 + 1)*L],
                                data1=us[:], initial=init, op0=ALU.mult, op1=ALU.add))
                            if s == 0:
                                vector.tensor_tensor(out=y_acc[:, pc*L:(pc+1)*L],
                                                     in0=h_pc[:, 0:L],
                                                     in1=C128[q][:, 0:L], op=ALU.mult)
                            else:
                                vector.tensor_tensor(out=hcs[:], in0=h_pc[:, s*L:(s+1)*L],
                                                     in1=C128[q][:, s*L:(s+1)*L], op=ALU.mult)
                                vector.tensor_tensor(out=y_acc[:, pc*L:(pc+1)*L],
                                                     in0=y_acc[:, pc*L:(pc+1)*L],
                                                     in1=hcs[:], op=ALU.add)
                        hv = h_pc[:].rearrange("p (s l) -> p s l", l=L)
                        vector.tensor_copy(out=hcar[q][:, pc*NS:(pc+1)*NS],
                                           in_=hv[:, :, L-1:L])
                        vector.scalar_tensor_tensor(out=xd[:, pc*L:(pc+1)*L],
                                                    in0=xcv[q][:, pc*L:(pc+1)*L],
                                                    scalar=Dv[:, pc:pc+1],
                                                    in1=y_acc[:, pc*L:(pc+1)*L],
                                                    op0=ALU.mult, op1=ALU.add)
                        dvy.inc(vector.tensor_tensor(out=ypbf[ci % 2][:, pc*L:(pc+1)*L],
                                                     in0=xd[:, pc*L:(pc+1)*L],
                                                     in1=szt[q][:, pc*L:(pc+1)*L], op=ALU.mult))

        with nc.Block() as blk:
            @blk.sync
            def _(sync):
                sync.wait_ge(st.h, st.n)
                if dbg is not None:
                    dsem = 0
                    for name, tile in [("dt128", dt128[(nch-1) % 2][:]), ("B128", B128[(nch-1) % 2][:]),
                                       ("C128", C128[(nch-1) % 2][:]), ("ab", ab[:]),
                                       ("hcar", hcar[(nch-1) % 2][:]), ("y_acc", y_acc[:]),
                                       ("w16bf", w16bf[:]), ("c16bf", c16bf[:]), ("p8", p8[:]),
                                       ("dt_row", dt_row[:]), ("mdt", mdt[:]), ("A_neg", A_neg[:]),
                                       ("p_stage", p_stage[:]),
                                       ("sel", sel[:]), ("h_pc", h_pc[:])]:
                        if name in dbg:
                            sync.dma_start(out=dbg[name], in_=tile).then_inc(st.h, 16)
                            dsem += 16
                    sync.wait_ge(st.h, st.n + dsem)


LN_EPS = 1e-5


class LNEmit:
    def __init__(self, tiles, TT, mm, sev):
        self.t = tiles
        self.TT = TT
        self.mm = mm
        self.sev = sev
        self.mm0 = mm.n
        self.sev0 = sev.n
        mm.n += 3
        sev.n += 50

    def vector(self, vector, wait_pre=None):
        t = self.t; TT = self.TT
        n = self.sev0
        for kc in range(16):
            sl = slice(kc * TT, (kc + 1) * TT)
            vector.tensor_copy(out=t["rbf"][:, sl], in_=t["r"][:, sl]).then_inc(self.sev.h, 1)
        vector.wait_ge(self.mm.h, self.mm0 + 1)
        vector.tensor_scalar(out=t["mrow"][:, 0:TT], in0=t["st_ps"][0][0:1, 0:TT],
                             scalar1=1.0 / 2048.0, scalar2=None, op0=ALU.mult)
        vector.tensor_scalar(out=t["vtmp"][:, 0:TT], in0=t["st_ps"][1][0:1, 0:TT],
                             scalar1=1.0 / 2048.0, scalar2=None, op0=ALU.mult)
        vector.tensor_tensor(out=t["rstd"][:, 0:TT], in0=t["mrow"][:, 0:TT],
                             in1=t["mrow"][:, 0:TT], op=ALU.mult)
        vector.tensor_tensor(out=t["vtmp"][:, 0:TT], in0=t["vtmp"][:, 0:TT],
                             in1=t["rstd"][:, 0:TT], op=ALU.subtract)
        vector.tensor_scalar(out=t["vtmp"][:, 0:TT], in0=t["vtmp"][:, 0:TT],
                             scalar1=1.0, scalar2=LN_EPS, op0=ALU.mult, op1=ALU.add)
        vector.reciprocal(out=t["rstd"][:, 0:TT], in_=t["vtmp"][:, 0:TT]).then_inc(self.sev.h, 1)
        vector.wait_ge(self.mm.h, self.mm0 + 3)
        for kc in range(16):
            sl = slice(kc * TT, (kc + 1) * TT)
            vector.tensor_tensor(out=t["r"][:, sl], in0=t["r"][:, sl],
                                 in1=t["bc_ps"][0][:, 0:TT], op=ALU.subtract)
            vector.tensor_tensor(out=t["out"][:, sl], in0=t["r"][:, sl],
                                 in1=t["bc_ps"][1][:, 0:TT], op=ALU.mult).then_inc(self.sev.h, 1)

    def scalar(self, scalar):
        t = self.t; TT = self.TT
        scalar.wait_ge(self.sev.h, self.sev0 + 16)
        for kc in range(16):
            sl = slice(kc * TT, (kc + 1) * TT)
            scalar.activation(out=t["sq"][:, sl], in_=t["rbf"][:, sl],
                              func=AF.Square).then_inc(self.sev.h, 1)
        scalar.wait_ge(self.sev.h, self.sev0 + 33)
        scalar.activation(out=t["vtmp"][:, 0:TT], in_=t["rstd"][:, 0:TT],
                          func=AF.Sqrt).then_inc(self.sev.h, 1)

    def tensor(self, tensor):
        t = self.t; TT = self.TT
        tensor.wait_ge(self.sev.h, self.sev0 + 32)
        for kc in range(16):
            sl = slice(kc * TT, (kc + 1) * TT)
            tensor.matmul(out=t["st_ps"][0][0:1, 0:TT], lhsT=t["ones_col"][:],
                          rhs=t["rbf"][:, sl], start=(kc == 0), stop=(kc == 15))
        for kc in range(16):
            sl = slice(kc * TT, (kc + 1) * TT)
            inst = tensor.matmul(out=t["st_ps"][1][0:1, 0:TT], lhsT=t["ones_col"][:],
                                 rhs=t["sq"][:, sl], start=(kc == 0), stop=(kc == 15))
        inst.then_inc(self.mm.h, 1)
        tensor.wait_ge(self.sev.h, self.sev0 + 34)
        tensor.matmul(out=t["bc_ps"][0][:, 0:TT], lhsT=t["ones_f32"][:, 0:128],
                      rhs=t["mrow"][:, 0:TT], start=True, stop=True).then_inc(self.mm.h, 1)
        tensor.matmul(out=t["bc_ps"][1][:, 0:TT], lhsT=t["ones_f32"][:, 0:128],
                      rhs=t["vtmp"][:, 0:TT], start=True, stop=True).then_inc(self.mm.h, 1)


def build_b(nc, io):
    TT = 512
    ntt = TLOC // TT
    tok0 = io["tok0"]

    with ExitStack() as ctx:
        sb = lambda *a: ctx.enter_context(nc.sbuf_tensor(*a))
        ps = lambda *a: ctx.enter_context(nc.psum_tensor(*a))
        ones_col = sb("b_ones_col", [128, 1], BF16)
        ones_f32 = sb("b_ones_f32", [1, 128], F32)
        ypre = [sb(f"b_ypre{i}", [128, 32 * TT], BF16) for i in range(2)]
        xres = sb("b_xres", [128, 16 * TT], F32)
        lhsb = [sb(f"b_lhs{i}", [128, 32 * 128], BF16) for i in range(2)]
        r = sb("b_r", [128, 16 * TT], F32)
        rbf = sb("b_rbf", [128, 16 * TT], BF16)
        sq = sb("b_sq", [128, 16 * TT], BF16)
        mrow = sb("b_mrow", [1, TT], F32)
        vtmp = sb("b_vtmp", [1, TT], F32)
        rstd = sb("b_rstd", [1, TT], F32)
        pmb = [ps(f"b_pm{i}", [128, TT], F32) for i in range(4)]
        st_ps = [ps(f"b_st{i}", [1, TT], F32) for i in range(2)]
        bc_ps = [ps(f"b_bc{i}", [128, TT], F32) for i in range(2)]

        ld = Sem(ctx, nc, "b_ld")
        mm = Sem(ctx, nc, "b_mm")
        sev = Sem(ctx, nc, "b_sev")
        st = Sem(ctx, nc, "b_st")

        with nc.Block() as blk:
            @blk.vector
            def _(vector):
                vector.memset(ones_col[:], 1.0)
                vector.memset(ones_f32[:], 1.0)

            @blk.sync
            def _(sync):
                for kc in range(32):
                    sync.dma_start(out=ypre[0][:, kc*TT:(kc+1)*TT],
                                   in_=io["y_pre_all"][kc, :, tok0:tok0+TT]).then_inc(ld.h, 16)
                ld.n += 32 * 16

        for tt in range(ntt):
            cur = ypre[tt % 2]
            ld0 = ld.n
            mm0 = mm.n
            sev0 = sev.n
            with nc.Block() as blk:
                @blk.sync
                def _(sync, tt=tt, mm0=mm0):
                    t0g = tok0 + (tt + 1) * TT
                    if tt + 1 < ntt:
                        for kc in range(32):
                            sync.dma_start(out=ypre[(tt+1) % 2][:, kc*TT:(kc+1)*TT],
                                           in_=io["y_pre_all"][kc, :, t0g:t0g+TT]).then_inc(ld.h, 16)
                        ld.n += 32 * 16
                    for kc in range(16):
                        sync.dma_start(out=xres[:, kc*TT:(kc+1)*TT],
                                       in_=io["xT_c"][kc*128:(kc+1)*128, tt*TT:(tt+1)*TT]).then_inc(ld.h, 16)
                    ld.n += 16 * 16
                    for mt in range(16):
                        if mt >= 2:
                            sync.wait_ge(mm.h, mm0 + mt - 1)
                        w3 = io["outw_T"][:, mt*128:(mt+1)*128].rearrange("(kc p) m -> p kc m", p=128)
                        sync.dma_start(out=lhsb[mt % 2][:].rearrange("p (kc m) -> p kc m", m=128),
                                       in_=w3).then_inc(ld.h, 16)
                        ld.n += 16

                @blk.tensor
                def _(tensor, tt=tt, cur=cur, ld0=ld0, mm0=mm0, sev0=sev0):
                    for mt in range(16):
                        tensor.wait_ge(ld.h, ld0 + (32 * 16 if tt + 1 < ntt else 0) + 16 * 16 + 16 * (mt + 1))
                        if mt >= 4:
                            tensor.wait_ge(sev.h, sev0 + mt - 3)
                        for kc in range(32):
                            inst = tensor.matmul(out=pmb[mt % 4][:],
                                                 lhsT=lhsb[mt % 2][:, kc*128:(kc+1)*128],
                                                 rhs=cur[:, kc*TT:(kc+1)*TT],
                                                 start=(kc == 0), stop=(kc == 31))
                        mm.inc(inst)

                @blk.vector
                def _(vector, mm0=mm0, sev0=sev0):
                    for mt in range(16):
                        vector.wait_ge(mm.h, mm0 + mt + 1)
                        sev.inc(vector.tensor_tensor(out=r[:, mt*TT:(mt+1)*TT], in0=pmb[mt % 4][:],
                                                     in1=xres[:, mt*TT:(mt+1)*TT], op=ALU.add))

            lnt = {"r": r, "rbf": rbf, "sq": sq, "out": rbf, "ones_col": ones_col,
                   "ones_f32": ones_f32, "st_ps": st_ps, "bc_ps": bc_ps,
                   "mrow": mrow, "vtmp": vtmp, "rstd": rstd}
            with nc.Block() as blk:
                if tt >= 1:
                    pass
                ln = LNEmit(lnt, TT, mm, sev)

                @blk.tensor
                def _(tensor, ln=ln):
                    ln.tensor(tensor)

                @blk.scalar
                def _(scalar, ln=ln):
                    ln.scalar(scalar)

                @blk.vector
                def _(vector, ln=ln, tt=tt):
                    if tt >= 1:
                        vector.wait_ge(st.h, 256 * tt)
                    ln.vector(vector)

                @blk.gpsimd
                def _(gp, tt=tt, ln=ln):
                    gp.wait_ge(sev.h, ln.sev0 + 50)
                    for kc in range(16):
                        gp.dma_start(out=io["s1_d"][kc, :, tt*TT:(tt+1)*TT],
                                     in_=rbf[:, kc*TT:(kc+1)*TT]).then_inc(st.h, 16)
                    st.n += 16 * 16

        with nc.Block() as blk:
            @blk.gpsimd
            def _(gp):
                gp.wait_ge(st.h, st.n)

    with ExitStack() as ctx:
        sb = lambda *a: ctx.enter_context(nc.sbuf_tensor(*a))
        ps = lambda *a: ctx.enter_context(nc.psum_tensor(*a))
        ones_col = sb("b2_ones_col", [128, 1], BF16)
        ones_f32 = sb("b2_ones_f32", [1, 128], F32)
        s1t = [sb(f"b2_s1{i}", [128, 16 * TT], BF16) for i in range(2)]
        wgt = [sb(f"b2_wg{i}", [128, 16 * 128], BF16) for i in range(2)]
        wut = [sb(f"b2_wu{i}", [128, 16 * 128], BF16) for i in range(2)]
        wdt = [sb(f"b2_wd{i}", [128, 16 * 128], BF16) for i in range(2)]
        gsil = sb("b2_gsil", [128, 2 * TT], BF16)
        inter = sb("b2_inter", [128, 16 * TT], BF16)
        acc = sb("b2_acc", [128, 16 * TT], F32)
        rbf = sb("b2_rbf", [128, 16 * TT], BF16)
        sq = sb("b2_sq", [128, 16 * TT], BF16)
        mrow = sb("b2_mrow", [1, TT], F32)
        vtmp = sb("b2_vtmp", [1, TT], F32)
        rstd = sb("b2_rstd", [1, TT], F32)
        pmb = [ps(f"b2_pm{i}", [128, TT], F32) for i in range(4)]
        st_ps = [ps(f"b2_st{i}", [1, TT], F32) for i in range(2)]
        bc_ps = [ps(f"b2_bc{i}", [128, TT], F32) for i in range(2)]

        ld = Sem(ctx, nc, "b2_ld")
        lw = Sem(ctx, nc, "b2_lw")
        mm = Sem(ctx, nc, "b2_mm")
        ssil = Sem(ctx, nc, "b2_ssil")
        sev = Sem(ctx, nc, "b2_sev")
        st = Sem(ctx, nc, "b2_st")

        def wslice3(wap, colbase, kbase):
            return wap[kbase:kbase + 2048, colbase:colbase + 128].rearrange(
                "(kc p) m -> p kc m", p=128)

        with nc.Block() as blk:
            @blk.vector
            def _(vector):
                vector.memset(ones_col[:], 1.0)
                vector.memset(ones_f32[:], 1.0)

            @blk.sync
            def _(sync):
                for kc in range(16):
                    sync.dma_start(out=s1t[0][:, kc*TT:(kc+1)*TT],
                                   in_=io["s1_d"][kc, :, 0:TT]).then_inc(ld.h, 16)
                ld.n += 16 * 16

        for tt in range(ntt):
            cur = s1t[tt % 2]
            for hb in range(4):
                lw0 = lw.n
                mm0 = mm.n
                sev0 = sev.n
                ssil0 = ssil.n
                with nc.Block() as blk:
                    @blk.sync
                    def _(sync, tt=tt, hb=hb, mm0=mm0):
                        for hm in range(16):
                            if hm >= 2:
                                sync.wait_ge(mm.h, mm0 + hm - 1)
                            sync.dma_start(out=wgt[hm % 2][:].rearrange("p (kc m) -> p kc m", m=128),
                                           in_=wslice3(io["wg_T"], hb*2048 + hm*128, 0)).then_inc(lw.h, 16)
                            sync.dma_start(out=wut[hm % 2][:].rearrange("p (kc m) -> p kc m", m=128),
                                           in_=wslice3(io["wu_T"], hb*2048 + hm*128, 0)).then_inc(lw.h, 16)
                            lw.n += 32
                        for dm in range(16):
                            if dm >= 2:
                                sync.wait_ge(mm.h, mm0 + 16 + dm - 1)
                            sync.dma_start(out=wdt[dm % 2][:].rearrange("p (kc m) -> p kc m", m=128),
                                           in_=wslice3(io["wd_T"], dm*128, hb*2048)).then_inc(lw.h, 16)
                            lw.n += 16

                    @blk.tensor
                    def _(tensor, tt=tt, hb=hb, cur=cur, lw0=lw0, mm0=mm0, sev0=sev0, ssil0=ssil0):
                        for hm in range(16):
                            tensor.wait_ge(lw.h, lw0 + 32 * (hm + 1))
                            if hm >= 2:
                                tensor.wait_ge(sev.h, sev0 + hm - 1)
                            ga = pmb[2 * (hm % 2)]
                            ua = pmb[2 * (hm % 2) + 1]
                            for kc in range(16):
                                tensor.matmul(out=ga[:], lhsT=wgt[hm % 2][:, kc*128:(kc+1)*128],
                                              rhs=cur[:, kc*TT:(kc+1)*TT],
                                              start=(kc == 0), stop=(kc == 15))
                            for kc in range(16):
                                inst = tensor.matmul(out=ua[:], lhsT=wut[hm % 2][:, kc*128:(kc+1)*128],
                                                     rhs=cur[:, kc*TT:(kc+1)*TT],
                                                     start=(kc == 0), stop=(kc == 15))
                            inst.then_inc(mm.h, 1)
                        mm.n += 16
                        for dm in range(16):
                            tensor.wait_ge(lw.h, lw0 + 32 * 16 + 16 * (dm + 1))
                            tensor.wait_ge(sev.h, sev0 + 16)
                            if dm >= 2:
                                tensor.wait_ge(sev.h, sev0 + 16 + dm - 1)
                            pa = pmb[dm % 2]
                            for kc in range(16):
                                inst = tensor.matmul(out=pa[:], lhsT=wdt[dm % 2][:, kc*128:(kc+1)*128],
                                                     rhs=inter[:, kc*TT:(kc+1)*TT],
                                                     start=(kc == 0), stop=(kc == 15))
                            inst.then_inc(mm.h, 1)
                        mm.n += 16

                    @blk.scalar
                    def _(scalar, mm0=mm0, ssil0=ssil0, sev0=sev0):
                        for hm in range(16):
                            scalar.wait_ge(mm.h, mm0 + hm + 1)
                            if hm >= 2:
                                scalar.wait_ge(sev.h, sev0 + hm - 1)
                            ga = pmb[2 * (hm % 2)]
                            scalar.activation(out=gsil[:, (hm % 2)*TT:(hm % 2 + 1)*TT],
                                              in_=ga[:], func=AF.Silu).then_inc(ssil.h, 1)
                        ssil.n += 16

                    @blk.vector
                    def _(vector, hb=hb, mm0=mm0, ssil0=ssil0, sev0=sev0):
                        for hm in range(16):
                            vector.wait_ge(ssil.h, ssil0 + hm + 1)
                            ua = pmb[2 * (hm % 2) + 1]
                            vector.tensor_tensor(out=inter[:, hm*TT:(hm+1)*TT],
                                                 in0=gsil[:, (hm % 2)*TT:(hm % 2 + 1)*TT],
                                                 in1=ua[:], op=ALU.mult).then_inc(sev.h, 1)
                        sev.n += 16
                        for dm in range(16):
                            vector.wait_ge(mm.h, mm0 + 16 + dm + 1)
                            pa = pmb[dm % 2]
                            if hb == 0:
                                vector.tensor_copy(out=acc[:, dm*TT:(dm+1)*TT], in_=pa[:]).then_inc(sev.h, 1)
                            else:
                                vector.tensor_tensor(out=acc[:, dm*TT:(dm+1)*TT],
                                                     in0=acc[:, dm*TT:(dm+1)*TT],
                                                     in1=pa[:], op=ALU.add).then_inc(sev.h, 1)
                        sev.n += 16

            lnt = {"r": acc, "rbf": rbf, "sq": sq, "out": rbf, "ones_col": ones_col,
                   "ones_f32": ones_f32, "st_ps": st_ps, "bc_ps": bc_ps,
                   "mrow": mrow, "vtmp": vtmp, "rstd": rstd}
            with nc.Block() as blk:
                res0 = sev.n
                sev.n += 16
                ln = LNEmit(lnt, TT, mm, sev)

                @blk.sync
                def _(sync, tt=tt):
                    if tt + 1 < ntt:
                        for kc in range(16):
                            sync.dma_start(out=s1t[(tt+1) % 2][:, kc*TT:(kc+1)*TT],
                                           in_=io["s1_d"][kc, :, (tt+1)*TT:(tt+2)*TT]).then_inc(ld.h, 16)
                        ld.n += 16 * 16

                @blk.vector
                def _(vector, tt=tt, cur=cur, ln=ln, res0=res0):
                    if tt >= 1:
                        vector.wait_ge(st.h, 256 * tt)
                    for kc in range(16):
                        vector.tensor_tensor(out=acc[:, kc*TT:(kc+1)*TT],
                                             in0=acc[:, kc*TT:(kc+1)*TT],
                                             in1=cur[:, kc*TT:(kc+1)*TT], op=ALU.add).then_inc(sev.h, 1)
                    ln.vector(vector)

                @blk.scalar
                def _(scalar, ln=ln):
                    ln.scalar(scalar)

                @blk.tensor
                def _(tensor, ln=ln):
                    ln.tensor(tensor)

                @blk.gpsimd
                def _(gp, tt=tt, ln=ln):
                    gp.wait_ge(sev.h, ln.sev0 + 50)
                    for kc in range(16):
                        gp.dma_start(out=io["s2"][kc, :, tt*TT:(tt+1)*TT],
                                     in_=rbf[:, kc*TT:(kc+1)*TT]).then_inc(st.h, 16)
                    st.n += 16 * 16

        with nc.Block() as blk:
            @blk.gpsimd
            def _(gp):
                gp.wait_ge(st.h, st.n)

def mark(sem, inst, k=1):
    sem.n += k
    inst.then_inc(sem.h, k)
    return (sem, sem.n)


def wt(eng, ev):
    eng.wait_ge(ev[0].h, ev[1])


def build_c2(nc, io, nch_limit=None, nheads=4, dbg=None):
    L = 128
    nch = nch_limit or (S // L)
    NRND = 8
    with ExitStack() as ctx:
        sb = lambda *a: ctx.enter_context(nc.sbuf_tensor(*a))
        ps = lambda *a: ctx.enter_context(nc.psum_tensor(*a))
        qhs = sb("c2_qh", [128, 4 * S // 2], BF16)
        khs = sb("c2_kh", [128, S], BF16)
        vhs = sb("c2_vh", [128, S], BF16)
        ident_f = sb("c2_idf", [128, 128], F32)
        ident_b = sb("c2_idb", [128, 128], BF16)
        idint = sb("c2_idi", [128, 128], I32)
        mbU = sb("c2_mbU", [128, 128], F32)
        mbUD = sb("c2_mbUD", [128, 128], F32)
        ones_row = sb("c2_ones", [1, 128], F32)
        glst = sb("c2_glst", [1, 2048], F32)
        btst = sb("c2_btst", [1, 2048], F32)
        gc = sb("c2_gc", [1, 128], F32)
        gamrow = sb("c2_gamrow", [1, 128], F32)
        nbrow = sb("c2_nbrow", [1, 128], F32)
        ghrow = sb("c2_ghrow", [1, 128], F32)
        glrow = sb("c2_glrow", [1, 128], F32)
        stg = sb("c2_stg", [128, 128], F32)
        cols = sb("c2_cols", [128, 8], F32)
        ngcol = sb("c2_ngcol", [128, 1], F32)
        gamc = sb("c2_gamc", [128, 1], F32)
        gamLc = sb("c2_gamLc", [128, 1], F32)
        bgc = sb("c2_bgc", [128, 1], F32)
        mU = sb("c2_mU", [128, 128], F32)
        mUD = sb("c2_mUD", [128, 128], F32)
        d2t = sb("c2_d2t", [128, 128], F32)
        d2tud = sb("c2_d2tud", [128, 128], F32)
        q0 = [sb(f"c2_q{i}", [128, 128], F32) for i in range(2)]
        tj = sb("c2_tj", [128, 128], F32)
        m2t = sb("c2_m2t", [128, 128], BF16)
        vt = sb("c2_vt", [128, 128], BF16)
        kt = sb("c2_kt", [128, 128], BF16)
        khat = sb("c2_khat", [128, 128], BF16)
        qgt = sb("c2_qgt", [128, 128], BF16)
        zj = sb("c2_zj", [128, 256], F32)
        et = sb("c2_et", [128, 128], F32)
        dlt = sb("c2_dlt", [128, 128], BF16)
        s4bf = sb("c2_s4bf", [128, 4 * 128], BF16)
        o1 = sb("c2_o1", [128, 128], F32)
        obf = sb("c2_obf", [128, 128], BF16)
        oev = sb("c2_oev", [128, 128], BF16)
        s4 = sb("c2_s4", [128, 4 * 128], F32)
        psA = ps("c2_psA", [128, 512], F32)
        psA2 = ps("c2_psA2", [128, 256], F32)
        psT = ps("c2_psT", [128, 256], F32)
        psBF = ps("c2_psBF", [128, 384], BF16)
        psC = ps("c2_psC", [128, 256], F32)
        psC2 = ps("c2_psC2", [128, 128], F32)
        psD = ps("c2_psD", [128, 256], F32)
        psO = ps("c2_psO", [128, 128], F32)

        ld = Sem(ctx, nc, "c2_ld")
        dv = Sem(ctx, nc, "c2_dv")
        sc = Sem(ctx, nc, "c2_sc")
        pe = Sem(ctx, nc, "c2_pe")
        st = Sem(ctx, nc, "c2_st")

        with nc.Block() as blk:
            @blk.sync
            def _(sync):
                for hc in range(2):
                    sync.dma_start(out=qhs[:, hc*S:(hc+1)*S], in_=io["qh"][hc, :, :]).then_inc(ld.h, 16)
                sync.dma_start(out=khs[:], in_=io["kh"][:, :]).then_inc(ld.h, 16)
                sync.dma_start(out=vhs[:], in_=io["vh"][:, :]).then_inc(ld.h, 16)
                ld.n += 4 * 16

            @blk.gpsimd
            def _(gp):
                gp.iota(idint[:], pattern=[[1, 128]], base=0, channel_multiplier=-1)

            @blk.vector
            def _(vector):
                vector.memset(ones_row[:], 1.0)
                vector.memset(s4[:], 0.0)
                vector.memset(s4bf[:], 0.0)
                vector.memset(stg[:], 0.0)

        with nc.Block() as blk:
            @blk.vector
            def _(vector):
                vector.tensor_single_scalar(out=ident_f[:], in_=idint[:], scalar=0, op=ALU.is_equal)
                vector.tensor_copy(out=ident_b[:], in_=ident_f[:])
                vector.tensor_single_scalar(out=mbU[:], in_=idint[:], scalar=1, op=ALU.is_lt)
                vector.tensor_scalar(out=mbU[:], in0=mbU[:], scalar1=-60.0, scalar2=None, op0=ALU.mult)
                vector.tensor_single_scalar(out=mbUD[:], in_=idint[:], scalar=0, op=ALU.is_lt)
                vector.tensor_scalar(out=mbUD[:], in0=mbUD[:], scalar1=-60.0, scalar2=None, op0=ALU.mult)

        for hh in range(nheads):
            qsl_base = (hh % 2) * S
            if hh in (0, 2):
                qld0 = ld.n
                with nc.Block() as blk:
                    @blk.sync
                    def _(sync, hh=hh):
                        for j in range(2):
                            sync.dma_start(out=qhs[:, j*S:(j+1)*S],
                                           in_=io["qh"][hh + j, :, :]).then_inc(ld.h, 16)
                        ld.n += 32
            for ci in range(nch):
                t0 = ci * L
                gidx = hh * nch + ci
                dv0, sc0, pe0, ld0, st0 = dv.n, sc.n, pe.n, ld.n, st.n
                nld = 32 if ci % 16 == 0 else 0
                with nc.Block() as blk:
                    @blk.sync
                    def _(sync, hh=hh, ci=ci):
                        if ci % 16 == 0:
                            sync.dma_start(out=glst[:], in_=io["gl"][hh:hh+1, ci*L:ci*L+2048]).then_inc(ld.h, 16)
                            sync.dma_start(out=btst[:], in_=io["bt"][hh:hh+1, ci*L:ci*L+2048]).then_inc(ld.h, 16)
                            ld.n += 32

                    @blk.vector
                    def _(vector, hh=hh, ci=ci, dv0=dv0, sc0=sc0, pe0=pe0, ld0=ld0, nld=nld, t0=t0, gidx=gidx):
                        co = (ci % 16) * L
                        vector.wait_ge(ld.h, ld0 + nld)
                        vector.tensor_tensor_scan(out=gc[:], data0=ones_row[:], data1=glst[:, co:co+L],
                                                  initial=0.0, op0=ALU.mult, op1=ALU.add)
                        vector.tensor_scalar(out=nbrow[:], in0=btst[:, co:co+L],
                                             scalar1=-1.0, scalar2=None, op0=ALU.mult)
                        vector.tensor_scalar(out=glrow[:], in0=ones_row[:], scalar1=gc[:, 127:128],
                                             scalar2=None, op0=ALU.mult).then_inc(dv.h, 1)
                        vector.wait_ge(pe.h, pe0 + 1)
                        vector.tensor_tensor(out=mU[:], in0=psA2[:, 0:128], in1=mbU[:], op=ALU.add)
                        vector.tensor_tensor(out=mUD[:], in0=psA2[:, 0:128], in1=mbUD[:], op=ALU.add)
                        vector.tensor_copy(out=cols[:, 0:4], in_=psT[:, 0:4])
                        vector.tensor_scalar(out=cols[:, 4:8], in0=psT[:, 0:4], scalar1=-1.0,
                                             scalar2=None, op0=ALU.mult)
                        vector.tensor_copy(out=vt[:], in_=psBF[:, 0:128])
                        vector.tensor_copy(out=kt[:], in_=psBF[:, 128:256]).then_inc(dv.h, 1)
                        vector.wait_ge(sc.h, sc0 + 2)
                        vector.tensor_tensor(out=mU[:], in0=d2t[:], in1=psA[:, 0:128], op=ALU.mult)
                        vector.tensor_tensor(out=q0[0][:], in0=mU[:], in1=psA[:, 384:512], op=ALU.mult)
                        vector.tensor_tensor(out=m2t[:], in0=d2tud[:], in1=psA[:, 128:256], op=ALU.mult)
                        vector.tensor_scalar(out=zj[:, 0:128], in0=vt[:], scalar1=cols[:, 1:2],
                                             scalar2=None, op0=ALU.mult)
                        vector.tensor_scalar(out=zj[:, 128:256], in0=kt[:], scalar1=cols[:, 1:2],
                                             scalar2=None, op0=ALU.mult)
                        vector.tensor_scalar(out=zj[:, 128:256], in0=zj[:, 128:256], scalar1=gamc[:, 0:1],
                                             scalar2=None, op0=ALU.mult)
                        vector.tensor_scalar(out=khat[:], in0=kt[:], scalar1=cols[:, 2:3],
                                             scalar2=None, op0=ALU.mult)
                        vector.tensor_tensor(out=qgt[:], in0=qhs[:, qsl_base+t0:qsl_base+t0+L],
                                             in1=psA2[:, 128:256], op=ALU.mult).then_inc(dv.h, 1)
                        for r in range(8):
                            vector.wait_ge(pe.h, pe0 + 3 * r + 2)
                            vector.tensor_tensor(out=zj[:], in0=zj[:], in1=psC[:, 0:256],
                                                 op=ALU.add).then_inc(dv.h, 1)
                            if r < 7:
                                vector.wait_ge(pe.h, pe0 + 3 * r + 3)
                                vector.tensor_copy(out=tj[:], in_=psT[:, 128:256]).then_inc(dv.h, 1)
                                vector.wait_ge(pe.h, pe0 + 3 * r + 4)
                                vector.tensor_copy(out=q0[(r + 1) % 2][:], in_=psC2[:]).then_inc(dv.h, 1)
                        vector.wait_ge(pe.h, pe0 + 24)
                        vector.tensor_copy(out=et[:], in_=psT[:, 128:256]).then_inc(dv.h, 1)
                        vector.wait_ge(pe.h, pe0 + 25)
                        vector.tensor_tensor(out=dlt[:], in0=zj[:, 0:128], in1=psD[:, 0:128],
                                             op=ALU.subtract).then_inc(dv.h, 1)
                        vector.wait_ge(pe.h, pe0 + 27)
                        vector.scalar_tensor_tensor(out=s4[:, hh*128:(hh+1)*128],
                                                    in0=s4[:, hh*128:(hh+1)*128],
                                                    scalar=gamLc[:, 0:1], in1=psD[:, 128:256],
                                                    op0=ALU.mult, op1=ALU.add)
                        vector.tensor_copy(out=s4bf[:, hh*128:(hh+1)*128],
                                           in_=s4[:, hh*128:(hh+1)*128]).then_inc(dv.h, 1)

                    @blk.scalar
                    def _(scalar, hh=hh, ci=ci, dv0=dv0, sc0=sc0, pe0=pe0, st0=st0, gidx=gidx):
                        scalar.wait_ge(dv.h, dv0 + 1)
                        scalar.activation(out=gamrow[:], in_=gc[:], func=AF.Exp)
                        scalar.activation(out=ghrow[:], in_=gc[:], func=AF.Exp,
                                          scale=-1.0, bias=gc[:, 127:128]).then_inc(sc.h, 1)
                        scalar.wait_ge(dv.h, dv0 + 2)
                        scalar.activation(out=d2t[:], in_=mU[:], func=AF.Exp, bias=cols[:, 4:5])
                        scalar.activation(out=d2tud[:], in_=mUD[:], func=AF.Exp, bias=cols[:, 4:5])
                        scalar.activation(out=gamc[:], in_=cols[:, 0:1], func=AF.Exp)
                        scalar.activation(out=gamLc[:], in_=cols[:, 3:4], func=AF.Exp).then_inc(sc.h, 1)
                        scalar.wait_ge(pe.h, pe0 + 26)
                        if gidx >= 2:
                            scalar.wait_ge(st.h, 16 * (gidx - 1))
                        scalar.copy(out=obf[:], in_=psO[:]).then_inc(sc.h, 1)
                        scalar.wait_ge(pe.h, pe0 + 28)
                        scalar.copy(out=oev[:], in_=psBF[:, 256:384]).then_inc(sc.h, 1)

                    @blk.gpsimd
                    def _(gp, dv0=dv0, sc0=sc0, ld0=ld0, nld=nld, hh=hh, t0=t0):
                        co = ((t0 // L) % 16) * L
                        gp.wait_ge(sc.h, sc0 + 1)
                        gp.dma_start(out=stg[0:1, :], in_=gc[:]).then_inc(ld.h, 16)
                        gp.dma_start(out=stg[1:2, :], in_=btst[:, co:co+L]).then_inc(ld.h, 16)
                        gp.dma_start(out=stg[2:3, :], in_=ghrow[:]).then_inc(ld.h, 16)
                        gp.dma_start(out=stg[3:4, :], in_=glrow[:]).then_inc(ld.h, 16)
                        ld.n += 64
                        gp.wait_ge(sc.h, sc0 + 4)
                        gp.dma_start(out=io["oo"][hh, :, t0:t0+L], in_=oev[:]).then_inc(st.h, 16)
                        st.n += 16

                    @blk.tensor
                    def _(tensor, hh=hh, ci=ci, t0=t0, qsl_base=qsl_base,
                          dv0=dv0, sc0=sc0, pe0=pe0, ld0=ld0, nld=nld):
                        tensor.wait_ge(ld.h, ld0 + nld + 64)
                        tensor.transpose(out=psT[:, 0:128], in_=stg[:], identity=ident_f[:])
                        tensor.matmul(out=psA2[:, 0:128], lhsT=ones_row[:, 0:128],
                                      rhs=gc[:], start=True, stop=True)
                        tensor.matmul(out=psA[:, 384:512], lhsT=ones_row[:, 0:128],
                                      rhs=nbrow[:], start=True, stop=True)
                        tensor.matmul(out=psA2[:, 128:256], lhsT=ones_row[:, 0:128],
                                      rhs=gamrow[:], start=True, stop=True)
                        tensor.matmul(out=psA[:, 0:128], lhsT=khs[:, t0:t0+L],
                                      rhs=khs[:, t0:t0+L], start=True, stop=True)
                        tensor.matmul(out=psA[:, 128:256], lhsT=khs[:, t0:t0+L],
                                      rhs=qhs[:, qsl_base+t0:qsl_base+t0+L], start=True, stop=True)
                        tensor.transpose(out=psBF[:, 0:128], in_=vhs[:, t0:t0+L], identity=ident_b[:])
                        tensor.transpose(out=psBF[:, 128:256], in_=khs[:, t0:t0+L],
                                         identity=ident_b[:]).then_inc(pe.h, 1)
                        pe.n += 1
                        tensor.wait_ge(dv.h, dv0 + 3)
                        for r in range(8):
                            tensor.matmul(out=psC[:, 0:256], lhsT=q0[r % 2][:], rhs=zj[:],
                                          start=True, stop=True).then_inc(pe.h, 1)
                            pe.n += 1
                            if r < 7:
                                tensor.transpose(out=psT[:, 128:256], in_=q0[r % 2][:],
                                                 identity=ident_f[:]).then_inc(pe.h, 1)
                                pe.n += 1
                                tensor.wait_ge(dv.h, dv0 + 3 + 3 * r + 2)
                                tensor.matmul(out=psC2[:], lhsT=tj[:], rhs=q0[r % 2][:],
                                              start=True, stop=True).then_inc(pe.h, 1)
                                pe.n += 1
                                tensor.wait_ge(dv.h, dv0 + 3 + 3 * r + 3)
                            else:
                                tensor.wait_ge(dv.h, dv0 + 3 + 3 * 7 + 1)
                        tensor.transpose(out=psT[:, 128:256], in_=zj[:, 128:256],
                                         identity=ident_f[:]).then_inc(pe.h, 1)
                        pe.n += 1
                        tensor.wait_ge(dv.h, dv0 + 26)
                        tensor.matmul(out=psD[:, 0:128], lhsT=et[:], rhs=s4[:, hh*128:(hh+1)*128],
                                      start=True, stop=True).then_inc(pe.h, 1)
                        pe.n += 1
                        tensor.wait_ge(dv.h, dv0 + 27)
                        tensor.matmul(out=psO[:], lhsT=m2t[:], rhs=dlt[:], start=True, stop=False)
                        tensor.matmul(out=psO[:], lhsT=qgt[:], rhs=s4bf[:, hh*128:(hh+1)*128],
                                      start=False, stop=True).then_inc(pe.h, 1)
                        pe.n += 1
                        tensor.matmul(out=psD[:, 128:256], lhsT=khat[:], rhs=dlt[:],
                                      start=True, stop=True).then_inc(pe.h, 1)
                        pe.n += 1
                        tensor.wait_ge(sc.h, sc0 + 3)
                        tensor.transpose(out=psBF[:, 256:384], in_=obf[:],
                                         identity=ident_b[:]).then_inc(pe.h, 1)
                        pe.n += 1

                dv.n += 28
                sc.n += 4

        with nc.Block() as blk:
            @blk.gpsimd
            def _(gp):
                gp.wait_ge(st.h, st.n)
                if dbg is not None:
                    dsem = 0
                    for name, tile in [("gc", gc[:]), ("cols", cols[:]), ("d2t", d2t[:]),
                                       ("d2tud", d2tud[:]), ("q0a", q0[0][:]), ("q0b", q0[1][:]),
                                       ("zj", zj[:]), ("m2t", m2t[:]), ("dlt", dlt[:]),
                                       ("s4", s4[:]), ("obf", obf[:]), ("khat", khat[:]),
                                       ("qgt", qgt[:]), ("vt", vt[:]), ("kt", kt[:]),
                                       ("et", et[:]), ("gamc", gamc[:]), ("gamLc", gamLc[:]),
                                       ("ngcol", ngcol[:]), ("mbU_t", mbU[:]), ("mbUD_t", mbUD[:])]:
                        if name in dbg:
                            gp.dma_start(out=dbg[name], in_=tile).then_inc(st.h, 16)
                            dsem += 16
                    gp.wait_ge(st.h, st.n + dsem)

def build_c1(nc, io):
    TT = 512
    ntt = TLOC // TT
    RDK = 1.0 / (128.0 ** 0.5)

    NMT = 41
    def cls(i):
        if i < 20: return "nrm"
        if i < 24: return "vv"
        if i < 40: return "gg"
        return "bgk"
    cum = {k: [0] for k in ("mm", "cons", "convd", "sqd", "nsum", "r1d", "r2d",
                            "r3d", "nbc", "fin", "st", "bgkd")}
    for g in range(ntt * NMT):
        i = g % NMT
        c = cls(i)
        add = {"mm": 1, "cons": 1,
               "convd": 1 if i <= 23 else 0,
               "sqd": 1 if c == "nrm" else 0,
               "nsum": 1 if c == "nrm" else 0,
               "r1d": 1 if c == "nrm" else 0,
               "r2d": 1 if c == "nrm" else 0,
               "r3d": 1 if c == "nrm" else 0,
               "nbc": 1 if c == "nrm" else 0,
               "fin": 1,
               "st": 2 if c == "bgk" else 1,
               "bgkd": 1 if c == "bgk" else 0}
        for k in cum:
            cum[k].append(cum[k][g] + add[k])

    with ExitStack() as ctx:
        sb = lambda *a: ctx.enter_context(nc.sbuf_tensor(*a))
        ps = lambda *a: ctx.enter_context(nc.psum_tensor(*a))
        ones_col = sb("c1_onc", [128, 1], BF16)
        ones_f32 = sb("c1_onf", [1, 128], F32)
        convw = sb("c1_cw", [128, 24 * 4], F32)
        hist = sb("c1_hist", [128, 24 * 3], F32)
        rhs = [sb(f"c1_rhs{i}", [128, 16 * TT], BF16) for i in range(2)]
        lhsb = [sb(f"c1_lhs{i}", [128, 16 * 128], BF16) for i in range(2)]
        xs = sb("c1_xs", [128, TT + 3], F32)
        cacc = sb("c1_cacc", [128, TT], F32)
        qc = sb("c1_qc", [128, TT], BF16)
        sq = sb("c1_sq", [128, TT], BF16)
        row1 = sb("c1_row1", [1, TT], F32)
        dmy = sb("c1_dmy", [1, 8], F32)
        row2 = sb("c1_row2", [1, TT], F32)
        ob = [sb(f"c1_ob{i}", [128, TT], BF16) for i in range(2)]
        bge = sb("c1_bge", [48, 2 * TT], F32)
        pmb = [ps(f"c1_pm{i}", [128, TT], F32) for i in range(2)]
        st_ps = ps("c1_stp", [1, TT], F32)
        bc_ps = ps("c1_bcp", [128, TT], F32)

        ld = Sem(ctx, nc, "c1_ld")
        lw = Sem(ctx, nc, "c1_lw")
        mm = Sem(ctx, nc, "c1_mm")
        nsum = Sem(ctx, nc, "c1_nsum")
        nbc = Sem(ctx, nc, "c1_nbc")
        cons = Sem(ctx, nc, "c1_cons")
        convd = Sem(ctx, nc, "c1_convd")
        sqd = Sem(ctx, nc, "c1_sqd")
        r1d = Sem(ctx, nc, "c1_r1d")
        r2d = Sem(ctx, nc, "c1_r2d")
        r3d = Sem(ctx, nc, "c1_r3d")
        fin = Sem(ctx, nc, "c1_fin")
        bgkd = Sem(ctx, nc, "c1_bgkd")
        stq = Sem(ctx, nc, "c1_stq")

        def wsl(colbase, w):
            return io["projw_T"][:, colbase:colbase + w].rearrange("(kc p) m -> p kc m", p=128)

        CPREP = 24

        with nc.Block() as blk:
            @blk.sync
            def _(sync):
                for mt in range(24):
                    sync.dma_start(out=convw[:, mt*4:(mt+1)*4],
                                   in_=io["conv_w"][mt*128:(mt+1)*128, :]).then_inc(ld.h, 16)
                for kc in range(16):
                    sync.dma_start(out=rhs[0][:, kc*3:(kc+1)*3],
                                   in_=io["s2_h"][kc, :, 0:3]).then_inc(ld.h, 16)
                ld.n += 40 * 16
                for mt in range(24):
                    if mt >= 2:
                        sync.wait_ge(mm.h, mt - 1)
                    sync.dma_start(out=lhsb[mt % 2][:].rearrange("p (kc m) -> p kc m", m=128),
                                   in_=wsl(mt * 128, 128)).then_inc(lw.h, 16)
                lw.n += 24 * 16

            @blk.vector
            def _(vector):
                vector.memset(ones_col[:], 1.0)
                vector.memset(ones_f32[:], 1.0)
                for mt in range(24):
                    vector.wait_ge(mm.h, mt + 1)
                    vector.tensor_copy(out=hist[:, mt*3:(mt+1)*3],
                                       in_=pmb[mt % 2][:, 0:3]).then_inc(cons.h, 1)
                cons.n += 24

            @blk.tensor
            def _(tensor):
                tensor.wait_ge(ld.h, 40 * 16)
                for mt in range(24):
                    tensor.wait_ge(lw.h, 16 * (mt + 1))
                    if mt >= 2:
                        tensor.wait_ge(cons.h, mt - 1)
                    for kc in range(16):
                        inst = tensor.matmul(out=pmb[mt % 2][:, 0:3],
                                             lhsT=lhsb[mt % 2][:, kc*128:(kc+1)*128],
                                             rhs=rhs[0][:, kc*3:(kc+1)*3],
                                             start=(kc == 0), stop=(kc == 15))
                    inst.then_inc(mm.h, 1)
                mm.n += 24

        MMP = 24

        for tt in range(ntt):
            cur = rhs[tt % 2]
            ld0 = ld.n
            lw0 = lw.n
            with nc.Block() as blk:
                @blk.sync
                def _(sync, tt=tt):
                    for kc in range(16):
                        sync.dma_start(out=cur[:, kc*TT:(kc+1)*TT],
                                       in_=io["s2_h"][kc, :, 3+tt*TT:3+(tt+1)*TT]).then_inc(ld.h, 16)
                    ld.n += 256
                    for i in range(NMT):
                        g = tt * NMT + i
                        if g >= 2:
                            sync.wait_ge(mm.h, MMP + cum["mm"][g - 1])
                        w = 64 if i == 40 else 128
                        sync.dma_start(out=lhsb[i % 2][:, 0:16*w].rearrange("p (kc m) -> p kc m", m=w),
                                       in_=wsl(i * 128, w)).then_inc(lw.h, 16)
                    lw.n += NMT * 16

                @blk.tensor
                def _(tensor, tt=tt, cur=cur, ld0=ld0, lw0=lw0):
                    tensor.wait_ge(ld.h, ld0 + 256)
                    for i in range(NMT):
                        g = tt * NMT + i
                        w = 64 if i == 40 else 128
                        tensor.wait_ge(lw.h, lw0 + 16 * (i + 1))
                        if g >= 2:
                            tensor.wait_ge(cons.h, CPREP + cum["cons"][g - 1])
                        for kc in range(16):
                            inst = tensor.matmul(out=pmb[i % 2][0:w, 0:TT],
                                                 lhsT=lhsb[i % 2][:, kc*w:(kc+1)*w],
                                                 rhs=cur[:, kc*TT:(kc+1)*TT],
                                                 start=(kc == 0), stop=(kc == 15))
                        inst.then_inc(mm.h, 1)
                        if cls(i) == "nrm":
                            tensor.wait_ge(sqd.h, cum["sqd"][g + 1])
                            tensor.matmul(out=st_ps[0:1, :], lhsT=ones_col[:], rhs=sq[:],
                                          start=True, stop=True).then_inc(nsum.h, 1)
                            tensor.wait_ge(r3d.h, cum["r3d"][g + 1])
                            tensor.matmul(out=bc_ps[:], lhsT=ones_f32[:, 0:128], rhs=row1[:],
                                          start=True, stop=True).then_inc(nbc.h, 1)
                    mm.n += NMT
                    nsum.n += 20
                    nbc.n += 20

                @blk.vector
                def _(vector, tt=tt):
                    for i in range(NMT):
                        g = tt * NMT + i
                        if i <= 23:
                            vector.wait_ge(mm.h, MMP + cum["mm"][g + 1])
                            if g >= 1:
                                vector.wait_ge(fin.h, cum["fin"][g])
                            vector.tensor_copy(out=xs[:, 0:3], in_=hist[:, i*3:(i+1)*3])
                            vector.tensor_copy(out=xs[:, 3:3+TT], in_=pmb[i % 2][:, 0:TT])
                            vector.tensor_copy(out=hist[:, i*3:(i+1)*3],
                                               in_=pmb[i % 2][:, TT-3:TT]).then_inc(cons.h, 1)
                            cons.n += 1
                            vector.tensor_scalar(out=cacc[:], in0=xs[:, 0:TT],
                                                 scalar1=convw[:, i*4:i*4+1], scalar2=None, op0=ALU.mult)
                            for jj in (1, 2, 3):
                                inst = vector.scalar_tensor_tensor(out=cacc[:], in0=xs[:, jj:jj+TT],
                                                                   scalar=convw[:, i*4+jj:i*4+jj+1],
                                                                   in1=cacc[:], op0=ALU.mult, op1=ALU.add)
                            inst.then_inc(convd.h, 1)
                            convd.n += 1
                        if cls(i) == "nrm":
                            vector.wait_ge(nsum.h, cum["nsum"][g + 1])
                            vector.tensor_scalar(out=row1[:], in0=st_ps[0:1, :], scalar1=1.0,
                                                 scalar2=1e-6, op0=ALU.mult,
                                                 op1=ALU.add).then_inc(r1d.h, 1)
                            r1d.n += 1
                            vector.wait_ge(r2d.h, cum["r2d"][g + 1])
                            vector.reciprocal(out=row1[:], in_=row2[:])
                            sc1 = RDK if i < 16 else 1.0
                            vector.tensor_scalar(out=row1[:], in0=row1[:], scalar1=sc1,
                                                 scalar2=None, op0=ALU.mult).then_inc(r3d.h, 1)
                            r3d.n += 1
                            vector.wait_ge(nbc.h, cum["nbc"][g + 1])
                            vector.wait_ge(stq.h, 16 * cum["st"][g])
                            vector.tensor_tensor(out=ob[g % 2][:], in0=qc[:],
                                                 in1=bc_ps[:], op=ALU.mult).then_inc(fin.h, 1)
                            fin.n += 1
                        elif cls(i) == "bgk":
                            vector.wait_ge(bgkd.h, cum["bgkd"][g + 1])
                            vector.tensor_tensor(out=bge[32:48, 0:TT], in0=bge[32:48, 0:TT],
                                                 in1=bge[32:48, TT:2*TT], op=ALU.add)
                            vector.tensor_scalar(out=bge[32:48, 0:TT], in0=bge[32:48, 0:TT],
                                                 scalar1=-1.0/16.0, scalar2=None,
                                                 op0=ALU.mult).then_inc(fin.h, 1)
                            fin.n += 1

                @blk.scalar
                def _(scalar, tt=tt):
                    for i in range(NMT):
                        g = tt * NMT + i
                        if i <= 23:
                            scalar.wait_ge(convd.h, cum["convd"][g + 1])
                            if cls(i) == "nrm":
                                scalar.activation(out=qc[:], in_=cacc[:], func=AF.Silu)
                                scalar.activation(out=sq[:], in_=qc[:],
                                                  func=AF.Square).then_inc(sqd.h, 1)
                                sqd.n += 1
                                scalar.wait_ge(r1d.h, cum["r1d"][g + 1])
                                scalar.activation(out=row2[:], in_=row1[:],
                                                  func=AF.Sqrt).then_inc(r2d.h, 1)
                                r2d.n += 1
                            else:
                                scalar.wait_ge(stq.h, 16 * cum["st"][g])
                                scalar.activation(out=ob[g % 2][:], in_=cacc[:],
                                                  func=AF.Silu).then_inc(fin.h, 1)
                                fin.n += 1
                        elif cls(i) == "gg":
                            scalar.wait_ge(mm.h, MMP + cum["mm"][g + 1])
                            scalar.wait_ge(stq.h, 16 * cum["st"][g])
                            inst = scalar.copy(out=ob[g % 2][:], in_=pmb[i % 2][:, 0:TT])
                            inst.then_inc(fin.h, 1)
                            scalar.copy(out=dmy[:], in_=dmy[:]).then_inc(cons.h, 1)
                            fin.n += 1
                            cons.n += 1
                        else:
                            scalar.wait_ge(mm.h, MMP + cum["mm"][g + 1])
                            scalar.wait_ge(stq.h, 16 * cum["st"][g])
                            scalar.activation(out=bge[0:16, 0:TT], in_=pmb[i % 2][0:16, 0:TT],
                                              func=AF.Sigmoid)
                            scalar.activation(out=bge[32:48, TT:2*TT], in_=pmb[i % 2][32:48, 0:TT],
                                              func=AF.Relu, scale=-1.0)
                            scalar.activation(out=bge[32:48, 0:TT], in_=pmb[i % 2][32:48, 0:TT],
                                              func=AF.Abs)
                            scalar.activation(out=bge[32:48, 0:TT], in_=bge[32:48, 0:TT],
                                              func=AF.Exp, scale=-1.0)
                            inst = scalar.activation(out=bge[32:48, 0:TT], in_=bge[32:48, 0:TT],
                                                     func=AF.Ln, bias=1.0)
                            inst.then_inc(bgkd.h, 1)
                            scalar.copy(out=dmy[:], in_=dmy[:]).then_inc(cons.h, 1)
                            bgkd.n += 1
                            cons.n += 1

                @blk.gpsimd
                def _(gp, tt=tt):
                    for i in range(NMT):
                        g = tt * NMT + i
                        gp.wait_ge(fin.h, cum["fin"][g + 1])
                        c = cls(i)
                        if c == "nrm" and i < 16:
                            dst = io["qn"][i, :, tt*TT:(tt+1)*TT]
                        elif c == "nrm":
                            dst = io["kn"][i - 16, :, tt*TT:(tt+1)*TT]
                        elif c == "vv":
                            dst = io["vv"][i - 20, :, tt*TT:(tt+1)*TT]
                        elif c == "gg":
                            dst = io["gg"][i - 24, :, tt*TT:(tt+1)*TT]
                        else:
                            gp.dma_start(out=io["beta"][:, tt*TT:(tt+1)*TT],
                                         in_=bge[0:16, 0:TT]).then_inc(stq.h, 16)
                            gp.dma_start(out=io["glog"][:, tt*TT:(tt+1)*TT],
                                         in_=bge[32:48, 0:TT]).then_inc(stq.h, 16)
                            stq.n += 32
                            continue
                        gp.dma_start(out=dst, in_=ob[g % 2][:]).then_inc(stq.h, 16)
                        stq.n += 16

        with nc.Block() as blk:
            @blk.gpsimd
            def _(gp):
                gp.wait_ge(stq.h, stq.n)

def build_c3(nc, io):
    TT = 512
    ntt = TLOC // TT
    with ExitStack() as ctx:
        sb = lambda *a: ctx.enter_context(nc.sbuf_tensor(*a))
        ps = lambda *a: ctx.enter_context(nc.psum_tensor(*a))
        ones_col = sb("c3_onc", [128, 1], BF16)
        ones_f32 = sb("c3_onf", [1, 128], F32)
        onw = sb("c3_onw", [128, 1], F32)
        ot = [sb(f"c3_ot{i}", [128, 16 * TT], BF16) for i in range(2)]
        gtile = sb("c3_gt", [128, 16 * TT], BF16)
        s2t = sb("c3_s2t", [128, 16 * TT], BF16)
        lhsb = [sb(f"c3_lh{i}", [128, 16 * 128], BF16) for i in range(2)]
        sq = sb("c3_sq", [128, TT], BF16)
        u1 = sb("c3_u1", [128, TT], BF16)
        row1 = sb("c3_row1", [1, TT], F32)
        row2 = sb("c3_row2", [1, TT], F32)
        r = sb("c3_r", [128, 16 * TT], F32)
        rbf = sb("c3_rbf", [128, 16 * TT], BF16)
        sqf = sb("c3_sqf", [128, 16 * TT], BF16)
        mrow = sb("c3_mrow", [1, TT], F32)
        vtmp = sb("c3_vtmp", [1, TT], F32)
        rstd = sb("c3_rstd", [1, TT], F32)
        pmb = [ps(f"c3_pm{i}", [128, TT], F32) for i in range(4)]
        st_ps = [ps(f"c3_st{i}", [1, TT], F32) for i in range(2)]
        bc_ps = [ps(f"c3_bc{i}", [128, TT], F32) for i in range(2)]

        ld = Sem(ctx, nc, "c3_ld")
        lw = Sem(ctx, nc, "c3_lw")
        mm = Sem(ctx, nc, "c3_mm")
        nsum = Sem(ctx, nc, "c3_ns")
        nbc = Sem(ctx, nc, "c3_nb")
        sgd = Sem(ctx, nc, "c3_sgd")
        r1d = Sem(ctx, nc, "c3_r1")
        r3d = Sem(ctx, nc, "c3_r3")
        fin = Sem(ctx, nc, "c3_fin")
        sev = Sem(ctx, nc, "c3_sev")
        stq = Sem(ctx, nc, "c3_stq")

        with nc.Block() as blk:
            @blk.vector
            def _(vector):
                vector.memset(ones_col[:], 1.0)
                vector.memset(ones_f32[:], 1.0)

            @blk.sync
            def _(sync):
                sync.dma_start(out=onw[:], in_=io["onw"][:]).then_inc(ld.h, 16)
                for kc in range(16):
                    sync.dma_start(out=ot[0][:, kc*TT:(kc+1)*TT],
                                   in_=io["oo_all"][kc, :, 0:TT]).then_inc(ld.h, 16)
                ld.n += 17 * 16

        for tt in range(ntt):
            cur = ot[tt % 2]
            ld0 = ld.n
            with nc.Block() as blk:
                @blk.sync
                def _(sync, tt=tt):
                    if tt + 1 < ntt:
                        for kc in range(16):
                            sync.dma_start(out=ot[(tt+1) % 2][:, kc*TT:(kc+1)*TT],
                                           in_=io["oo_all"][kc, :, (tt+1)*TT:(tt+2)*TT]).then_inc(ld.h, 16)
                        ld.n += 256
                    for kc in range(16):
                        sync.dma_start(out=gtile[:, kc*TT:(kc+1)*TT],
                                       in_=io["gg"][kc, :, tt*TT:(tt+1)*TT]).then_inc(ld.h, 16)
                    for kc in range(16):
                        sync.dma_start(out=s2t[:, kc*TT:(kc+1)*TT],
                                       in_=io["s2f"][kc, :, tt*TT:(tt+1)*TT]).then_inc(ld.h, 16)
                    ld.n += 512

                @blk.scalar
                def _(scalar, tt=tt, ld0=ld0):
                    gld = 256 if tt + 1 < ntt else 0
                    scalar.wait_ge(ld.h, ld0 + gld + 256)
                    for hc in range(16):
                        g = tt * 16 + hc
                        scalar.activation(out=sq[:], in_=cur[:, hc*TT:(hc+1)*TT],
                                          func=AF.Square).then_inc(sgd.h, 1)
                        scalar.wait_ge(r1d.h, g + 1)
                        scalar.activation(out=row2[:], in_=row1[:], func=AF.Sqrt).then_inc(sgd.h, 1)
                        scalar.activation(out=u1[:], in_=gtile[:, hc*TT:(hc+1)*TT],
                                          func=AF.Silu).then_inc(sgd.h, 1)
                        sgd.n += 3

                @blk.tensor
                def _(tensor, tt=tt):
                    for hc in range(16):
                        g = tt * 16 + hc
                        tensor.wait_ge(sgd.h, 3 * g + 1)
                        tensor.matmul(out=st_ps[0][0:1, :], lhsT=ones_col[:], rhs=sq[:],
                                      start=True, stop=True).then_inc(nsum.h, 1)
                        nsum.n += 1
                        tensor.wait_ge(r3d.h, g + 1)
                        tensor.matmul(out=bc_ps[0][:], lhsT=ones_f32[:, 0:128], rhs=row1[:],
                                      start=True, stop=True).then_inc(nbc.h, 1)
                        nbc.n += 1

                @blk.vector
                def _(vector, tt=tt):
                    for hc in range(16):
                        g = tt * 16 + hc
                        vector.wait_ge(nsum.h, g + 1)
                        vector.tensor_scalar(out=row1[:], in0=st_ps[0][0:1, :], scalar1=1.0/128.0,
                                             scalar2=1e-5, op0=ALU.mult, op1=ALU.add).then_inc(r1d.h, 1)
                        r1d.n += 1
                        vector.wait_ge(sgd.h, 3 * g + 2)
                        vector.reciprocal(out=row1[:], in_=row2[:]).then_inc(r3d.h, 1)
                        r3d.n += 1
                        vector.wait_ge(nbc.h, g + 1)
                        vector.wait_ge(sgd.h, 3 * g + 3)
                        vector.tensor_scalar(out=sq[:], in0=cur[:, hc*TT:(hc+1)*TT],
                                             scalar1=onw[:, 0:1], scalar2=None, op0=ALU.mult)
                        vector.tensor_tensor(out=sq[:], in0=sq[:], in1=bc_ps[0][:], op=ALU.mult)
                        vector.tensor_tensor(out=cur[:, hc*TT:(hc+1)*TT], in0=sq[:],
                                             in1=u1[:], op=ALU.mult).then_inc(fin.h, 1)
                        fin.n += 1

            mm0 = mm.n
            sev0 = sev.n
            with nc.Block() as blk:
                @blk.sync
                def _(sync, mm0=mm0):
                    for mt in range(16):
                        if mt >= 2:
                            sync.wait_ge(mm.h, mm0 + mt - 1)
                        sync.dma_start(out=lhsb[mt % 2][:].rearrange("p (kc m) -> p kc m", m=128),
                                       in_=io["ow_T"][:, mt*128:(mt+1)*128].rearrange(
                                           "(kc p) m -> p kc m", p=128)).then_inc(lw.h, 16)
                    lw.n += 256

                @blk.tensor
                def _(tensor, cur=cur, mm0=mm0, sev0=sev0):
                    lw0 = lw.n - 256
                    for mt in range(16):
                        tensor.wait_ge(lw.h, lw0 + 16 * (mt + 1))
                        if mt >= 4:
                            tensor.wait_ge(sev.h, sev0 + mt - 3)
                        for kc in range(16):
                            inst = tensor.matmul(out=pmb[mt % 4][:],
                                                 lhsT=lhsb[mt % 2][:, kc*128:(kc+1)*128],
                                                 rhs=cur[:, kc*TT:(kc+1)*TT],
                                                 start=(kc == 0), stop=(kc == 15))
                        inst.then_inc(mm.h, 1)
                    mm.n += 16

                @blk.vector
                def _(vector, mm0=mm0, sev0=sev0):
                    for mt in range(16):
                        vector.wait_ge(mm.h, mm0 + mt + 1)
                        vector.tensor_tensor(out=r[:, mt*TT:(mt+1)*TT], in0=pmb[mt % 4][:],
                                             in1=s2t[:, mt*TT:(mt+1)*TT], op=ALU.add).then_inc(sev.h, 1)
                        sev.n += 1

            lnt = {"r": r, "rbf": rbf, "sq": sqf, "out": rbf, "ones_col": ones_col,
                   "ones_f32": ones_f32, "st_ps": st_ps, "bc_ps": bc_ps,
                   "mrow": mrow, "vtmp": vtmp, "rstd": rstd}
            with nc.Block() as blk:
                ln = LNEmit(lnt, TT, mm, sev)

                @blk.tensor
                def _(tensor, ln=ln):
                    ln.tensor(tensor)

                @blk.scalar
                def _(scalar, ln=ln):
                    ln.scalar(scalar)

                @blk.vector
                def _(vector, ln=ln, tt=tt):
                    if tt >= 1:
                        vector.wait_ge(stq.h, 256 * tt)
                    ln.vector(vector)

                @blk.gpsimd
                def _(gp, tt=tt, ln=ln):
                    gp.wait_ge(sev.h, ln.sev0 + 50)
                    for kc in range(16):
                        gp.dma_start(out=io["s3"][kc, :, tt*TT:(tt+1)*TT],
                                     in_=rbf[:, kc*TT:(kc+1)*TT]).then_inc(stq.h, 16)
                    stq.n += 256

        with nc.Block() as blk:
            @blk.gpsimd
            def _(gp):
                gp.wait_ge(stq.h, stq.n)


def build_d1(nc, units):
    with ExitStack() as ctx:
        sb = lambda *a: ctx.enter_context(nc.sbuf_tensor(*a))
        ps = lambda *a: ctx.enter_context(nc.psum_tensor(*a))
        ident_b = sb("d1_idb", [128, 128], BF16)
        idint = sb("d1_idi", [128, 128], I32)
        ln2b = sb("d1_ln2b", [128, 1], F32)
        pk = [sb(f"d1_pk{i}", [128, 1024], I32) for i in range(2)]
        scl = [sb(f"d1_sc{i}", [128, 64], F32) for i in range(2)]
        sce = sb("d1_sce", [128, 2048], F32)
        idx = sb("d1_idx", [128, 2048], I32)
        mi = sb("d1_mi", [128, 2048], I32)
        s_i = sb("d1_si", [128, 2048], I32)
        ef = sb("d1_ef", [128, 2048], F32)
        m0f = sb("d1_m0", [128, 2048], F32)
        sf = sb("d1_sf", [128, 2048], F32)
        p2 = sb("d1_p2", [128, 2048], F32)
        v1 = sb("d1_v1", [128, 2048], F32)
        v0 = sb("d1_v0", [128, 2048], F32)
        cm = sb("d1_cm", [128, 2048], I32)
        vv = [sb(f"d1_vv{i}", [128, 2048], BF16) for i in range(2)]
        ev = sb("d1_ev", [128, 2048], BF16)
        psB = ps("d1_psB", [128, 2048], BF16)

        ld = Sem(ctx, nc, "d1_ld")
        dv = Sem(ctx, nc, "d1_dv")
        sc_s = Sem(ctx, nc, "d1_sc")
        pe = Sem(ctx, nc, "d1_pe")
        stq = Sem(ctx, nc, "d1_st")

        with nc.Block() as blk:
            @blk.gpsimd
            def _(gp):
                gp.iota(idint[:], pattern=[[1, 128]], base=0, channel_multiplier=-1)

            @blk.sync
            def _(sync):
                sync.dma_start(out=pk[0][:], in_=units[0]["pk"]).then_inc(ld.h, 16)
                sync.dma_start(out=scl[0][:], in_=units[0]["sc"]).then_inc(ld.h, 16)
                ld.n += 32

        with nc.Block() as blk:
            @blk.vector
            def _(vector):
                vector.tensor_single_scalar(out=ident_b[:], in_=idint[:], scalar=0, op=ALU.is_equal)
                vector.memset(ln2b[:], -0.6931471805599453)

        for u, unit in enumerate(units):
            qb = u % 2
            ld0 = ld.n
            dv0 = dv.n
            sc0 = sc_s.n
            pe0 = pe.n
            st0 = stq.n
            with nc.Block() as blk:
                @blk.sync
                def _(sync, u=u, unit=unit, qb=qb):
                    if u + 1 < len(units):
                        sync.dma_start(out=pk[(u+1) % 2][:], in_=units[u+1]["pk"]).then_inc(ld.h, 16)
                        sync.dma_start(out=scl[(u+1) % 2][:], in_=units[u+1]["sc"]).then_inc(ld.h, 16)
                        ld.n += 32

                @blk.vector
                def _(vector, u=u, qb=qb, dv0=dv0, sc0=sc0, st0=st0):
                    vector.wait_ge(ld.h, 32 * (u + 1))
                    iv = idx[:].rearrange("p (k two) -> p k two", two=2)
                    vector.tensor_single_scalar(out=iv[:, :, 0:1], in_=pk[qb][:], scalar=15,
                                                op=ALU.bitwise_and)
                    vector.tensor_single_scalar(out=iv[:, :, 1:2],
                                                in_=pk[qb][:], scalar=4, op=ALU.logical_shift_right)
                    vector.tensor_single_scalar(out=iv[:, :, 1:2], in_=iv[:, :, 1:2],
                                                scalar=15, op=ALU.bitwise_and)
                    vector.tensor_single_scalar(out=mi[:], in_=idx[:], scalar=7, op=ALU.bitwise_and)
                    vector.tensor_single_scalar(out=s_i[:], in_=idx[:], scalar=3, op=ALU.logical_shift_right)
                    vector.tensor_copy(out=sf[:], in_=s_i[:])
                    vector.tensor_single_scalar(out=idx[:], in_=mi[:], scalar=1, op=ALU.logical_shift_right)
                    vector.tensor_copy(out=ef[:], in_=idx[:])
                    vector.tensor_single_scalar(out=mi[:], in_=mi[:], scalar=1, op=ALU.bitwise_and)
                    vector.tensor_copy(out=m0f[:], in_=mi[:])
                    se = sce[:].rearrange("p (k r) -> p k r", r=32)
                    si = scl[qb][:].rearrange("p (k one) -> p k one", one=1)
                    for j in range(32):
                        vector.tensor_copy(out=se[:, :, j:j+1], in_=si)
                    inst = vector.tensor_scalar(out=v0[:], in0=m0f[:], scalar1=0.5,
                                                scalar2=None, op0=ALU.mult)
                    inst.then_inc(dv.h, 1)
                    dv.n += 1

                @blk.scalar
                def _(scalar, qb=qb, dv0=dv0):
                    scalar.wait_ge(dv.h, dv0 + 1)
                    LN2 = 0.6931471805599453
                    scalar.activation(out=p2[:], in_=ef[:], func=AF.Exp, scale=LN2,
                                      bias=ln2b[:, 0:1]).then_inc(sc_s.h, 1)
                    sc_s.n += 1

                @blk.vector
                def _(vector, qb=qb, sc0=sc0, dv0=dv0):
                    vector.wait_ge(sc_s.h, sc0 + 1)
                    vector.tensor_scalar(out=v1[:], in0=m0f[:], scalar1=0.5, scalar2=1.0,
                                         op0=ALU.mult, op1=ALU.add)
                    vector.tensor_tensor(out=v1[:], in0=v1[:], in1=p2[:], op=ALU.mult)
                    vector.tensor_single_scalar(out=cm[:], in_=idx[:], scalar=1, op=ALU.is_lt)
                    vector.tensor_copy(out=p2[:], in_=cm[:])
                    vector.tensor_tensor(out=v0[:], in0=v0[:], in1=v1[:], op=ALU.subtract)
                    vector.tensor_tensor(out=v0[:], in0=v0[:], in1=p2[:], op=ALU.mult)
                    vector.tensor_tensor(out=v1[:], in0=v1[:], in1=v0[:], op=ALU.add)
                    vector.tensor_scalar(out=sf[:], in0=sf[:], scalar1=-2.0, scalar2=1.0,
                                         op0=ALU.mult, op1=ALU.add)
                    vector.tensor_tensor(out=v1[:], in0=v1[:], in1=sf[:], op=ALU.mult)
                    vector.tensor_tensor(out=vv[qb][:], in0=v1[:], in1=sce[:],
                                         op=ALU.mult).then_inc(dv.h, 1)
                    dv.n += 1

                @blk.tensor
                def _(tensor, u=u, qb=qb, dv0=dv0, sc0=sc0):
                    tensor.wait_ge(dv.h, dv0 + 2)
                    if u >= 1:
                        tensor.wait_ge(sc_s.h, sc0)
                    for kb in range(16):
                        inst = tensor.transpose(out=psB[:, kb*128:(kb+1)*128],
                                                in_=vv[qb][:, kb*128:(kb+1)*128],
                                                identity=ident_b[:])
                    inst.then_inc(pe.h, 1)
                    pe.n += 1

                @blk.scalar
                def _(scalar, u=u, qb=qb, pe0=pe0, sc0=sc0, st0=st0):
                    scalar.wait_ge(pe.h, pe0 + 1)
                    if u >= 1:
                        scalar.wait_ge(stq.h, 256 * u)
                    for kb in range(16):
                        inst = scalar.copy(out=ev[:, kb*128:(kb+1)*128],
                                           in_=psB[:, kb*128:(kb+1)*128])
                    inst.then_inc(sc_s.h, 1)
                    sc_s.n += 1

                @blk.gpsimd
                def _(gp, unit=unit, sc0=sc0, st0=st0):
                    gp.wait_ge(sc_s.h, sc0 + 2)
                    for kb in range(16):
                        gp.dma_start(out=unit["dst"][kb],
                                     in_=ev[:, kb*128:(kb+1)*128]).then_inc(stq.h, 16)
                    stq.n += 256

        with nc.Block() as blk:
            @blk.gpsimd
            def _(gp):
                gp.wait_ge(stq.h, stq.n)

def build_d2(nc, io):
    TT = 512
    ntt = TLOC // TT
    NHB = 8
    C0 = 0.7978845608028654
    with ExitStack() as ctx:
        sb = lambda *a: ctx.enter_context(nc.sbuf_tensor(*a))
        ps = lambda *a: ctx.enter_context(nc.psum_tensor(*a))
        ones_col = sb("d2_onc", [128, 1], BF16)
        ones_f32 = sb("d2_onf", [1, 128], F32)
        s1t = [sb(f"d2_s1{i}", [128, 16 * TT], BF16) for i in range(2)]
        wgt = [sb(f"d2_wg{i}", [128, 16 * 128], BF16) for i in range(2)]
        wut = [sb(f"d2_wu{i}", [128, 16 * 128], BF16) for i in range(2)]
        wdt = [sb(f"d2_wd{i}", [128, 16 * 128], BF16) for i in range(2)]
        gsb = sb("d2_gsb", [128, 2 * TT], BF16)
        gtmp = sb("d2_gtmp", [128, TT], BF16)
        gtmp2 = sb("d2_gtmp2", [128, 2 * TT], BF16)
        inter = sb("d2_inter", [128, 16 * TT], BF16)
        acc = sb("d2_acc", [128, 16 * TT], F32)
        rbf = sb("d2_rbf", [128, 16 * TT], BF16)
        sq = sb("d2_sq", [128, 16 * TT], BF16)
        outf = sb("d2_outf", [128, 16 * TT], F32)
        mrow = sb("d2_mrow", [1, TT], F32)
        vtmp = sb("d2_vtmp", [1, TT], F32)
        rstd = sb("d2_rstd", [1, TT], F32)
        pmb = [ps(f"d2_pm{i}", [128, TT], F32) for i in range(4)]
        st_ps = [ps(f"d2_st{i}", [1, TT], F32) for i in range(2)]
        bc_ps = [ps(f"d2_bc{i}", [128, TT], F32) for i in range(2)]

        ld = Sem(ctx, nc, "d2_ld")
        lw = Sem(ctx, nc, "d2_lw")
        mm = Sem(ctx, nc, "d2_mm")
        sgel = Sem(ctx, nc, "d2_sgel")
        sev = Sem(ctx, nc, "d2_sev")
        st = Sem(ctx, nc, "d2_st")

        def wsl(wap, colbase, kbase):
            return wap[kbase:kbase + 2048, colbase:colbase + 128].rearrange(
                "(kc p) m -> p kc m", p=128)

        with nc.Block() as blk:
            @blk.vector
            def _(vector):
                vector.memset(ones_col[:], 1.0)
                vector.memset(ones_f32[:], 1.0)

            @blk.sync
            def _(sync):
                for kc in range(16):
                    sync.dma_start(out=s1t[0][:, kc*TT:(kc+1)*TT],
                                   in_=io["s3"][kc, :, 0:TT]).then_inc(ld.h, 16)
                ld.n += 256

        for tt in range(ntt):
            cur = s1t[tt % 2]
            for hb in range(NHB):
                lw0 = lw.n
                mm0 = mm.n
                sev0 = sev.n
                sg0 = sgel.n
                with nc.Block() as blk:
                    @blk.sync
                    def _(sync, tt=tt, hb=hb, mm0=mm0):
                        for hm in range(16):
                            if hm >= 2:
                                sync.wait_ge(mm.h, mm0 + hm - 1)
                            sync.dma_start(out=wgt[hm % 2][:].rearrange("p (kc m) -> p kc m", m=128),
                                           in_=wsl(io["wg1_T"], hb*2048 + hm*128, 0)).then_inc(lw.h, 16)
                            sync.dma_start(out=wut[hm % 2][:].rearrange("p (kc m) -> p kc m", m=128),
                                           in_=wsl(io["wu1_T"], hb*2048 + hm*128, 0)).then_inc(lw.h, 16)
                        for dm in range(16):
                            if dm >= 2:
                                sync.wait_ge(mm.h, mm0 + 16 + dm - 1)
                            sync.dma_start(out=wdt[dm % 2][:].rearrange("p (kc m) -> p kc m", m=128),
                                           in_=wsl(io["wd1_T"], dm*128, hb*2048)).then_inc(lw.h, 16)
                        lw.n += 48 * 16

                    @blk.tensor
                    def _(tensor, tt=tt, hb=hb, cur=cur, lw0=lw0, mm0=mm0, sev0=sev0):
                        for hm in range(16):
                            tensor.wait_ge(lw.h, lw0 + 32 * (hm + 1))
                            if hm >= 2:
                                tensor.wait_ge(sev.h, sev0 + hm - 1)
                            ga = pmb[2 * (hm % 2)]
                            ua = pmb[2 * (hm % 2) + 1]
                            for kc in range(16):
                                tensor.matmul(out=ga[:], lhsT=wgt[hm % 2][:, kc*128:(kc+1)*128],
                                              rhs=cur[:, kc*TT:(kc+1)*TT],
                                              start=(kc == 0), stop=(kc == 15))
                            for kc in range(16):
                                inst = tensor.matmul(out=ua[:], lhsT=wut[hm % 2][:, kc*128:(kc+1)*128],
                                                     rhs=cur[:, kc*TT:(kc+1)*TT],
                                                     start=(kc == 0), stop=(kc == 15))
                            inst.then_inc(mm.h, 1)
                        mm.n += 16
                        for dm in range(16):
                            tensor.wait_ge(lw.h, lw0 + 32 * 16 + 16 * (dm + 1))
                            tensor.wait_ge(sev.h, sev0 + 16)
                            if dm >= 2:
                                tensor.wait_ge(sev.h, sev0 + 16 + dm - 1)
                            pa = pmb[dm % 2]
                            for kc in range(16):
                                inst = tensor.matmul(out=pa[:], lhsT=wdt[dm % 2][:, kc*128:(kc+1)*128],
                                                     rhs=inter[:, kc*TT:(kc+1)*TT],
                                                     start=(kc == 0), stop=(kc == 15))
                            inst.then_inc(mm.h, 1)
                        mm.n += 16

                    @blk.scalar
                    def _(scalar, mm0=mm0, sg0=sg0, sev0=sev0):
                        for hm in range(16):
                            scalar.wait_ge(mm.h, mm0 + hm + 1)
                            if hm >= 1:
                                scalar.wait_ge(sev.h, sev0 + hm)
                            ga = pmb[2 * (hm % 2)]
                            sl = slice((hm % 2)*TT, (hm % 2 + 1)*TT)
                            scalar.copy(out=gsb[:, sl], in_=ga[:])
                            scalar.activation(out=gtmp[:], in_=ga[:], func=AF.Square).then_inc(sgel.h, 1)
                            scalar.wait_ge(sgel.h, sg0 + 3 * hm + 2)
                            scalar.activation(out=gtmp[:], in_=gtmp2[:, sl], func=AF.Tanh,
                                              scale=C0).then_inc(sgel.h, 1)
                        sgel.n += 32

                    @blk.vector
                    def _(vector, hb=hb, mm0=mm0, sg0=sg0, sev0=sev0):
                        for hm in range(16):
                            vector.wait_ge(sgel.h, sg0 + 3 * hm + 1)
                            sl = slice((hm % 2)*TT, (hm % 2 + 1)*TT)
                            vector.tensor_scalar(out=gtmp2[:, sl], in0=gtmp[:], scalar1=0.044715,
                                                 scalar2=1.0, op0=ALU.mult, op1=ALU.add)
                            vector.tensor_tensor(out=gtmp2[:, sl], in0=gtmp2[:, sl],
                                                 in1=gsb[:, sl], op=ALU.mult).then_inc(sgel.h, 1)
                            vector.wait_ge(sgel.h, sg0 + 3 * hm + 3)
                            vector.tensor_scalar(out=gtmp2[:, sl], in0=gtmp[:], scalar1=0.5,
                                                 scalar2=0.5, op0=ALU.mult, op1=ALU.add)
                            vector.tensor_tensor(out=gtmp2[:, sl], in0=gtmp2[:, sl],
                                                 in1=gsb[:, sl], op=ALU.mult)
                            ua = pmb[2 * (hm % 2) + 1]
                            vector.tensor_tensor(out=inter[:, hm*TT:(hm+1)*TT], in0=gtmp2[:, sl],
                                                 in1=ua[:], op=ALU.mult).then_inc(sev.h, 1)
                        sev.n += 16
                        sgel.n += 16
                        for dm in range(16):
                            vector.wait_ge(mm.h, mm0 + 16 + dm + 1)
                            pa = pmb[dm % 2]
                            if hb == 0:
                                vector.tensor_copy(out=acc[:, dm*TT:(dm+1)*TT], in_=pa[:]).then_inc(sev.h, 1)
                            else:
                                vector.tensor_tensor(out=acc[:, dm*TT:(dm+1)*TT],
                                                     in0=acc[:, dm*TT:(dm+1)*TT],
                                                     in1=pa[:], op=ALU.add).then_inc(sev.h, 1)
                        sev.n += 16

            lnt = {"r": acc, "rbf": rbf, "sq": sq, "out": outf, "ones_col": ones_col,
                   "ones_f32": ones_f32, "st_ps": st_ps, "bc_ps": bc_ps,
                   "mrow": mrow, "vtmp": vtmp, "rstd": rstd}
            with nc.Block() as blk:
                res0 = sev.n
                sev.n += 16
                ln = LNEmit(lnt, TT, mm, sev)

                @blk.sync
                def _(sync, tt=tt):
                    if tt + 1 < ntt:
                        for kc in range(16):
                            sync.dma_start(out=s1t[(tt+1) % 2][:, kc*TT:(kc+1)*TT],
                                           in_=io["s3"][kc, :, (tt+1)*TT:(tt+2)*TT]).then_inc(ld.h, 16)
                        ld.n += 256

                @blk.vector
                def _(vector, tt=tt, cur=cur, ln=ln):
                    if tt >= 1:
                        vector.wait_ge(st.h, 256 * tt)
                    for kc in range(16):
                        vector.tensor_tensor(out=acc[:, kc*TT:(kc+1)*TT],
                                             in0=acc[:, kc*TT:(kc+1)*TT],
                                             in1=cur[:, kc*TT:(kc+1)*TT], op=ALU.add).then_inc(sev.h, 1)
                    ln.vector(vector)

                @blk.scalar
                def _(scalar, ln=ln):
                    ln.scalar(scalar)

                @blk.tensor
                def _(tensor, ln=ln):
                    ln.tensor(tensor)

                @blk.gpsimd
                def _(gp, tt=tt, ln=ln):
                    gp.wait_ge(sev.h, ln.sev0 + 50)
                    for kc in range(16):
                        gp.dma_start(out=io["outf"][kc, :, tt*TT:(tt+1)*TT],
                                     in_=outf[:, kc*TT:(kc+1)*TT]).then_inc(st.h, 16)
                    st.n += 256

        with nc.Block() as blk:
            @blk.gpsimd
            def _(gp):
                gp.wait_ge(st.h, st.n)

def _ct(a, dt=np.float32):
    return np.ascontiguousarray(a, dtype=dt)


def kernel(**inputs):
    d = inputs
    x = np.asarray(d["L0_attn_x"], np.float32)
    xT = host_xT(x)

    projc = np.concatenate([
        np.asarray(d["L1_attn_qkv_w"], np.float32).T,
        np.asarray(d["L1_attn_g_w"], np.float32).T,
        np.concatenate([np.asarray(d["L1_attn_b_w"], np.float32).T,
                        np.zeros((2048, 16), np.float32),
                        np.asarray(d["L1_attn_gk_w"], np.float32).T,
                        np.zeros((2048, 16), np.float32)], axis=1)], axis=1)
    Wf = {
        "xt": (xT, 2048, TLOC, 1),
        "outw": (np.asarray(d["L0_attn_out_W"], np.float32).T, 512, 2048, 0),
        "wg": (np.asarray(d["L0_ffn_w_gate"], np.float32).T, 256, 8192, 0),
        "wu": (np.asarray(d["L0_ffn_w_up"], np.float32).T, 256, 8192, 0),
        "wd": (np.asarray(d["L0_ffn_w_down"], np.float32).T, 1024, 2048, 0),
        "pj": (projc, 256, 5184, 0),
        "ow": (np.asarray(d["L1_attn_o_w"], np.float32).T, 256, 2048, 0),
    }
    nc = mknc()
    pairs = []
    for name, (arr, r, c, ax) in Wf.items():
        src = nc.dram_tensor("s_" + name, [r, c], F32, kind="ExternalInput").ap()
        dst = nc.dram_tensor("d_" + name, [r, c], BF16, kind="ExternalOutput").ap()
        pairs.append((src, dst, r, c))
    build_cast(nc, pairs)
    in_maps = []
    for cc in range(NCORE):
        m = {}
        for name, (arr, r, c, ax) in Wf.items():
            if ax == 1:
                m["s_" + name] = _ct(arr[:, cc*c:(cc+1)*c])
            else:
                m["s_" + name] = _ct(arr[cc*r:(cc+1)*r, :])
        in_maps.append(m)
    res = run_bass_kernel_spmd(nc, in_maps, core_ids=list(range(NCORE)))
    WB = {}
    for name, (arr, r, c, ax) in Wf.items():
        WB[name] = np.concatenate([np.asarray(rr["d_" + name]) for rr in res.results], axis=ax)
    xT_bf3 = _ct(WB["xt"].reshape(NKC, 128, T), BF16NP)

    nc = mknc()
    io = {
        "xT_bf": nc.dram_tensor("xT_bf", [NKC, 128, T], BF16, kind="ExternalInput").ap(),
        "w_in_T": nc.dram_tensor("w_in_T", [2048, 1024], F32, kind="ExternalInput").ap(),
        "wx_T": nc.dram_tensor("wx_T", [512, 33], F32, kind="ExternalInput").ap(),
        "conv_w": nc.dram_tensor("conv_wD", [512, 4], F32, kind="ExternalInput").ap(),
        "conv_b": nc.dram_tensor("conv_bD", [512, 1], F32, kind="ExternalInput").ap(),
        "x_conv": nc.dram_tensor("x_conv", [NPC, 128, T], BF16, kind="ExternalOutput").ap(),
        "sz": nc.dram_tensor("szD", [NPC, 128, T], BF16, kind="ExternalOutput").ap(),
        "p_part": nc.dram_tensor("p_part", [80, T], F32, kind="ExternalOutput").ap(),
    }
    build_a1(nc, io)
    W_in = np.asarray(d["L0_attn_W_in"], np.float32)
    W_x = np.asarray(d["L0_attn_W_x"], np.float32)
    in_maps = []
    for cc in range(NCORE):
        rows = list(range(cc*512, (cc+1)*512)) + list(range(DI + cc*512, DI + (cc+1)*512))
        in_maps.append({
            "xT_bf": xT_bf3,
            "w_in_T": _ct(W_in[rows, :].T),
            "wx_T": _ct(W_x[:, cc*512:(cc+1)*512].T),
            "conv_wD": _ct(np.asarray(d["L0_attn_conv_w"], np.float32)[cc*512:(cc+1)*512, 0, :]),
            "conv_bD": _ct(np.asarray(d["L0_attn_conv_b"], np.float32)[cc*512:(cc+1)*512, None]),
        })
    res = run_bass_kernel_spmd(nc, in_maps, core_ids=list(range(NCORE)))
    x_conv = np.concatenate([np.asarray(r["x_conv"]).reshape(512, T) for r in res.results], axis=0)
    szv = np.concatenate([np.asarray(r["szD"]).reshape(512, T) for r in res.results], axis=0)
    p80 = np.stack([np.asarray(r["p_part"]) for r in res.results], axis=0)

    nc = mknc()
    io = {
        "x_conv": nc.dram_tensor("x_conv", [NPC, 128, T], BF16, kind="ExternalInput").ap(),
        "sz": nc.dram_tensor("szD", [NPC, 128, T], BF16, kind="ExternalInput").ap(),
        "p_all": nc.dram_tensor("p_all", [8, 80, T], F32, kind="ExternalInput").ap(),
        "A_log": nc.dram_tensor("A_logD", [512, 16], F32, kind="ExternalInput").ap(),
        "Dvec": nc.dram_tensor("DvecD", [512, 1], F32, kind="ExternalInput").ap(),
        "dt_bias": nc.dram_tensor("dt_biasD", [1, 4096], F32, kind="ExternalInput").ap(),
        "y_pre": nc.dram_tensor("y_pre", [NPC, 128, T], BF16, kind="ExternalOutput").ap(),
    }
    build_a3(nc, io)
    dtb = _ct(np.asarray(d["L0_attn_dt_bias"], np.float32).reshape(1, 4096))
    in_maps = []
    for cc in range(NCORE):
        sl = slice(cc*512, (cc+1)*512)
        in_maps.append({
            "x_conv": _ct(x_conv[sl].reshape(NPC, 128, T), BF16NP),
            "szD": _ct(szv[sl].reshape(NPC, 128, T), BF16NP),
            "p_all": p80,
            "A_logD": _ct(np.asarray(d["L0_attn_A_log"], np.float32)[sl]),
            "DvecD": _ct(np.asarray(d["L0_attn_D"], np.float32)[sl, None]),
            "dt_biasD": dtb,
        })
    res = run_bass_kernel_spmd(nc, in_maps, core_ids=list(range(NCORE)))
    y_pre = np.concatenate([np.asarray(r["y_pre"]).reshape(512, T) for r in res.results], axis=0)

    nc = mknc()
    io = {
        "y_pre_all": nc.dram_tensor("y_pre_all", [32, 128, TLOC], BF16, kind="ExternalInput").ap(),
        "xT_c": nc.dram_tensor("xT_c", [2048, TLOC], F32, kind="ExternalInput").ap(),
        "outw_T": nc.dram_tensor("outw_Td", [4096, 2048], BF16, kind="ExternalInput").ap(),
        "wg_T": nc.dram_tensor("wg_Td", [2048, 8192], BF16, kind="ExternalInput").ap(),
        "wu_T": nc.dram_tensor("wu_Td", [2048, 8192], BF16, kind="ExternalInput").ap(),
        "wd_T": nc.dram_tensor("wd_Td", [8192, 2048], BF16, kind="ExternalInput").ap(),
        "s1_d": nc.dram_tensor("s1_d", [16, 128, TLOC], BF16, kind="ExternalOutput").ap(),
        "s2": nc.dram_tensor("s2", [16, 128, TLOC], BF16, kind="ExternalOutput").ap(),
        "tok0": 0,
    }
    build_b(nc, io)
    ypre3 = y_pre.reshape(32, 128, T)
    in_maps = []
    for cc in range(NCORE):
        in_maps.append({
            "y_pre_all": _ct(ypre3[:, :, cc*TLOC:(cc+1)*TLOC], BF16NP),
            "xT_c": _ct(xT[:, cc*TLOC:(cc+1)*TLOC]),
            "outw_Td": WB["outw"], "wg_Td": WB["wg"],
            "wu_Td": WB["wu"], "wd_Td": WB["wd"],
        })
    res = run_bass_kernel_spmd(nc, in_maps, core_ids=list(range(NCORE)))
    s2 = np.concatenate([np.asarray(r["s2"]).reshape(2048, TLOC) for r in res.results], axis=1)

    nc = mknc()
    io = {
        "s2_h": nc.dram_tensor("s2_h", [16, 128, TLOC + 3], BF16, kind="ExternalInput").ap(),
        "projw_T": nc.dram_tensor("projw_T", [2048, 5184], BF16, kind="ExternalInput").ap(),
        "conv_w": nc.dram_tensor("conv1w", [3072, 4], F32, kind="ExternalInput").ap(),
        "qn": nc.dram_tensor("qn", [16, 128, TLOC], BF16, kind="ExternalOutput").ap(),
        "kn": nc.dram_tensor("kn", [4, 128, TLOC], BF16, kind="ExternalOutput").ap(),
        "vv": nc.dram_tensor("vvD", [4, 128, TLOC], BF16, kind="ExternalOutput").ap(),
        "glog": nc.dram_tensor("glogD", [16, TLOC], F32, kind="ExternalOutput").ap(),
        "beta": nc.dram_tensor("betaD", [16, TLOC], F32, kind="ExternalOutput").ap(),
        "gg": nc.dram_tensor("ggD", [16, 128, TLOC], BF16, kind="ExternalOutput").ap(),
    }
    build_c1(nc, io)
    s2_3 = s2.reshape(16, 128, T)
    cw1 = _ct(np.asarray(d["L1_attn_conv_w"], np.float32))
    in_maps = []
    for cc in range(NCORE):
        h = np.zeros((16, 128, TLOC + 3), BF16NP)
        h[:, :, 3:] = s2_3[:, :, cc*TLOC:(cc+1)*TLOC]
        if cc % 4 != 0:
            h[:, :, 0:3] = s2_3[:, :, cc*TLOC-3:cc*TLOC]
        in_maps.append({"s2_h": h, "projw_T": WB["pj"], "conv1w": cw1})
    res = run_bass_kernel_spmd(nc, in_maps, core_ids=list(range(NCORE)))
    qn = np.concatenate([np.asarray(r["qn"]) for r in res.results], axis=2)
    kn = np.concatenate([np.asarray(r["kn"]) for r in res.results], axis=2)
    vvv = np.concatenate([np.asarray(r["vvD"]) for r in res.results], axis=2)
    glog = np.concatenate([np.asarray(r["glogD"]) for r in res.results], axis=1)
    beta = np.concatenate([np.asarray(r["betaD"]) for r in res.results], axis=1)
    gg = np.concatenate([np.asarray(r["ggD"]) for r in res.results], axis=2)

    nc = mknc()
    io = {
        "qh": nc.dram_tensor("qh", [4, 128, S], BF16, kind="ExternalInput").ap(),
        "kh": nc.dram_tensor("kh", [128, S], BF16, kind="ExternalInput").ap(),
        "vh": nc.dram_tensor("vh", [128, S], BF16, kind="ExternalInput").ap(),
        "gl": nc.dram_tensor("gl", [4, S], F32, kind="ExternalInput").ap(),
        "bt": nc.dram_tensor("bt", [4, S], F32, kind="ExternalInput").ap(),
        "oo": nc.dram_tensor("oo", [4, 128, S], BF16, kind="ExternalOutput").ap(),
    }
    build_c2(nc, io)
    in_maps = []
    for cc in range(NCORE):
        b = cc // 4
        g = cc % 4
        hs = [4*g + j for j in range(4)]
        csl = slice(b*S, (b+1)*S)
        in_maps.append({
            "qh": _ct(qn[hs, :, csl], BF16NP),
            "kh": _ct(kn[g, :, csl], BF16NP),
            "vh": _ct(vvv[g, :, csl], BF16NP),
            "gl": _ct(glog[hs, csl]),
            "bt": _ct(beta[hs, csl]),
        })
    res = run_bass_kernel_spmd(nc, in_maps, core_ids=list(range(NCORE)))
    oo = np.zeros((16, 128, T), BF16NP)
    for cc in range(NCORE):
        b = cc // 4
        g = cc % 4
        oo[4*g:4*g+4, :, b*S:(b+1)*S] = np.asarray(res.results[cc]["oo"])

    nc = mknc()
    io = {
        "oo_all": nc.dram_tensor("oo_all", [16, 128, TLOC], BF16, kind="ExternalInput").ap(),
        "gg": nc.dram_tensor("ggD", [16, 128, TLOC], BF16, kind="ExternalInput").ap(),
        "s2f": nc.dram_tensor("s2f", [16, 128, TLOC], BF16, kind="ExternalInput").ap(),
        "ow_T": nc.dram_tensor("ow_Td", [2048, 2048], BF16, kind="ExternalInput").ap(),
        "onw": nc.dram_tensor("onwD", [128, 1], F32, kind="ExternalInput").ap(),
        "s3": nc.dram_tensor("s3", [16, 128, TLOC], BF16, kind="ExternalOutput").ap(),
    }
    build_c3(nc, io)
    onw = _ct(np.asarray(d["L1_attn_out_norm_w"], np.float32)[:, None])
    in_maps = []
    for cc in range(NCORE):
        sl = slice(cc*TLOC, (cc+1)*TLOC)
        in_maps.append({
            "oo_all": _ct(oo[:, :, sl], BF16NP),
            "ggD": _ct(gg[:, :, sl], BF16NP),
            "s2f": _ct(s2_3[:, :, sl], BF16NP),
            "ow_Td": WB["ow"],
            "onwD": onw,
        })
    res = run_bass_kernel_spmd(nc, in_maps, core_ids=list(range(NCORE)))
    s3 = np.concatenate([np.asarray(r["s3"]) for r in res.results], axis=2)

    nc = mknc()
    gpk = nc.dram_tensor("gpk", [2048, 1024], I32, kind="ExternalInput").ap()
    gsc = nc.dram_tensor("gsc", [2048, 64], F32, kind="ExternalInput").ap()
    upk = nc.dram_tensor("upk", [2048, 1024], I32, kind="ExternalInput").ap()
    usc = nc.dram_tensor("usc", [2048, 64], F32, kind="ExternalInput").ap()
    dpk = nc.dram_tensor("dpk", [256, 8192], I32, kind="ExternalInput").ap()
    dsc = nc.dram_tensor("dsc", [256, 512], F32, kind="ExternalInput").ap()
    wgs = nc.dram_tensor("wgs", [2048, 2048], BF16, kind="ExternalOutput").ap()
    wus = nc.dram_tensor("wus", [2048, 2048], BF16, kind="ExternalOutput").ap()
    wds = nc.dram_tensor("wds", [16384, 256], BF16, kind="ExternalOutput").ap()
    units = []
    for t in range(16):
        units.append({"pk": gpk[t*128:(t+1)*128, :], "sc": gsc[t*128:(t+1)*128, :],
                      "dst": [wgs[kb*128:(kb+1)*128, t*128:(t+1)*128] for kb in range(16)]})
    for t in range(16):
        units.append({"pk": upk[t*128:(t+1)*128, :], "sc": usc[t*128:(t+1)*128, :],
                      "dst": [wus[kb*128:(kb+1)*128, t*128:(t+1)*128] for kb in range(16)]})
    for t in range(2):
        for j in range(8):
            units.append({"pk": dpk[t*128:(t+1)*128, j*1024:(j+1)*1024],
                          "sc": dsc[t*128:(t+1)*128, j*64:(j+1)*64],
                          "dst": [wds[j*2048 + kb*128: j*2048 + (kb+1)*128,
                                      t*128:(t+1)*128] for kb in range(16)]})
    build_d1(nc, units)
    in_maps = []
    for cc in range(NCORE):
        in_maps.append({
            "gpk": _ct(np.asarray(d["L1_ffn_gate_W_packed"])[cc*2048:(cc+1)*2048], np.int32),
            "gsc": _ct(np.asarray(d["L1_ffn_gate_scales"], np.float32)[cc*2048:(cc+1)*2048]),
            "upk": _ct(np.asarray(d["L1_ffn_up_W_packed"])[cc*2048:(cc+1)*2048], np.int32),
            "usc": _ct(np.asarray(d["L1_ffn_up_scales"], np.float32)[cc*2048:(cc+1)*2048]),
            "dpk": _ct(np.asarray(d["L1_ffn_down_W_packed"])[cc*256:(cc+1)*256], np.int32),
            "dsc": _ct(np.asarray(d["L1_ffn_down_scales"], np.float32)[cc*256:(cc+1)*256]),
        })
    res = run_bass_kernel_spmd(nc, in_maps, core_ids=list(range(NCORE)))
    wg1 = np.concatenate([np.asarray(r["wgs"]) for r in res.results], axis=1)
    wu1 = np.concatenate([np.asarray(r["wus"]) for r in res.results], axis=1)
    wd1 = np.concatenate([np.asarray(r["wds"]) for r in res.results], axis=1)

    nc = mknc()
    io = {
        "s3": nc.dram_tensor("s3D", [16, 128, TLOC], BF16, kind="ExternalInput").ap(),
        "wg1_T": nc.dram_tensor("wg1_T", [2048, 16384], BF16, kind="ExternalInput").ap(),
        "wu1_T": nc.dram_tensor("wu1_T", [2048, 16384], BF16, kind="ExternalInput").ap(),
        "wd1_T": nc.dram_tensor("wd1_T", [16384, 2048], BF16, kind="ExternalInput").ap(),
        "outf": nc.dram_tensor("outf", [16, 128, TLOC], F32, kind="ExternalOutput").ap(),
    }
    build_d2(nc, io)
    in_maps = []
    for cc in range(NCORE):
        sl = slice(cc*TLOC, (cc+1)*TLOC)
        in_maps.append({
            "s3D": _ct(s3[:, :, sl], BF16NP),
            "wg1_T": wg1, "wu1_T": wu1, "wd1_T": wd1,
        })
    res = run_bass_kernel_spmd(nc, in_maps, core_ids=list(range(NCORE)))
    outf = np.concatenate([np.asarray(r["outf"]) for r in res.results], axis=2)
    full = outf.reshape(2048, T)
    out = full.reshape(2048, B, S).transpose(1, 2, 0)
    return _ct(out)


# version 18
# speedup vs baseline: 1.0101x; 1.0101x over previous
import numpy as np
import ml_dtypes
from contextlib import ExitStack

import concourse.bass as bass
import concourse.mybir as mybir
from concourse.bass_utils import run_bass_kernel_spmd

F32 = mybir.dt.float32
BF16 = mybir.dt.bfloat16
I32 = mybir.dt.int32
AF = mybir.ActivationFunctionType
ALU = mybir.AluOpType
BF16NP = ml_dtypes.bfloat16

NCORE = 8
B, S, DM = 2, 8192, 2048
T = B * S
TLOC = T // NCORE
DI, DST = 4096, 16
DLOC = DI // NCORE
NPC = DLOC // 128
NKC = DM // 128


def dynv(eng, expr, lo, hi):
    return eng.snap(eng.compute_val(expr), min_val=lo, max_val=hi)


def mknc():
    return bass.Bass("TRN2", target_bir_lowering=False, debug=False)


class Sem:
    def __init__(self, ctx, nc, name):
        self.h = ctx.enter_context(nc.semaphore(name))
        self.n = 0

    def inc(self, inst, k=1):
        inst.then_inc(self.h, k)
        self.n += k
        return self.n


def build_cast(nc, pairs):
    CW = 8192
    with ExitStack() as ctx:
        stg = [ctx.enter_context(nc.sbuf_tensor(f"cstg{i}", [128, CW], F32)) for i in range(2)]
        out = [ctx.enter_context(nc.sbuf_tensor(f"cout{i}", [128, CW], BF16)) for i in range(2)]
        ld = Sem(ctx, nc, "c_ld")
        cv = Sem(ctx, nc, "c_cv")
        st = Sem(ctx, nc, "c_st")
        jobs = []
        for (src, dst, rows, cols) in pairs:
            for r0 in range(0, rows, 128):
                for c0 in range(0, cols, CW):
                    jobs.append((src, dst, r0, c0, min(CW, cols - c0)))
        with nc.Block() as blk:
            @blk.sync
            def _(sync):
                for i, (src, dst, r0, c0, w) in enumerate(jobs):
                    bi = i % 2
                    if i >= 2:
                        sync.wait_ge(cv.h, i - 1)
                    sync.dma_start(out=stg[bi][:, 0:w], in_=src[r0:r0+128, c0:c0+w]).then_inc(ld.h, 16)
                    ld.n += 16

            @blk.vector
            def _(vector):
                for i, (src, dst, r0, c0, w) in enumerate(jobs):
                    bi = i % 2
                    vector.wait_ge(ld.h, 16 * (i + 1))
                    if i >= 2:
                        vector.wait_ge(st.h, 16 * (i - 1))
                    cv.inc(vector.tensor_copy(out=out[bi][:, 0:w], in_=stg[bi][:, 0:w]))

            @blk.gpsimd
            def _(gp):
                for i, (src, dst, r0, c0, w) in enumerate(jobs):
                    bi = i % 2
                    gp.wait_ge(cv.h, i + 1)
                    gp.dma_start(out=dst[r0:r0+128, c0:c0+w], in_=out[bi][:, 0:w]).then_inc(st.h, 16)
                    st.n += 16
                gp.wait_ge(st.h, st.n)


def build_a1(nc, io):
    TC = 512
    nch = T // TC
    with ExitStack() as ctx:
        sb = lambda *a: ctx.enter_context(nc.sbuf_tensor(*a))
        ps = lambda *a: ctx.enter_context(nc.psum_tensor(*a))
        w_in_bf = sb("w_in_bf", [128, NKC * 1024], BF16)
        wx_bf = sb("wx_bf", [128, NPC * 80], BF16)
        conv_w = sb("conv_wT", [128, NPC * 4], F32)
        conv_b = sb("conv_bT", [128, NPC], F32)
        stg = sb("stgT", [128, NKC * 1024], F32)
        rhs = [sb(f"rhsT{i}", [128, NKC * TC], BF16) for i in range(2)]
        xs = sb("xsT", [128, NPC * (TC + 3)], F32)
        cacc = sb("caccT", [128, NPC * TC], F32)
        xcv = sb("xcvT", [128, NPC * TC], BF16)
        szt = sb("sztT", [128, NPC * TC], BF16)
        p_sb = sb("p_sbT", [80, TC], F32)
        pm = [ps(f"pmT{i}", [128, TC], F32) for i in range(4)]
        pp = ps("ppT", [80, TC], F32)

        ld = Sem(ctx, nc, "a_ld")
        mm = Sem(ctx, nc, "a_mm")
        convd = Sem(ctx, nc, "a_convd")
        silud = Sem(ctx, nc, "a_silud")
        szd = Sem(ctx, nc, "a_szd")
        pevd = Sem(ctx, nc, "a_pevd")
        st = Sem(ctx, nc, "a_st")

        with nc.Block() as blk:
            @blk.sync
            def _(sync):
                for kc in range(NKC):
                    sync.dma_start(out=stg[:, kc*1024:(kc+1)*1024],
                                   in_=io["w_in_T"][kc*128:(kc+1)*128, :]).then_inc(ld.h, 16)
                for pc in range(NPC):
                    sync.dma_start(out=conv_w[:, pc*4:(pc+1)*4],
                                   in_=io["conv_w"][pc*128:(pc+1)*128, :]).then_inc(ld.h, 16)
                    sync.dma_start(out=conv_b[:, pc:pc+1],
                                   in_=io["conv_b"][pc*128:(pc+1)*128, :]).then_inc(ld.h, 16)
                ld.n += (NKC + 2 * NPC) * 16

            @blk.vector
            def _(vector):
                vector.wait_ge(ld.h, (NKC + 2 * NPC) * 16)
                vector.tensor_copy(out=w_in_bf[:], in_=stg[:])
                vector.memset(xs[:], 0.0)
                vector.memset(wx_bf[:], 0.0)

        with nc.Block() as blk:
            @blk.sync
            def _(sync):
                for pc in range(NPC):
                    sync.dma_start(out=stg[:, pc*33:(pc+1)*33], in_=io["wx_T"][pc*128:(pc+1)*128, :]).then_inc(ld.h, 16)
                ld.n += NPC * 16
                for kc in range(NKC):
                    sync.dma_start(out=rhs[0][:, kc*TC:(kc+1)*TC],
                                   in_=io["xT_bf"][0, kc, :, 0:TC]).then_inc(ld.h, 16)
                ld.n += NKC * 16

            @blk.vector
            def _(vector):
                vector.wait_ge(ld.h, (NKC + 2 * NPC + NPC) * 16)
                for pc in range(NPC):
                    vector.tensor_copy(out=wx_bf[:, pc*80+0:pc*80+16], in_=stg[:, pc*33:pc*33+16])
                    vector.tensor_copy(out=wx_bf[:, pc*80+32:pc*80+48], in_=stg[:, pc*33+16:pc*33+32])
                    vector.tensor_copy(out=wx_bf[:, pc*80+64:pc*80+65], in_=stg[:, pc*33+32:pc*33+33])

        base_ld = ld.n
        LDC = NKC * 16
        for ci in range(nch):
            first_in_b = (ci * TC) % S == 0
            cur = rhs[ci % 2]
            nxt = rhs[(ci + 1) % 2]
            t0 = ci * TC
            with nc.Block() as blk:
                @blk.sync
                def _(sync, ci=ci, nxt=nxt, t0=t0):
                    if ci + 1 < nch:
                        tn = t0 + TC
                        cblk, u0 = tn // TLOC, tn % TLOC
                        for kc in range(NKC):
                            sync.dma_start(out=nxt[:, kc*TC:(kc+1)*TC],
                                           in_=io["xT_bf"][cblk, kc, :, u0:u0+TC]).then_inc(ld.h, 16)
                        ld.n += LDC

                @blk.tensor
                def _(tensor, ci=ci, cur=cur):
                    tensor.wait_ge(ld.h, base_ld + LDC * ci)
                    for mt in range(4):
                        for kc in range(NKC):
                            inst = tensor.matmul(
                                out=pm[mt][:],
                                lhsT=w_in_bf[:, kc*1024 + mt*128: kc*1024 + (mt+1)*128],
                                rhs=cur[:, kc*TC:(kc+1)*TC],
                                start=(kc == 0), stop=(kc == NKC - 1))
                        mm.inc(inst)
                    tensor.wait_ge(convd.h, ci + 1)
                    for mt in range(4):
                        for kc in range(NKC):
                            inst = tensor.matmul(
                                out=pm[mt][:],
                                lhsT=w_in_bf[:, kc*1024 + (4+mt)*128: kc*1024 + (5+mt)*128],
                                rhs=cur[:, kc*TC:(kc+1)*TC],
                                start=(kc == 0), stop=(kc == NKC - 1))
                        mm.inc(inst)
                    tensor.wait_ge(silud.h, ci + 1)
                    tensor.wait_ge(pevd.h, ci)
                    for pc in range(NPC):
                        inst = tensor.matmul(out=pp[:], lhsT=wx_bf[:, pc*80:(pc+1)*80],
                                             rhs=xcv[:, pc*TC:(pc+1)*TC],
                                             start=(pc == 0), stop=(pc == NPC - 1))
                    mm.inc(inst)

                @blk.vector
                def _(vector, ci=ci, first_in_b=first_in_b):
                    mm0 = 9 * ci
                    for pc in range(NPC):
                        vector.wait_ge(mm.h, mm0 + pc + 1)
                        o = pc * (TC + 3)
                        if first_in_b:
                            vector.memset(xs[:, o:o+3], 0.0)
                        vector.tensor_copy(out=xs[:, o+3:o+3+TC], in_=pm[pc][:])
                        vector.tensor_scalar(out=cacc[:, pc*TC:(pc+1)*TC], in0=xs[:, o:o+TC],
                                             scalar1=conv_w[:, pc*4:pc*4+1], scalar2=None, op0=ALU.mult)
                        for j in (1, 2, 3):
                            vector.scalar_tensor_tensor(
                                out=cacc[:, pc*TC:(pc+1)*TC], in0=xs[:, o+j:o+j+TC],
                                scalar=conv_w[:, pc*4+j:pc*4+j+1], in1=cacc[:, pc*TC:(pc+1)*TC],
                                op0=ALU.mult, op1=ALU.add)
                        inst = vector.tensor_copy(out=xs[:, o:o+3], in_=xs[:, o+TC:o+TC+3])
                    convd.inc(inst)
                    vector.wait_ge(mm.h, mm0 + 9)
                    pevd.inc(vector.tensor_copy(out=p_sb[:], in_=pp[:]))

                @blk.scalar
                def _(scalar, ci=ci):
                    scalar.wait_ge(convd.h, ci + 1)
                    scalar.wait_ge(st.h, 16 * 11 * ci)
                    for pc in range(NPC):
                        inst = scalar.activation(out=xcv[:, pc*TC:(pc+1)*TC], in_=cacc[:, pc*TC:(pc+1)*TC],
                                                 func=AF.Silu, bias=conv_b[:, pc:pc+1])
                    silud.inc(inst)
                    scalar.wait_ge(mm.h, 9 * ci + 8)
                    for pc in range(NPC):
                        inst = scalar.activation(out=szt[:, pc*TC:(pc+1)*TC], in_=pm[pc][:], func=AF.Silu)
                    szd.inc(inst)

                @blk.gpsimd
                def _(gp, ci=ci, t0=t0):
                    gp.wait_ge(silud.h, ci + 1)
                    for pc in range(NPC):
                        gp.dma_start(out=io["x_conv"][pc, :, t0:t0+TC], in_=xcv[:, pc*TC:(pc+1)*TC]).then_inc(st.h, 16)
                    gp.wait_ge(szd.h, ci + 1)
                    for pc in range(NPC):
                        gp.dma_start(out=io["sz"][pc, :, t0:t0+TC], in_=szt[:, pc*TC:(pc+1)*TC]).then_inc(st.h, 16)
                    gp.wait_ge(pevd.h, ci + 1)
                    gp.dma_start(out=io["p_part"][0:16, t0:t0+TC], in_=p_sb[0:16, :]).then_inc(st.h, 16)
                    gp.dma_start(out=io["p_part"][16:32, t0:t0+TC], in_=p_sb[32:48, :]).then_inc(st.h, 16)
                    gp.dma_start(out=io["p_part"][32:33, t0:t0+TC], in_=p_sb[64:65, :]).then_inc(st.h, 16)
                    st.n += 11 * 16

        with nc.Block() as blk:
            @blk.gpsimd
            def _(gp):
                gp.wait_ge(st.h, st.n)

def _f32(a):
    return np.ascontiguousarray(a, dtype=np.float32)


def host_xT(x):
    return _f32(x.transpose(2, 0, 1).reshape(DM, T))


def kernel(**inputs):
    raise NotImplementedError("assembled later")


def build_a3(nc, io, nch_limit=None, dbg=None):
    L = 512
    nch = nch_limit or (T // L)
    NS = DST
    with ExitStack() as ctx:
        sb = lambda *a: ctx.enter_context(nc.sbuf_tensor(*a))
        ps = lambda *a: ctx.enter_context(nc.psum_tensor(*a))
        A_neg = sb("A_neg", [128, NPC * NS], F32)
        Dv = sb("DvS", [128, NPC], F32)
        mdt = sb("mdtS", [1, 1], F32)
        ones_bf = sb("ones_bfS", [1, 128], BF16)
        ones_col = sb("ones_colS", [128, 1], BF16)
        p_stage = sb("p_stageS", [80, 8 * L], F32)
        sel_i = sb("sel_iS", [16, 16 * 128], I32)
        sel = sb("selS", [16, 16 * 128], BF16)
        p8 = sb("p8S", [80, L], F32)
        sp1 = sb("sp1S", [1, L], F32)
        sp2 = sb("sp2S", [1, L], F32)
        dt_row = sb("dt_rowS", [1, L], F32)
        dt_bf = sb("dt_bfS", [1, L], BF16)
        dt16_sb = sb("dt16_sbS", [16, L], F32)
        w16bf = sb("w16bfS", [16, L], BF16)
        c16bf = sb("c16bfS", [16, L], BF16)
        B128 = [sb(f"B128_{q}", [128, NS * L], BF16) for q in range(2)]
        C128 = [sb(f"C128_{q}", [128, NS * L], BF16) for q in range(2)]
        dt128 = [sb(f"dt128_{q}", [128, L], F32) for q in range(2)]
        xcv = [sb(f"xcv3{q}", [128, NPC * L], BF16) for q in range(2)]
        szt = [sb(f"szt3{q}", [128, NPC * L], BF16) for q in range(2)]
        xd = sb("xdS", [128, NPC * L], F32)
        ab = sb("abS", [128, 4 * L], F32)
        us = sb("usS", [128, L], F32)
        hcs = sb("hcsS", [128, L], F32)
        h_pc = sb("h_pcS", [128, NS * L], F32)
        hcar = [sb(f"hcar{q}", [128, NPC * NS], F32) for q in range(2)]
        y_acc = sb("y_accS", [128, NPC * L], F32)
        ypbf = [sb(f"ypbf{q}", [128, NPC * L], BF16) for q in range(2)]
        pmb = [ps(f"pmbS{i}", [128, 512], F32) for i in range(4)]
        pm16 = ps("pm16S", [16, 512], F32)

        ld = Sem(ctx, nc, "s_ld")
        pltree = Sem(ctx, nc, "s_pltree")
        plc = Sem(ctx, nc, "s_plc")
        plw = Sem(ctx, nc, "s_plw")
        pldt = Sem(ctx, nc, "s_pldt")
        s4 = Sem(ctx, nc, "s_s4")
        sevdt = Sem(ctx, nc, "s_sevdt")
        pm16d = Sem(ctx, nc, "s_pm16d")
        pmm = Sem(ctx, nc, "s_pmm")
        sev = Sem(ctx, nc, "s_sev")
        sab = Sem(ctx, nc, "s_sab")
        dvs = Sem(ctx, nc, "s_dvs")
        dvy = Sem(ctx, nc, "s_dvy")
        st = Sem(ctx, nc, "s_st")

        with nc.Block() as blk:
            @blk.sync
            def _(sync):
                for pc in range(NPC):
                    sync.dma_start(out=A_neg[:, pc*NS:(pc+1)*NS],
                                   in_=io["A_log"][pc*128:(pc+1)*128, :]).then_inc(ld.h, 16)
                    sync.dma_start(out=Dv[:, pc:pc+1],
                                   in_=io["Dvec"][pc*128:(pc+1)*128, :]).then_inc(ld.h, 16)
                sync.dma_start(out=p_stage[0:1, 0:DI], in_=io["dt_bias"][:]).then_inc(ld.h, 16)
                ld.n += (2 * NPC + 1) * 16

            @blk.scalar
            def _(scalar):
                scalar.wait_ge(ld.h, ld.n)
                scalar.activation(out=A_neg[:], in_=A_neg[:], func=AF.Exp)

            @blk.vector
            def _(vector):
                vector.wait_ge(ld.h, ld.n)
                vector.memset(ones_bf[:], 1.0)
                vector.memset(ones_col[:], 1.0)
                vector.tensor_scalar(out=p_stage[32:33, 0:DI], in0=p_stage[0:1, 0:DI],
                                     scalar1=1.0 / 4096.0, scalar2=0.0, op0=ALU.mult, op1=ALU.add,
                                     accum_out=mdt[:])

            @blk.gpsimd
            def _(gp):
                gp.iota(sel_i[:], pattern=[[1, 16], [0, 128]], base=0, channel_multiplier=-1)

        with nc.Block() as blk:
            @blk.vector
            def _(vector):
                vector.tensor_scalar(out=A_neg[:], in0=A_neg[:], scalar1=-1.0, scalar2=None, op0=ALU.mult)
                vector.tensor_single_scalar(out=sel[:], in_=sel_i[:], scalar=0, op=ALU.is_equal)



        def meta_loads(sync, m):
            q = m % 2
            t0 = m * L
            sync.dma_start(out=p_stage[0:16, :],
                           in_=io["p_all"][:, 0:16, t0:t0+L].transpose([1, 0, 2])).then_inc(ld.h, 16)
            sync.dma_start(out=p_stage[32:48, :],
                           in_=io["p_all"][:, 16:32, t0:t0+L].transpose([1, 0, 2])).then_inc(ld.h, 16)
            sync.dma_start(out=p_stage[64:65, :],
                           in_=io["p_all"][:, 32:33, t0:t0+L].transpose([1, 0, 2])).then_inc(ld.h, 16)
            for pc in range(NPC):
                sync.dma_start(out=xcv[q][:, pc*L:(pc+1)*L], in_=io["x_conv"][pc, :, t0:t0+L]).then_inc(ld.h, 16)
                sync.dma_start(out=szt[q][:, pc*L:(pc+1)*L], in_=io["sz"][pc, :, t0:t0+L]).then_inc(ld.h, 16)
            ld.n += 11 * 16

        def meta_pool(gp, m, ld_base):
            q = m % 2
            gp.wait_ge(ld.h, ld_base + 3 * 16)
            gp.tensor_tensor(out=p_stage[:, 0:4*L], in0=p_stage[:, 0:4*L],
                             in1=p_stage[:, 4*L:8*L], op=ALU.add)
            gp.tensor_tensor(out=p_stage[:, 0:2*L], in0=p_stage[:, 0:2*L],
                             in1=p_stage[:, 2*L:4*L], op=ALU.add)
            pltree.inc(gp.tensor_tensor(out=p8[:], in0=p_stage[:, 0:L],
                                        in1=p_stage[:, L:2*L], op=ALU.add))
            plc.inc(gp.tensor_copy(out=c16bf[:], in_=p8[32:48, :]))
            gp.wait_ge(s4.h, m + 1)
            gp.tensor_tensor(out=dt_row[:], in0=sp1[:], in1=sp2[:], op=ALU.add)
            pldt.inc(gp.tensor_copy(out=dt_bf[:], in_=dt_row[:]))
            gp.wait_ge(sevdt.h, m + 1)
            plw.inc(gp.tensor_tensor(out=w16bf[:], in0=p8[0:16, :], in1=dt16_sb[:], op=ALU.mult))

        def meta_scalar_dt(scalar, m):
            scalar.wait_ge(pltree.h, m + 1)
            scalar.activation(out=sp1[:], in_=p8[64:65, :], func=AF.Abs, bias=mdt[0:1, 0:1])
            scalar.activation(out=sp1[:], in_=sp1[:], func=AF.Exp, scale=-1.0)
            scalar.activation(out=sp1[:], in_=sp1[:], func=AF.Ln, bias=1.0)
            s4.inc(scalar.activation(out=sp2[:], in_=p8[64:65, :], func=AF.Relu, bias=mdt[0:1, 0:1]))

        def meta_pe(tensor, m):
            q = m % 2
            tensor.wait_ge(pldt.h, m + 1)
            pm16d.inc(tensor.matmul(out=pm16[:, 0:L], lhsT=ones_bf[:, 0:16], rhs=dt_bf[:],
                                    start=True, stop=True))
            r = 0
            if 33 * m + r >= 4:
                tensor.wait_ge(sev.h, 33 * m + r - 3)
            pmm.inc(tensor.matmul(out=pmb[r % 4][:], lhsT=ones_bf[:, 0:128], rhs=dt_bf[:],
                                  start=True, stop=True))
            r += 1
            for rowsrc, gate in ((w16bf, plw), (c16bf, plc)):
                tensor.wait_ge(gate.h, m + 1)
                for i in range(NS):
                    if 33 * m + r >= 4:
                        tensor.wait_ge(sev.h, 33 * m + r - 3)
                    pmm.inc(tensor.matmul(out=pmb[r % 4][:], lhsT=sel[:, i*128:(i+1)*128],
                                          rhs=rowsrc[:], start=True, stop=True))
                    r += 1

        _EVAC_RANGES = {1: range(0, 11), 2: range(11, 22), 3: range(22, 32)}

        def meta_scalar_evac(scalar, m, phase):
            q = m % 2
            if phase == 0:
                scalar.wait_ge(pm16d.h, m + 1)
                sevdt.inc(scalar.copy(out=dt16_sb[:], in_=pm16[:, 0:L]))
                scalar.wait_ge(pmm.h, 33 * m + 1)
                sev.inc(scalar.copy(out=dt128[q][:], in_=pmb[0][:]))
            else:
                for j in _EVAC_RANGES[phase]:
                    r = 1 + j
                    dst = B128[q][:, j*L:(j+1)*L] if j < NS else C128[q][:, (j-NS)*L:(j-NS+1)*L]
                    scalar.wait_ge(pmm.h, 33 * m + r + 1)
                    sev.inc(scalar.copy(out=dst, in_=pmb[r % 4][:]))

        ld_base0 = ld.n
        with nc.Block() as blk:
            @blk.sync
            def _(sync):
                meta_loads(sync, 0)

            @blk.gpsimd
            def _(gp):
                meta_pool(gp, 0, ld_base0)

            @blk.scalar
            def _(scalar):
                meta_scalar_dt(scalar, 0)
                for phase in range(4):
                    meta_scalar_evac(scalar, 0, phase)


            @blk.tensor
            def _(tensor):
                meta_pe(tensor, 0)

        for ci in range(nch):
            q = ci % 2
            first_in_b = (ci * L) % S == 0
            m = ci + 1
            ld_base = ld.n
            with nc.Block() as blk:
                @blk.sync
                def _(sync, ci=ci, m=m, ld_base=ld_base):
                    if m < nch:
                        meta_loads(sync, m)
                    t0 = ci * L
                    sync.wait_ge(dvy.h, 4 * (ci + 1))
                    for pc in range(NPC):
                        sync.dma_start(out=io["y_pre"][pc, :, t0:t0+L],
                                       in_=ypbf[ci % 2][:, pc*L:(pc+1)*L]).then_inc(st.h, 16)
                    st.n += 64

                @blk.gpsimd
                def _(gp, m=m, ld_base=ld_base):
                    if m < nch:
                        meta_pool(gp, m, ld_base)

                @blk.scalar
                def _(scalar, ci=ci, q=q, m=m):
                    for pc in range(NPC):
                        for s in range(NS):
                            j = pc * NS + s
                            g = 64 * ci + j
                            if g >= 4:
                                scalar.wait_ge(dvs.h, g - 3)
                            sab.inc(scalar.activation(out=ab[:, (j % 4)*L:(j % 4 + 1)*L],
                                                      in_=dt128[q][:], func=AF.Exp,
                                                      scale=A_neg[:, j:j+1]))
                        if m < nch:
                            if pc == 0:
                                meta_scalar_dt(scalar, m)
                                meta_scalar_evac(scalar, m, 0)
                            else:
                                meta_scalar_evac(scalar, m, pc)

                @blk.tensor
                def _(tensor, m=m):
                    if m < nch:
                        meta_pe(tensor, m)

                @blk.vector
                def _(vector, ci=ci, q=q, first_in_b=first_in_b):
                    if ci >= 2:
                        vector.wait_ge(st.h, 64 * (ci - 1))
                    for pc in range(NPC):
                        for s in range(NS):
                            j = pc * NS + s
                            g = 64 * ci + j
                            vector.wait_ge(sab.h, g + 1)
                            vector.tensor_tensor(out=us[:], in0=xcv[q][:, pc*L:(pc+1)*L],
                                                 in1=B128[q][:, s*L:(s+1)*L], op=ALU.mult)
                            init = 0.0 if first_in_b else hcar[1 - q][:, pc*NS+s:pc*NS+s+1]
                            dvs.inc(vector.tensor_tensor_scan(
                                out=h_pc[:, s*L:(s+1)*L], data0=ab[:, (j % 4)*L:(j % 4 + 1)*L],
                                data1=us[:], initial=init, op0=ALU.mult, op1=ALU.add))
                            if s == 0:
                                vector.tensor_tensor(out=y_acc[:, pc*L:(pc+1)*L],
                                                     in0=h_pc[:, 0:L],
                                                     in1=C128[q][:, 0:L], op=ALU.mult)
                            else:
                                vector.tensor_tensor(out=hcs[:], in0=h_pc[:, s*L:(s+1)*L],
                                                     in1=C128[q][:, s*L:(s+1)*L], op=ALU.mult)
                                vector.tensor_tensor(out=y_acc[:, pc*L:(pc+1)*L],
                                                     in0=y_acc[:, pc*L:(pc+1)*L],
                                                     in1=hcs[:], op=ALU.add)
                        hv = h_pc[:].rearrange("p (s l) -> p s l", l=L)
                        vector.tensor_copy(out=hcar[q][:, pc*NS:(pc+1)*NS],
                                           in_=hv[:, :, L-1:L])
                        vector.scalar_tensor_tensor(out=xd[:, pc*L:(pc+1)*L],
                                                    in0=xcv[q][:, pc*L:(pc+1)*L],
                                                    scalar=Dv[:, pc:pc+1],
                                                    in1=y_acc[:, pc*L:(pc+1)*L],
                                                    op0=ALU.mult, op1=ALU.add)
                        dvy.inc(vector.tensor_tensor(out=ypbf[ci % 2][:, pc*L:(pc+1)*L],
                                                     in0=xd[:, pc*L:(pc+1)*L],
                                                     in1=szt[q][:, pc*L:(pc+1)*L], op=ALU.mult))

        with nc.Block() as blk:
            @blk.sync
            def _(sync):
                sync.wait_ge(st.h, st.n)
                if dbg is not None:
                    dsem = 0
                    for name, tile in [("dt128", dt128[(nch-1) % 2][:]), ("B128", B128[(nch-1) % 2][:]),
                                       ("C128", C128[(nch-1) % 2][:]), ("ab", ab[:]),
                                       ("hcar", hcar[(nch-1) % 2][:]), ("y_acc", y_acc[:]),
                                       ("w16bf", w16bf[:]), ("c16bf", c16bf[:]), ("p8", p8[:]),
                                       ("dt_row", dt_row[:]), ("mdt", mdt[:]), ("A_neg", A_neg[:]),
                                       ("p_stage", p_stage[:]),
                                       ("sel", sel[:]), ("h_pc", h_pc[:])]:
                        if name in dbg:
                            sync.dma_start(out=dbg[name], in_=tile).then_inc(st.h, 16)
                            dsem += 16
                    sync.wait_ge(st.h, st.n + dsem)


LN_EPS = 1e-5


class LNEmit:
    def __init__(self, tiles, TT, mm, sev):
        self.t = tiles
        self.TT = TT
        self.mm = mm
        self.sev = sev
        self.mm0 = mm.n
        self.sev0 = sev.n
        mm.n += 3
        sev.n += 50

    def vector(self, vector, wait_pre=None):
        t = self.t; TT = self.TT
        n = self.sev0
        for kc in range(16):
            sl = slice(kc * TT, (kc + 1) * TT)
            vector.tensor_copy(out=t["rbf"][:, sl], in_=t["r"][:, sl]).then_inc(self.sev.h, 1)
        vector.wait_ge(self.mm.h, self.mm0 + 1)
        vector.tensor_scalar(out=t["mrow"][:, 0:TT], in0=t["st_ps"][0][0:1, 0:TT],
                             scalar1=1.0 / 2048.0, scalar2=None, op0=ALU.mult)
        vector.tensor_scalar(out=t["vtmp"][:, 0:TT], in0=t["st_ps"][1][0:1, 0:TT],
                             scalar1=1.0 / 2048.0, scalar2=None, op0=ALU.mult)
        vector.tensor_tensor(out=t["rstd"][:, 0:TT], in0=t["mrow"][:, 0:TT],
                             in1=t["mrow"][:, 0:TT], op=ALU.mult)
        vector.tensor_tensor(out=t["vtmp"][:, 0:TT], in0=t["vtmp"][:, 0:TT],
                             in1=t["rstd"][:, 0:TT], op=ALU.subtract)
        vector.tensor_scalar(out=t["vtmp"][:, 0:TT], in0=t["vtmp"][:, 0:TT],
                             scalar1=1.0, scalar2=LN_EPS, op0=ALU.mult, op1=ALU.add)
        vector.reciprocal(out=t["rstd"][:, 0:TT], in_=t["vtmp"][:, 0:TT]).then_inc(self.sev.h, 1)
        vector.wait_ge(self.mm.h, self.mm0 + 3)
        for kc in range(16):
            sl = slice(kc * TT, (kc + 1) * TT)
            vector.tensor_tensor(out=t["r"][:, sl], in0=t["r"][:, sl],
                                 in1=t["bc_ps"][0][:, 0:TT], op=ALU.subtract)
            vector.tensor_tensor(out=t["out"][:, sl], in0=t["r"][:, sl],
                                 in1=t["bc_ps"][1][:, 0:TT], op=ALU.mult).then_inc(self.sev.h, 1)

    def scalar(self, scalar):
        t = self.t; TT = self.TT
        scalar.wait_ge(self.sev.h, self.sev0 + 16)
        for kc in range(16):
            sl = slice(kc * TT, (kc + 1) * TT)
            scalar.activation(out=t["sq"][:, sl], in_=t["rbf"][:, sl],
                              func=AF.Square).then_inc(self.sev.h, 1)
        scalar.wait_ge(self.sev.h, self.sev0 + 33)
        scalar.activation(out=t["vtmp"][:, 0:TT], in_=t["rstd"][:, 0:TT],
                          func=AF.Sqrt).then_inc(self.sev.h, 1)

    def tensor(self, tensor):
        t = self.t; TT = self.TT
        tensor.wait_ge(self.sev.h, self.sev0 + 32)
        for kc in range(16):
            sl = slice(kc * TT, (kc + 1) * TT)
            tensor.matmul(out=t["st_ps"][0][0:1, 0:TT], lhsT=t["ones_col"][:],
                          rhs=t["rbf"][:, sl], start=(kc == 0), stop=(kc == 15))
        for kc in range(16):
            sl = slice(kc * TT, (kc + 1) * TT)
            inst = tensor.matmul(out=t["st_ps"][1][0:1, 0:TT], lhsT=t["ones_col"][:],
                                 rhs=t["sq"][:, sl], start=(kc == 0), stop=(kc == 15))
        inst.then_inc(self.mm.h, 1)
        tensor.wait_ge(self.sev.h, self.sev0 + 34)
        tensor.matmul(out=t["bc_ps"][0][:, 0:TT], lhsT=t["ones_f32"][:, 0:128],
                      rhs=t["mrow"][:, 0:TT], start=True, stop=True).then_inc(self.mm.h, 1)
        tensor.matmul(out=t["bc_ps"][1][:, 0:TT], lhsT=t["ones_f32"][:, 0:128],
                      rhs=t["vtmp"][:, 0:TT], start=True, stop=True).then_inc(self.mm.h, 1)


def build_b(nc, io):
    TT = 512
    ntt = TLOC // TT
    yploc = nc.dram_tensor("b_yploc", [32, 128, TLOC], BF16, kind="Internal").ap()

    with ExitStack() as ctx:
        sb = lambda *a: ctx.enter_context(nc.sbuf_tensor(*a))
        ps = lambda *a: ctx.enter_context(nc.psum_tensor(*a))
        ones_col = sb("b_ones_col", [128, 1], BF16)
        ones_f32 = sb("b_ones_f32", [1, 128], F32)
        ypre = [sb(f"b_ypre{i}", [128, 32 * TT], BF16) for i in range(2)]
        xres = sb("b_xres", [128, 16 * TT], F32)
        lhsb = [sb(f"b_lhs{i}", [128, 32 * 128], BF16) for i in range(2)]
        r = sb("b_r", [128, 16 * TT], F32)
        rbf = sb("b_rbf", [128, 16 * TT], BF16)
        sq = sb("b_sq", [128, 16 * TT], BF16)
        mrow = sb("b_mrow", [1, TT], F32)
        vtmp = sb("b_vtmp", [1, TT], F32)
        rstd = sb("b_rstd", [1, TT], F32)
        pmb = [ps(f"b_pm{i}", [128, TT], F32) for i in range(4)]
        st_ps = [ps(f"b_st{i}", [1, TT], F32) for i in range(2)]
        bc_ps = [ps(f"b_bc{i}", [128, TT], F32) for i in range(2)]

        ld = Sem(ctx, nc, "b_ld")
        mm = Sem(ctx, nc, "b_mm")
        sev = Sem(ctx, nc, "b_sev")
        st = Sem(ctx, nc, "b_st")

        with nc.Block() as blk:
            @blk.vector
            def _(vector):
                vector.memset(ones_col[:], 1.0)
                vector.memset(ones_f32[:], 1.0)

            @blk.sync
            def _(sync):
                pid = sync.partition_id()
                tb = dynv(sync, pid, 0, 7)
                ypv = io["y_pre_all"][:, :, :].rearrange("k p (tb u) -> k tb p u", u=TLOC)
                sync.dma_start(out=yploc[:, :, :],
                               in_=ypv[:, bass.ts(tb, 1), :, :].rearrange(
                                   "k one p u -> k (one p) u")).then_inc(ld.h, 16)
                sync.wait_ge(ld.h, 16)
                for kc in range(32):
                    sync.dma_start(out=ypre[0][:, kc*TT:(kc+1)*TT],
                                   in_=yploc[kc, :, 0:TT]).then_inc(ld.h, 16)
                ld.n += 33 * 16

        for tt in range(ntt):
            cur = ypre[tt % 2]
            ld0 = ld.n
            mm0 = mm.n
            sev0 = sev.n
            with nc.Block() as blk:
                @blk.sync
                def _(sync, tt=tt, mm0=mm0):
                    if tt + 1 < ntt:
                        for kc in range(32):
                            sync.dma_start(out=ypre[(tt+1) % 2][:, kc*TT:(kc+1)*TT],
                                           in_=yploc[kc, :, (tt+1)*TT:(tt+2)*TT]).then_inc(ld.h, 16)
                        ld.n += 32 * 16
                    for kc in range(16):
                        sync.dma_start(out=xres[:, kc*TT:(kc+1)*TT],
                                       in_=io["xT_c"][kc*128:(kc+1)*128, tt*TT:(tt+1)*TT]).then_inc(ld.h, 16)
                    ld.n += 16 * 16
                    for mt in range(16):
                        if mt >= 2:
                            sync.wait_ge(mm.h, mm0 + mt - 1)
                        w3 = io["outw_T"][:, mt*128:(mt+1)*128].rearrange("(kc p) m -> p kc m", p=128)
                        sync.dma_start(out=lhsb[mt % 2][:].rearrange("p (kc m) -> p kc m", m=128),
                                       in_=w3).then_inc(ld.h, 16)
                        ld.n += 16

                @blk.tensor
                def _(tensor, tt=tt, cur=cur, ld0=ld0, mm0=mm0, sev0=sev0):
                    for mt in range(16):
                        tensor.wait_ge(ld.h, ld0 + (32 * 16 if tt + 1 < ntt else 0) + 16 * 16 + 16 * (mt + 1))
                        if mt >= 4:
                            tensor.wait_ge(sev.h, sev0 + mt - 3)
                        for kc in range(32):
                            inst = tensor.matmul(out=pmb[mt % 4][:],
                                                 lhsT=lhsb[mt % 2][:, kc*128:(kc+1)*128],
                                                 rhs=cur[:, kc*TT:(kc+1)*TT],
                                                 start=(kc == 0), stop=(kc == 31))
                        mm.inc(inst)

                @blk.vector
                def _(vector, mm0=mm0, sev0=sev0):
                    for mt in range(16):
                        vector.wait_ge(mm.h, mm0 + mt + 1)
                        sev.inc(vector.tensor_tensor(out=r[:, mt*TT:(mt+1)*TT], in0=pmb[mt % 4][:],
                                                     in1=xres[:, mt*TT:(mt+1)*TT], op=ALU.add))

            lnt = {"r": r, "rbf": rbf, "sq": sq, "out": rbf, "ones_col": ones_col,
                   "ones_f32": ones_f32, "st_ps": st_ps, "bc_ps": bc_ps,
                   "mrow": mrow, "vtmp": vtmp, "rstd": rstd}
            with nc.Block() as blk:
                if tt >= 1:
                    pass
                ln = LNEmit(lnt, TT, mm, sev)

                @blk.tensor
                def _(tensor, ln=ln):
                    ln.tensor(tensor)

                @blk.scalar
                def _(scalar, ln=ln):
                    ln.scalar(scalar)

                @blk.vector
                def _(vector, ln=ln, tt=tt):
                    if tt >= 1:
                        vector.wait_ge(st.h, 256 * tt)
                    ln.vector(vector)

                @blk.gpsimd
                def _(gp, tt=tt, ln=ln):
                    gp.wait_ge(sev.h, ln.sev0 + 50)
                    for kc in range(16):
                        gp.dma_start(out=io["s1_d"][kc, :, tt*TT:(tt+1)*TT],
                                     in_=rbf[:, kc*TT:(kc+1)*TT]).then_inc(st.h, 16)
                    st.n += 16 * 16

        with nc.Block() as blk:
            @blk.gpsimd
            def _(gp):
                gp.wait_ge(st.h, st.n)

    with ExitStack() as ctx:
        sb = lambda *a: ctx.enter_context(nc.sbuf_tensor(*a))
        ps = lambda *a: ctx.enter_context(nc.psum_tensor(*a))
        ones_col = sb("b2_ones_col", [128, 1], BF16)
        ones_f32 = sb("b2_ones_f32", [1, 128], F32)
        s1t = [sb(f"b2_s1{i}", [128, 16 * TT], BF16) for i in range(2)]
        wgt = [sb(f"b2_wg{i}", [128, 16 * 128], BF16) for i in range(2)]
        wut = [sb(f"b2_wu{i}", [128, 16 * 128], BF16) for i in range(2)]
        wdt = [sb(f"b2_wd{i}", [128, 16 * 128], BF16) for i in range(2)]
        gsil = sb("b2_gsil", [128, 2 * TT], BF16)
        inter = sb("b2_inter", [128, 16 * TT], BF16)
        acc = sb("b2_acc", [128, 16 * TT], F32)
        rbf = sb("b2_rbf", [128, 16 * TT], BF16)
        sq = sb("b2_sq", [128, 16 * TT], BF16)
        mrow = sb("b2_mrow", [1, TT], F32)
        vtmp = sb("b2_vtmp", [1, TT], F32)
        rstd = sb("b2_rstd", [1, TT], F32)
        pmb = [ps(f"b2_pm{i}", [128, TT], F32) for i in range(4)]
        st_ps = [ps(f"b2_st{i}", [1, TT], F32) for i in range(2)]
        bc_ps = [ps(f"b2_bc{i}", [128, TT], F32) for i in range(2)]

        ld = Sem(ctx, nc, "b2_ld")
        lw = Sem(ctx, nc, "b2_lw")
        mm = Sem(ctx, nc, "b2_mm")
        ssil = Sem(ctx, nc, "b2_ssil")
        sev = Sem(ctx, nc, "b2_sev")
        st = Sem(ctx, nc, "b2_st")

        def wslice3(wap, colbase, kbase):
            return wap[kbase:kbase + 2048, colbase:colbase + 128].rearrange(
                "(kc p) m -> p kc m", p=128)

        with nc.Block() as blk:
            @blk.vector
            def _(vector):
                vector.memset(ones_col[:], 1.0)
                vector.memset(ones_f32[:], 1.0)

            @blk.sync
            def _(sync):
                for kc in range(16):
                    sync.dma_start(out=s1t[0][:, kc*TT:(kc+1)*TT],
                                   in_=io["s1_d"][kc, :, 0:TT]).then_inc(ld.h, 16)
                ld.n += 16 * 16

        for tt in range(ntt):
            cur = s1t[tt % 2]
            for hb in range(4):
                lw0 = lw.n
                mm0 = mm.n
                sev0 = sev.n
                ssil0 = ssil.n
                with nc.Block() as blk:
                    @blk.sync
                    def _(sync, tt=tt, hb=hb, mm0=mm0):
                        for hm in range(16):
                            if hm >= 2:
                                sync.wait_ge(mm.h, mm0 + hm - 1)
                            sync.dma_start(out=wgt[hm % 2][:].rearrange("p (kc m) -> p kc m", m=128),
                                           in_=wslice3(io["wg_T"], hb*2048 + hm*128, 0)).then_inc(lw.h, 16)
                            sync.dma_start(out=wut[hm % 2][:].rearrange("p (kc m) -> p kc m", m=128),
                                           in_=wslice3(io["wu_T"], hb*2048 + hm*128, 0)).then_inc(lw.h, 16)
                            lw.n += 32
                        for dm in range(16):
                            if dm >= 2:
                                sync.wait_ge(mm.h, mm0 + 16 + dm - 1)
                            sync.dma_start(out=wdt[dm % 2][:].rearrange("p (kc m) -> p kc m", m=128),
                                           in_=wslice3(io["wd_T"], dm*128, hb*2048)).then_inc(lw.h, 16)
                            lw.n += 16

                    @blk.tensor
                    def _(tensor, tt=tt, hb=hb, cur=cur, lw0=lw0, mm0=mm0, sev0=sev0, ssil0=ssil0):
                        for hm in range(16):
                            tensor.wait_ge(lw.h, lw0 + 32 * (hm + 1))
                            if hm >= 2:
                                tensor.wait_ge(sev.h, sev0 + hm - 1)
                            ga = pmb[2 * (hm % 2)]
                            ua = pmb[2 * (hm % 2) + 1]
                            for kc in range(16):
                                tensor.matmul(out=ga[:], lhsT=wgt[hm % 2][:, kc*128:(kc+1)*128],
                                              rhs=cur[:, kc*TT:(kc+1)*TT],
                                              start=(kc == 0), stop=(kc == 15))
                            for kc in range(16):
                                inst = tensor.matmul(out=ua[:], lhsT=wut[hm % 2][:, kc*128:(kc+1)*128],
                                                     rhs=cur[:, kc*TT:(kc+1)*TT],
                                                     start=(kc == 0), stop=(kc == 15))
                            inst.then_inc(mm.h, 1)
                        mm.n += 16
                        for dm in range(16):
                            tensor.wait_ge(lw.h, lw0 + 32 * 16 + 16 * (dm + 1))
                            tensor.wait_ge(sev.h, sev0 + 16)
                            if dm >= 2:
                                tensor.wait_ge(sev.h, sev0 + 16 + dm - 1)
                            pa = pmb[dm % 2]
                            for kc in range(16):
                                inst = tensor.matmul(out=pa[:], lhsT=wdt[dm % 2][:, kc*128:(kc+1)*128],
                                                     rhs=inter[:, kc*TT:(kc+1)*TT],
                                                     start=(kc == 0), stop=(kc == 15))
                            inst.then_inc(mm.h, 1)
                        mm.n += 16

                    @blk.scalar
                    def _(scalar, mm0=mm0, ssil0=ssil0, sev0=sev0):
                        for hm in range(16):
                            scalar.wait_ge(mm.h, mm0 + hm + 1)
                            if hm >= 2:
                                scalar.wait_ge(sev.h, sev0 + hm - 1)
                            ga = pmb[2 * (hm % 2)]
                            scalar.activation(out=gsil[:, (hm % 2)*TT:(hm % 2 + 1)*TT],
                                              in_=ga[:], func=AF.Silu).then_inc(ssil.h, 1)
                        ssil.n += 16

                    @blk.vector
                    def _(vector, hb=hb, mm0=mm0, ssil0=ssil0, sev0=sev0):
                        for hm in range(16):
                            vector.wait_ge(ssil.h, ssil0 + hm + 1)
                            ua = pmb[2 * (hm % 2) + 1]
                            vector.tensor_tensor(out=inter[:, hm*TT:(hm+1)*TT],
                                                 in0=gsil[:, (hm % 2)*TT:(hm % 2 + 1)*TT],
                                                 in1=ua[:], op=ALU.mult).then_inc(sev.h, 1)
                        sev.n += 16
                        for dm in range(16):
                            vector.wait_ge(mm.h, mm0 + 16 + dm + 1)
                            pa = pmb[dm % 2]
                            if hb == 0:
                                vector.tensor_copy(out=acc[:, dm*TT:(dm+1)*TT], in_=pa[:]).then_inc(sev.h, 1)
                            else:
                                vector.tensor_tensor(out=acc[:, dm*TT:(dm+1)*TT],
                                                     in0=acc[:, dm*TT:(dm+1)*TT],
                                                     in1=pa[:], op=ALU.add).then_inc(sev.h, 1)
                        sev.n += 16

            lnt = {"r": acc, "rbf": rbf, "sq": sq, "out": rbf, "ones_col": ones_col,
                   "ones_f32": ones_f32, "st_ps": st_ps, "bc_ps": bc_ps,
                   "mrow": mrow, "vtmp": vtmp, "rstd": rstd}
            with nc.Block() as blk:
                res0 = sev.n
                sev.n += 16
                ln = LNEmit(lnt, TT, mm, sev)

                @blk.sync
                def _(sync, tt=tt):
                    if tt + 1 < ntt:
                        for kc in range(16):
                            sync.dma_start(out=s1t[(tt+1) % 2][:, kc*TT:(kc+1)*TT],
                                           in_=io["s1_d"][kc, :, (tt+1)*TT:(tt+2)*TT]).then_inc(ld.h, 16)
                        ld.n += 16 * 16

                @blk.vector
                def _(vector, tt=tt, cur=cur, ln=ln, res0=res0):
                    if tt >= 1:
                        vector.wait_ge(st.h, 256 * tt)
                    for kc in range(16):
                        vector.tensor_tensor(out=acc[:, kc*TT:(kc+1)*TT],
                                             in0=acc[:, kc*TT:(kc+1)*TT],
                                             in1=cur[:, kc*TT:(kc+1)*TT], op=ALU.add).then_inc(sev.h, 1)
                    ln.vector(vector)

                @blk.scalar
                def _(scalar, ln=ln):
                    ln.scalar(scalar)

                @blk.tensor
                def _(tensor, ln=ln):
                    ln.tensor(tensor)

                @blk.gpsimd
                def _(gp, tt=tt, ln=ln):
                    gp.wait_ge(sev.h, ln.sev0 + 50)
                    for kc in range(16):
                        gp.dma_start(out=io["s2"][kc, :, tt*TT:(tt+1)*TT],
                                     in_=rbf[:, kc*TT:(kc+1)*TT]).then_inc(st.h, 16)
                    st.n += 16 * 16

        with nc.Block() as blk:
            @blk.gpsimd
            def _(gp):
                gp.wait_ge(st.h, st.n)

def mark(sem, inst, k=1):
    sem.n += k
    inst.then_inc(sem.h, k)
    return (sem, sem.n)


def wt(eng, ev):
    eng.wait_ge(ev[0].h, ev[1])


def build_c2(nc, io, nch_limit=None, nheads=4, dbg=None):
    L = 128
    nch = nch_limit or (S // L)
    NRND = 8
    with ExitStack() as ctx:
        sb = lambda *a: ctx.enter_context(nc.sbuf_tensor(*a))
        ps = lambda *a: ctx.enter_context(nc.psum_tensor(*a))
        qhs = sb("c2_qh", [128, 4 * S // 2], BF16)
        khs = sb("c2_kh", [128, S], BF16)
        vhs = sb("c2_vh", [128, S], BF16)
        ident_f = sb("c2_idf", [128, 128], F32)
        ident_b = sb("c2_idb", [128, 128], BF16)
        idint = sb("c2_idi", [128, 128], I32)
        mbU = sb("c2_mbU", [128, 128], F32)
        mbUD = sb("c2_mbUD", [128, 128], F32)
        ones_row = sb("c2_ones", [1, 128], F32)
        glst = sb("c2_glst", [1, 2048], F32)
        btst = sb("c2_btst", [1, 2048], F32)
        gc = sb("c2_gc", [1, 128], F32)
        gamrow = sb("c2_gamrow", [1, 128], F32)
        nbrow = sb("c2_nbrow", [1, 128], F32)
        ghrow = sb("c2_ghrow", [1, 128], F32)
        glrow = sb("c2_glrow", [1, 128], F32)
        stg = sb("c2_stg", [128, 128], F32)
        cols = sb("c2_cols", [128, 8], F32)
        ngcol = sb("c2_ngcol", [128, 1], F32)
        gamc = sb("c2_gamc", [128, 1], F32)
        gamLc = sb("c2_gamLc", [128, 1], F32)
        bgc = sb("c2_bgc", [128, 1], F32)
        mU = sb("c2_mU", [128, 128], F32)
        mUD = sb("c2_mUD", [128, 128], F32)
        d2t = sb("c2_d2t", [128, 128], F32)
        d2tud = sb("c2_d2tud", [128, 128], F32)
        q0 = [sb(f"c2_q{i}", [128, 128], F32) for i in range(2)]
        tj = sb("c2_tj", [128, 128], F32)
        m2t = sb("c2_m2t", [128, 128], BF16)
        vt = sb("c2_vt", [128, 128], BF16)
        kt = sb("c2_kt", [128, 128], BF16)
        khat = sb("c2_khat", [128, 128], BF16)
        qgt = sb("c2_qgt", [128, 128], BF16)
        zj = sb("c2_zj", [128, 256], F32)
        et = sb("c2_et", [128, 128], F32)
        dlt = sb("c2_dlt", [128, 128], BF16)
        s4bf = sb("c2_s4bf", [128, 4 * 128], BF16)
        o1 = sb("c2_o1", [128, 128], F32)
        obf = sb("c2_obf", [128, 128], BF16)
        oev = sb("c2_oev", [128, 128], BF16)
        s4 = sb("c2_s4", [128, 4 * 128], F32)
        psA = ps("c2_psA", [128, 512], F32)
        psA2 = ps("c2_psA2", [128, 256], F32)
        psT = ps("c2_psT", [128, 256], F32)
        psBF = ps("c2_psBF", [128, 384], BF16)
        psC = ps("c2_psC", [128, 256], F32)
        psC2 = ps("c2_psC2", [128, 128], F32)
        psD = ps("c2_psD", [128, 256], F32)
        psO = ps("c2_psO", [128, 128], F32)

        ld = Sem(ctx, nc, "c2_ld")
        dv = Sem(ctx, nc, "c2_dv")
        sc = Sem(ctx, nc, "c2_sc")
        pe = Sem(ctx, nc, "c2_pe")
        st = Sem(ctx, nc, "c2_st")

        glloc = nc.dram_tensor("c2_glloc", [4, 4, TLOC], F32, kind="Internal").ap()
        btloc = nc.dram_tensor("c2_btloc", [4, 4, TLOC], F32, kind="Internal").ap()
        qloc = nc.dram_tensor("c2_qloc", [4, 4, 128, TLOC], BF16, kind="Internal").ap()
        kloc = nc.dram_tensor("c2_kloc", [4, 128, TLOC], BF16, kind="Internal").ap()
        vloc = nc.dram_tensor("c2_vloc", [4, 128, TLOC], BF16, kind="Internal").ap()

        def emit_qpick(sync):
            pid = sync.partition_id()
            bqi = dynv(sync, pid // 4, 0, 1)
            qgv = io["qg"][:, :, :].rearrange("(s h) p u -> h s p u", h=16)
            for j in range(4):
                hj = dynv(sync, (pid % 4) * 4 + j, 0, 15)
                sync.dma_start(out=qloc[j, :, :, :],
                               in_=qgv[bass.ts(hj, 1), bass.ts(bqi, 4), :, :].rearrange(
                                   "one s p u -> s (one p) u")).then_inc(ld.h, 16)
                ld.n += 16

        with nc.Block() as blk:
            @blk.sync
            def _(sync):
                pid = sync.partition_id()
                bqi = dynv(sync, pid // 4, 0, 1)
                gi = dynv(sync, pid % 4, 0, 3)
                emit_qpick(sync)
                kgv = io["kg"][:, :, :].rearrange("(s g) p u -> g s p u", g=4)
                sync.dma_start(out=kloc[:, :, :],
                               in_=kgv[bass.ts(gi, 1), bass.ts(bqi, 4), :, :].rearrange(
                                   "one s p u -> s (one p) u")).then_inc(ld.h, 16)
                vgv = io["vg"][:, :, :].rearrange("(s g) p u -> g s p u", g=4)
                sync.dma_start(out=vloc[:, :, :],
                               in_=vgv[bass.ts(gi, 1), bass.ts(bqi, 4), :, :].rearrange(
                                   "one s p u -> s (one p) u")).then_inc(ld.h, 16)
                glv = io["glbt"][:, :].rearrange("(s r) u -> r s u", r=32)
                g4i = dynv(sync, pid % 4, 0, 3)
                sync.dma_start(out=glloc[:, :, :],
                               in_=glv[bass.ts(g4i, 4), bass.ts(bqi, 4), :]).then_inc(ld.h, 16)
                bt4i = dynv(sync, 4 + pid % 4, 0, 7)
                sync.dma_start(out=btloc[:, :, :],
                               in_=glv[bass.ts(bt4i, 4), bass.ts(bqi, 4), :]).then_inc(ld.h, 16)
                ld.n += 4 * 16
                sync.wait_ge(ld.h, ld.n)
                for j in range(2):
                    for q in range(4):
                        sync.dma_start(out=qhs[:, j*S + q*TLOC: j*S + (q+1)*TLOC],
                                       in_=qloc[j, q, :, :]).then_inc(ld.h, 16)
                for q in range(4):
                    sync.dma_start(out=khs[:, q*TLOC:(q+1)*TLOC],
                                   in_=kloc[q, :, :]).then_inc(ld.h, 16)
                    sync.dma_start(out=vhs[:, q*TLOC:(q+1)*TLOC],
                                   in_=vloc[q, :, :]).then_inc(ld.h, 16)
                ld.n += 16 * 16

            @blk.gpsimd
            def _(gp):
                gp.iota(idint[:], pattern=[[1, 128]], base=0, channel_multiplier=-1)

            @blk.vector
            def _(vector):
                vector.memset(ones_row[:], 1.0)
                vector.memset(s4[:], 0.0)
                vector.memset(s4bf[:], 0.0)
                vector.memset(stg[:], 0.0)

        with nc.Block() as blk:
            @blk.vector
            def _(vector):
                vector.tensor_single_scalar(out=ident_f[:], in_=idint[:], scalar=0, op=ALU.is_equal)
                vector.tensor_copy(out=ident_b[:], in_=ident_f[:])
                vector.tensor_single_scalar(out=mbU[:], in_=idint[:], scalar=1, op=ALU.is_lt)
                vector.tensor_scalar(out=mbU[:], in0=mbU[:], scalar1=-60.0, scalar2=None, op0=ALU.mult)
                vector.tensor_single_scalar(out=mbUD[:], in_=idint[:], scalar=0, op=ALU.is_lt)
                vector.tensor_scalar(out=mbUD[:], in0=mbUD[:], scalar1=-60.0, scalar2=None, op0=ALU.mult)

        for hh in range(nheads):
            qsl_base = (hh % 2) * S
            if hh == 2:
                qld0 = ld.n
                with nc.Block() as blk:
                    @blk.sync
                    def _(sync, hh=hh):
                        for j in range(2):
                            for q in range(4):
                                sync.dma_start(out=qhs[:, j*S + q*TLOC: j*S + (q+1)*TLOC],
                                               in_=qloc[2 + j, q, :, :]).then_inc(ld.h, 16)
                        ld.n += 8 * 16
            for ci in range(nch):
                t0 = ci * L
                gidx = hh * nch + ci
                dv0, sc0, pe0, ld0, st0 = dv.n, sc.n, pe.n, ld.n, st.n
                nld = 32 if ci % 16 == 0 else 0
                with nc.Block() as blk:
                    @blk.sync
                    def _(sync, hh=hh, ci=ci):
                        if ci % 16 == 0:
                            q = (ci * L) // TLOC
                            sync.dma_start(out=glst[:], in_=glloc[hh, q, :][None, :]
                                           if False else glloc[hh:hh+1, q, :]).then_inc(ld.h, 16)
                            sync.dma_start(out=btst[:], in_=btloc[hh:hh+1, q, :]).then_inc(ld.h, 16)
                            ld.n += 32

                    @blk.vector
                    def _(vector, hh=hh, ci=ci, dv0=dv0, sc0=sc0, pe0=pe0, ld0=ld0, nld=nld, t0=t0, gidx=gidx):
                        co = (ci % 16) * L
                        vector.wait_ge(ld.h, ld0 + nld)
                        vector.tensor_tensor_scan(out=gc[:], data0=ones_row[:], data1=glst[:, co:co+L],
                                                  initial=0.0, op0=ALU.mult, op1=ALU.add)
                        vector.tensor_scalar(out=nbrow[:], in0=btst[:, co:co+L],
                                             scalar1=-1.0, scalar2=None, op0=ALU.mult)
                        vector.tensor_scalar(out=glrow[:], in0=ones_row[:], scalar1=gc[:, 127:128],
                                             scalar2=None, op0=ALU.mult).then_inc(dv.h, 1)
                        vector.wait_ge(pe.h, pe0 + 1)
                        vector.tensor_tensor(out=mU[:], in0=psA2[:, 0:128], in1=mbU[:], op=ALU.add)
                        vector.tensor_tensor(out=mUD[:], in0=psA2[:, 0:128], in1=mbUD[:], op=ALU.add)
                        vector.tensor_copy(out=cols[:, 0:4], in_=psT[:, 0:4])
                        vector.tensor_scalar(out=cols[:, 4:8], in0=psT[:, 0:4], scalar1=-1.0,
                                             scalar2=None, op0=ALU.mult)
                        vector.tensor_copy(out=vt[:], in_=psBF[:, 0:128])
                        vector.tensor_copy(out=kt[:], in_=psBF[:, 128:256]).then_inc(dv.h, 1)
                        vector.wait_ge(sc.h, sc0 + 2)
                        vector.tensor_tensor(out=mU[:], in0=d2t[:], in1=psA[:, 0:128], op=ALU.mult)
                        vector.tensor_tensor(out=q0[0][:], in0=mU[:], in1=psA[:, 384:512], op=ALU.mult)
                        vector.tensor_tensor(out=m2t[:], in0=d2tud[:], in1=psA[:, 128:256], op=ALU.mult)
                        vector.tensor_scalar(out=zj[:, 0:128], in0=vt[:], scalar1=cols[:, 1:2],
                                             scalar2=None, op0=ALU.mult)
                        vector.tensor_scalar(out=zj[:, 128:256], in0=kt[:], scalar1=cols[:, 1:2],
                                             scalar2=None, op0=ALU.mult)
                        vector.tensor_scalar(out=zj[:, 128:256], in0=zj[:, 128:256], scalar1=gamc[:, 0:1],
                                             scalar2=None, op0=ALU.mult)
                        vector.tensor_scalar(out=khat[:], in0=kt[:], scalar1=cols[:, 2:3],
                                             scalar2=None, op0=ALU.mult)
                        vector.tensor_tensor(out=qgt[:], in0=qhs[:, qsl_base+t0:qsl_base+t0+L],
                                             in1=psA2[:, 128:256], op=ALU.mult).then_inc(dv.h, 1)
                        for r in range(8):
                            vector.wait_ge(pe.h, pe0 + 3 * r + 2)
                            vector.tensor_tensor(out=zj[:], in0=zj[:], in1=psC[:, 0:256],
                                                 op=ALU.add).then_inc(dv.h, 1)
                            if r < 7:
                                vector.wait_ge(pe.h, pe0 + 3 * r + 3)
                                vector.tensor_copy(out=tj[:], in_=psT[:, 128:256]).then_inc(dv.h, 1)
                                vector.wait_ge(pe.h, pe0 + 3 * r + 4)
                                vector.tensor_copy(out=q0[(r + 1) % 2][:], in_=psC2[:]).then_inc(dv.h, 1)
                        vector.wait_ge(pe.h, pe0 + 24)
                        vector.tensor_copy(out=et[:], in_=psT[:, 128:256]).then_inc(dv.h, 1)
                        vector.wait_ge(pe.h, pe0 + 25)
                        vector.tensor_tensor(out=dlt[:], in0=zj[:, 0:128], in1=psD[:, 0:128],
                                             op=ALU.subtract).then_inc(dv.h, 1)
                        vector.wait_ge(pe.h, pe0 + 27)
                        vector.scalar_tensor_tensor(out=s4[:, hh*128:(hh+1)*128],
                                                    in0=s4[:, hh*128:(hh+1)*128],
                                                    scalar=gamLc[:, 0:1], in1=psD[:, 128:256],
                                                    op0=ALU.mult, op1=ALU.add)
                        vector.tensor_copy(out=s4bf[:, hh*128:(hh+1)*128],
                                           in_=s4[:, hh*128:(hh+1)*128]).then_inc(dv.h, 1)

                    @blk.scalar
                    def _(scalar, hh=hh, ci=ci, dv0=dv0, sc0=sc0, pe0=pe0, st0=st0, gidx=gidx):
                        scalar.wait_ge(dv.h, dv0 + 1)
                        scalar.activation(out=gamrow[:], in_=gc[:], func=AF.Exp)
                        scalar.activation(out=ghrow[:], in_=gc[:], func=AF.Exp,
                                          scale=-1.0, bias=gc[:, 127:128]).then_inc(sc.h, 1)
                        scalar.wait_ge(dv.h, dv0 + 2)
                        scalar.activation(out=d2t[:], in_=mU[:], func=AF.Exp, bias=cols[:, 4:5])
                        scalar.activation(out=d2tud[:], in_=mUD[:], func=AF.Exp, bias=cols[:, 4:5])
                        scalar.activation(out=gamc[:], in_=cols[:, 0:1], func=AF.Exp)
                        scalar.activation(out=gamLc[:], in_=cols[:, 3:4], func=AF.Exp).then_inc(sc.h, 1)
                        scalar.wait_ge(pe.h, pe0 + 26)
                        if gidx >= 2:
                            scalar.wait_ge(st.h, 16 * (gidx - 1))
                        scalar.copy(out=obf[:], in_=psO[:]).then_inc(sc.h, 1)
                        scalar.wait_ge(pe.h, pe0 + 28)
                        scalar.copy(out=oev[:], in_=psBF[:, 256:384]).then_inc(sc.h, 1)

                    @blk.gpsimd
                    def _(gp, dv0=dv0, sc0=sc0, ld0=ld0, nld=nld, hh=hh, t0=t0):
                        co = ((t0 // L) % 16) * L
                        gp.wait_ge(sc.h, sc0 + 1)
                        gp.dma_start(out=stg[0:1, :], in_=gc[:]).then_inc(ld.h, 16)
                        gp.dma_start(out=stg[1:2, :], in_=btst[:, co:co+L]).then_inc(ld.h, 16)
                        gp.dma_start(out=stg[2:3, :], in_=ghrow[:]).then_inc(ld.h, 16)
                        gp.dma_start(out=stg[3:4, :], in_=glrow[:]).then_inc(ld.h, 16)
                        ld.n += 64
                        gp.wait_ge(sc.h, sc0 + 4)
                        qq = t0 // TLOC
                        u0 = t0 % TLOC
                        gp.dma_start(out=io["ooq"][qq, hh, :, u0:u0+L], in_=oev[:]).then_inc(st.h, 16)
                        st.n += 16

                    @blk.tensor
                    def _(tensor, hh=hh, ci=ci, t0=t0, qsl_base=qsl_base,
                          dv0=dv0, sc0=sc0, pe0=pe0, ld0=ld0, nld=nld):
                        tensor.wait_ge(ld.h, ld0 + nld + 64)
                        tensor.transpose(out=psT[:, 0:128], in_=stg[:], identity=ident_f[:])
                        tensor.matmul(out=psA2[:, 0:128], lhsT=ones_row[:, 0:128],
                                      rhs=gc[:], start=True, stop=True)
                        tensor.matmul(out=psA[:, 384:512], lhsT=ones_row[:, 0:128],
                                      rhs=nbrow[:], start=True, stop=True)
                        tensor.matmul(out=psA2[:, 128:256], lhsT=ones_row[:, 0:128],
                                      rhs=gamrow[:], start=True, stop=True)
                        tensor.matmul(out=psA[:, 0:128], lhsT=khs[:, t0:t0+L],
                                      rhs=khs[:, t0:t0+L], start=True, stop=True)
                        tensor.matmul(out=psA[:, 128:256], lhsT=khs[:, t0:t0+L],
                                      rhs=qhs[:, qsl_base+t0:qsl_base+t0+L], start=True, stop=True)
                        tensor.transpose(out=psBF[:, 0:128], in_=vhs[:, t0:t0+L], identity=ident_b[:])
                        tensor.transpose(out=psBF[:, 128:256], in_=khs[:, t0:t0+L],
                                         identity=ident_b[:]).then_inc(pe.h, 1)
                        pe.n += 1
                        tensor.wait_ge(dv.h, dv0 + 3)
                        for r in range(8):
                            tensor.matmul(out=psC[:, 0:256], lhsT=q0[r % 2][:], rhs=zj[:],
                                          start=True, stop=True).then_inc(pe.h, 1)
                            pe.n += 1
                            if r < 7:
                                tensor.transpose(out=psT[:, 128:256], in_=q0[r % 2][:],
                                                 identity=ident_f[:]).then_inc(pe.h, 1)
                                pe.n += 1
                                tensor.wait_ge(dv.h, dv0 + 3 + 3 * r + 2)
                                tensor.matmul(out=psC2[:], lhsT=tj[:], rhs=q0[r % 2][:],
                                              start=True, stop=True).then_inc(pe.h, 1)
                                pe.n += 1
                                tensor.wait_ge(dv.h, dv0 + 3 + 3 * r + 3)
                            else:
                                tensor.wait_ge(dv.h, dv0 + 3 + 3 * 7 + 1)
                        tensor.transpose(out=psT[:, 128:256], in_=zj[:, 128:256],
                                         identity=ident_f[:]).then_inc(pe.h, 1)
                        pe.n += 1
                        tensor.wait_ge(dv.h, dv0 + 26)
                        tensor.matmul(out=psD[:, 0:128], lhsT=et[:], rhs=s4[:, hh*128:(hh+1)*128],
                                      start=True, stop=True).then_inc(pe.h, 1)
                        pe.n += 1
                        tensor.wait_ge(dv.h, dv0 + 27)
                        tensor.matmul(out=psO[:], lhsT=m2t[:], rhs=dlt[:], start=True, stop=False)
                        tensor.matmul(out=psO[:], lhsT=qgt[:], rhs=s4bf[:, hh*128:(hh+1)*128],
                                      start=False, stop=True).then_inc(pe.h, 1)
                        pe.n += 1
                        tensor.matmul(out=psD[:, 128:256], lhsT=khat[:], rhs=dlt[:],
                                      start=True, stop=True).then_inc(pe.h, 1)
                        pe.n += 1
                        tensor.wait_ge(sc.h, sc0 + 3)
                        tensor.transpose(out=psBF[:, 256:384], in_=obf[:],
                                         identity=ident_b[:]).then_inc(pe.h, 1)
                        pe.n += 1

                dv.n += 28
                sc.n += 4

        with nc.Block() as blk:
            @blk.gpsimd
            def _(gp):
                gp.wait_ge(st.h, st.n)
                if dbg is not None:
                    dsem = 0
                    for name, tile in [("gc", gc[:]), ("cols", cols[:]), ("d2t", d2t[:]),
                                       ("d2tud", d2tud[:]), ("q0a", q0[0][:]), ("q0b", q0[1][:]),
                                       ("zj", zj[:]), ("m2t", m2t[:]), ("dlt", dlt[:]),
                                       ("s4", s4[:]), ("obf", obf[:]), ("khat", khat[:]),
                                       ("qgt", qgt[:]), ("vt", vt[:]), ("kt", kt[:]),
                                       ("et", et[:]), ("gamc", gamc[:]), ("gamLc", gamLc[:]),
                                       ("ngcol", ngcol[:]), ("mbU_t", mbU[:]), ("mbUD_t", mbUD[:])]:
                        if name in dbg:
                            gp.dma_start(out=dbg[name], in_=tile).then_inc(st.h, 16)
                            dsem += 16
                    gp.wait_ge(st.h, st.n + dsem)

def build_c1(nc, io):
    TT = 512
    ntt = TLOC // TT
    RDK = 1.0 / (128.0 ** 0.5)

    NMT = 41
    def cls(i):
        if i < 20: return "nrm"
        if i < 24: return "vv"
        if i < 40: return "gg"
        return "bgk"
    cum = {k: [0] for k in ("mm", "cons", "convd", "sqd", "nsum", "r1d", "r2d",
                            "r3d", "nbc", "fin", "st", "bgkd")}
    for g in range(ntt * NMT):
        i = g % NMT
        c = cls(i)
        add = {"mm": 1, "cons": 1,
               "convd": 1 if i <= 23 else 0,
               "sqd": 1 if c == "nrm" else 0,
               "nsum": 1 if c == "nrm" else 0,
               "r1d": 1 if c == "nrm" else 0,
               "r2d": 1 if c == "nrm" else 0,
               "r3d": 1 if c == "nrm" else 0,
               "nbc": 1 if c == "nrm" else 0,
               "fin": 1,
               "st": 2 if c == "bgk" else 1,
               "bgkd": 1 if c == "bgk" else 0}
        for k in cum:
            cum[k].append(cum[k][g] + add[k])

    with ExitStack() as ctx:
        sb = lambda *a: ctx.enter_context(nc.sbuf_tensor(*a))
        ps = lambda *a: ctx.enter_context(nc.psum_tensor(*a))
        ones_col = sb("c1_onc", [128, 1], BF16)
        ones_f32 = sb("c1_onf", [1, 128], F32)
        convw = sb("c1_cw", [128, 24 * 4], F32)
        hist = sb("c1_hist", [128, 24 * 3], F32)
        rhs = [sb(f"c1_rhs{i}", [128, 16 * TT], BF16) for i in range(2)]
        lhsb = [sb(f"c1_lhs{i}", [128, 16 * 128], BF16) for i in range(2)]
        xs = sb("c1_xs", [128, TT + 3], F32)
        cacc = sb("c1_cacc", [128, TT], F32)
        qc = sb("c1_qc", [128, TT], BF16)
        sq = sb("c1_sq", [128, TT], BF16)
        row1 = sb("c1_row1", [1, TT], F32)
        dmy = sb("c1_dmy", [1, 8], F32)
        row2 = sb("c1_row2", [1, TT], F32)
        ob = [sb(f"c1_ob{i}", [128, TT], BF16) for i in range(2)]
        bge = sb("c1_bge", [48, 2 * TT], F32)
        pmb = [ps(f"c1_pm{i}", [128, TT], F32) for i in range(2)]
        st_ps = ps("c1_stp", [1, TT], F32)
        bc_ps = ps("c1_bcp", [128, TT], F32)

        ld = Sem(ctx, nc, "c1_ld")
        lw = Sem(ctx, nc, "c1_lw")
        mm = Sem(ctx, nc, "c1_mm")
        nsum = Sem(ctx, nc, "c1_nsum")
        nbc = Sem(ctx, nc, "c1_nbc")
        cons = Sem(ctx, nc, "c1_cons")
        convd = Sem(ctx, nc, "c1_convd")
        sqd = Sem(ctx, nc, "c1_sqd")
        r1d = Sem(ctx, nc, "c1_r1d")
        r2d = Sem(ctx, nc, "c1_r2d")
        r3d = Sem(ctx, nc, "c1_r3d")
        fin = Sem(ctx, nc, "c1_fin")
        bgkd = Sem(ctx, nc, "c1_bgkd")
        stq = Sem(ctx, nc, "c1_stq")

        def wsl(colbase, w):
            return io["projw_T"][:, colbase:colbase + w].rearrange("(kc p) m -> p kc m", p=128)

        CPREP = 24

        with nc.Block() as blk:
            @blk.sync
            def _(sync):
                for mt in range(24):
                    sync.dma_start(out=convw[:, mt*4:(mt+1)*4],
                                   in_=io["conv_w"][mt*128:(mt+1)*128, :]).then_inc(ld.h, 16)
                for kc in range(16):
                    sync.dma_start(out=rhs[0][:, kc*3:(kc+1)*3],
                                   in_=io["s2_h"][kc, :, 0:3]).then_inc(ld.h, 16)
                ld.n += 40 * 16
                for mt in range(24):
                    if mt >= 2:
                        sync.wait_ge(mm.h, mt - 1)
                    sync.dma_start(out=lhsb[mt % 2][:].rearrange("p (kc m) -> p kc m", m=128),
                                   in_=wsl(mt * 128, 128)).then_inc(lw.h, 16)
                lw.n += 24 * 16

            @blk.vector
            def _(vector):
                vector.memset(ones_col[:], 1.0)
                vector.memset(ones_f32[:], 1.0)
                for mt in range(24):
                    vector.wait_ge(mm.h, mt + 1)
                    vector.tensor_copy(out=hist[:, mt*3:(mt+1)*3],
                                       in_=pmb[mt % 2][:, 0:3]).then_inc(cons.h, 1)
                cons.n += 24

            @blk.tensor
            def _(tensor):
                tensor.wait_ge(ld.h, 40 * 16)
                for mt in range(24):
                    tensor.wait_ge(lw.h, 16 * (mt + 1))
                    if mt >= 2:
                        tensor.wait_ge(cons.h, mt - 1)
                    for kc in range(16):
                        inst = tensor.matmul(out=pmb[mt % 2][:, 0:3],
                                             lhsT=lhsb[mt % 2][:, kc*128:(kc+1)*128],
                                             rhs=rhs[0][:, kc*3:(kc+1)*3],
                                             start=(kc == 0), stop=(kc == 15))
                    inst.then_inc(mm.h, 1)
                mm.n += 24

        MMP = 24

        for tt in range(ntt):
            cur = rhs[tt % 2]
            ld0 = ld.n
            lw0 = lw.n
            with nc.Block() as blk:
                @blk.sync
                def _(sync, tt=tt):
                    for kc in range(16):
                        sync.dma_start(out=cur[:, kc*TT:(kc+1)*TT],
                                       in_=io["s2_h"][kc, :, 3+tt*TT:3+(tt+1)*TT]).then_inc(ld.h, 16)
                    ld.n += 256
                    for i in range(NMT):
                        g = tt * NMT + i
                        if g >= 2:
                            sync.wait_ge(mm.h, MMP + cum["mm"][g - 1])
                        w = 64 if i == 40 else 128
                        sync.dma_start(out=lhsb[i % 2][:, 0:16*w].rearrange("p (kc m) -> p kc m", m=w),
                                       in_=wsl(i * 128, w)).then_inc(lw.h, 16)
                    lw.n += NMT * 16

                @blk.tensor
                def _(tensor, tt=tt, cur=cur, ld0=ld0, lw0=lw0):
                    tensor.wait_ge(ld.h, ld0 + 256)
                    for i in range(NMT):
                        g = tt * NMT + i
                        w = 64 if i == 40 else 128
                        tensor.wait_ge(lw.h, lw0 + 16 * (i + 1))
                        if g >= 2:
                            tensor.wait_ge(cons.h, CPREP + cum["cons"][g - 1])
                        for kc in range(16):
                            inst = tensor.matmul(out=pmb[i % 2][0:w, 0:TT],
                                                 lhsT=lhsb[i % 2][:, kc*w:(kc+1)*w],
                                                 rhs=cur[:, kc*TT:(kc+1)*TT],
                                                 start=(kc == 0), stop=(kc == 15))
                        inst.then_inc(mm.h, 1)
                        if cls(i) == "nrm":
                            tensor.wait_ge(sqd.h, cum["sqd"][g + 1])
                            tensor.matmul(out=st_ps[0:1, :], lhsT=ones_col[:], rhs=sq[:],
                                          start=True, stop=True).then_inc(nsum.h, 1)
                            tensor.wait_ge(r3d.h, cum["r3d"][g + 1])
                            tensor.matmul(out=bc_ps[:], lhsT=ones_f32[:, 0:128], rhs=row1[:],
                                          start=True, stop=True).then_inc(nbc.h, 1)
                    mm.n += NMT
                    nsum.n += 20
                    nbc.n += 20

                @blk.vector
                def _(vector, tt=tt):
                    for i in range(NMT):
                        g = tt * NMT + i
                        if i <= 23:
                            vector.wait_ge(mm.h, MMP + cum["mm"][g + 1])
                            if g >= 1:
                                vector.wait_ge(fin.h, cum["fin"][g])
                            vector.tensor_copy(out=xs[:, 0:3], in_=hist[:, i*3:(i+1)*3])
                            vector.tensor_copy(out=xs[:, 3:3+TT], in_=pmb[i % 2][:, 0:TT])
                            vector.tensor_copy(out=hist[:, i*3:(i+1)*3],
                                               in_=pmb[i % 2][:, TT-3:TT]).then_inc(cons.h, 1)
                            cons.n += 1
                            vector.tensor_scalar(out=cacc[:], in0=xs[:, 0:TT],
                                                 scalar1=convw[:, i*4:i*4+1], scalar2=None, op0=ALU.mult)
                            for jj in (1, 2, 3):
                                inst = vector.scalar_tensor_tensor(out=cacc[:], in0=xs[:, jj:jj+TT],
                                                                   scalar=convw[:, i*4+jj:i*4+jj+1],
                                                                   in1=cacc[:], op0=ALU.mult, op1=ALU.add)
                            inst.then_inc(convd.h, 1)
                            convd.n += 1
                        if cls(i) == "nrm":
                            vector.wait_ge(nsum.h, cum["nsum"][g + 1])
                            vector.tensor_scalar(out=row1[:], in0=st_ps[0:1, :], scalar1=1.0,
                                                 scalar2=1e-6, op0=ALU.mult,
                                                 op1=ALU.add).then_inc(r1d.h, 1)
                            r1d.n += 1
                            vector.wait_ge(r2d.h, cum["r2d"][g + 1])
                            vector.reciprocal(out=row1[:], in_=row2[:])
                            sc1 = RDK if i < 16 else 1.0
                            vector.tensor_scalar(out=row1[:], in0=row1[:], scalar1=sc1,
                                                 scalar2=None, op0=ALU.mult).then_inc(r3d.h, 1)
                            r3d.n += 1
                            vector.wait_ge(nbc.h, cum["nbc"][g + 1])
                            vector.wait_ge(stq.h, 16 * cum["st"][g])
                            vector.tensor_tensor(out=ob[g % 2][:], in0=qc[:],
                                                 in1=bc_ps[:], op=ALU.mult).then_inc(fin.h, 1)
                            fin.n += 1
                        elif cls(i) == "bgk":
                            vector.wait_ge(bgkd.h, cum["bgkd"][g + 1])
                            vector.tensor_tensor(out=bge[32:48, 0:TT], in0=bge[32:48, 0:TT],
                                                 in1=bge[32:48, TT:2*TT], op=ALU.add)
                            vector.tensor_scalar(out=bge[32:48, 0:TT], in0=bge[32:48, 0:TT],
                                                 scalar1=-1.0/16.0, scalar2=None,
                                                 op0=ALU.mult).then_inc(fin.h, 1)
                            fin.n += 1

                @blk.scalar
                def _(scalar, tt=tt):
                    for i in range(NMT):
                        g = tt * NMT + i
                        if i <= 23:
                            scalar.wait_ge(convd.h, cum["convd"][g + 1])
                            if cls(i) == "nrm":
                                scalar.activation(out=qc[:], in_=cacc[:], func=AF.Silu)
                                scalar.activation(out=sq[:], in_=qc[:],
                                                  func=AF.Square).then_inc(sqd.h, 1)
                                sqd.n += 1
                                scalar.wait_ge(r1d.h, cum["r1d"][g + 1])
                                scalar.activation(out=row2[:], in_=row1[:],
                                                  func=AF.Sqrt).then_inc(r2d.h, 1)
                                r2d.n += 1
                            else:
                                scalar.wait_ge(stq.h, 16 * cum["st"][g])
                                scalar.activation(out=ob[g % 2][:], in_=cacc[:],
                                                  func=AF.Silu).then_inc(fin.h, 1)
                                fin.n += 1
                        elif cls(i) == "gg":
                            scalar.wait_ge(mm.h, MMP + cum["mm"][g + 1])
                            scalar.wait_ge(stq.h, 16 * cum["st"][g])
                            inst = scalar.copy(out=ob[g % 2][:], in_=pmb[i % 2][:, 0:TT])
                            inst.then_inc(fin.h, 1)
                            scalar.copy(out=dmy[:], in_=dmy[:]).then_inc(cons.h, 1)
                            fin.n += 1
                            cons.n += 1
                        else:
                            scalar.wait_ge(mm.h, MMP + cum["mm"][g + 1])
                            scalar.wait_ge(stq.h, 16 * cum["st"][g])
                            scalar.activation(out=bge[0:16, 0:TT], in_=pmb[i % 2][0:16, 0:TT],
                                              func=AF.Sigmoid)
                            scalar.activation(out=bge[32:48, TT:2*TT], in_=pmb[i % 2][32:48, 0:TT],
                                              func=AF.Relu, scale=-1.0)
                            scalar.activation(out=bge[32:48, 0:TT], in_=pmb[i % 2][32:48, 0:TT],
                                              func=AF.Abs)
                            scalar.activation(out=bge[32:48, 0:TT], in_=bge[32:48, 0:TT],
                                              func=AF.Exp, scale=-1.0)
                            inst = scalar.activation(out=bge[32:48, 0:TT], in_=bge[32:48, 0:TT],
                                                     func=AF.Ln, bias=1.0)
                            inst.then_inc(bgkd.h, 1)
                            scalar.copy(out=dmy[:], in_=dmy[:]).then_inc(cons.h, 1)
                            bgkd.n += 1
                            cons.n += 1

                @blk.gpsimd
                def _(gp, tt=tt):
                    for i in range(NMT):
                        g = tt * NMT + i
                        gp.wait_ge(fin.h, cum["fin"][g + 1])
                        c = cls(i)
                        if c == "nrm" and i < 16:
                            dst = io["qn"][i, :, tt*TT:(tt+1)*TT]
                        elif c == "nrm":
                            dst = io["kn"][i - 16, :, tt*TT:(tt+1)*TT]
                        elif c == "vv":
                            dst = io["vv"][i - 20, :, tt*TT:(tt+1)*TT]
                        elif c == "gg":
                            dst = io["gg"][i - 24, :, tt*TT:(tt+1)*TT]
                        else:
                            gp.dma_start(out=io["beta"][:, tt*TT:(tt+1)*TT],
                                         in_=bge[0:16, 0:TT]).then_inc(stq.h, 16)
                            gp.dma_start(out=io["glog"][:, tt*TT:(tt+1)*TT],
                                         in_=bge[32:48, 0:TT]).then_inc(stq.h, 16)
                            stq.n += 32
                            continue
                        gp.dma_start(out=dst, in_=ob[g % 2][:]).then_inc(stq.h, 16)
                        stq.n += 16

        with nc.Block() as blk:
            @blk.gpsimd
            def _(gp):
                gp.wait_ge(stq.h, stq.n)

def build_c3(nc, io):
    TT = 512
    ntt = TLOC // TT
    ogloc = nc.dram_tensor("c3_ogloc", [16, 128, TLOC], BF16, kind="Internal").ap()
    with ExitStack() as ctx:
        sb = lambda *a: ctx.enter_context(nc.sbuf_tensor(*a))
        ps = lambda *a: ctx.enter_context(nc.psum_tensor(*a))
        ones_col = sb("c3_onc", [128, 1], BF16)
        ones_f32 = sb("c3_onf", [1, 128], F32)
        onw = sb("c3_onw", [128, 1], F32)
        ot = [sb(f"c3_ot{i}", [128, 16 * TT], BF16) for i in range(2)]
        gtile = sb("c3_gt", [128, 16 * TT], BF16)
        s2t = sb("c3_s2t", [128, 16 * TT], BF16)
        lhsb = [sb(f"c3_lh{i}", [128, 16 * 128], BF16) for i in range(2)]
        sq = sb("c3_sq", [128, TT], BF16)
        u1 = sb("c3_u1", [128, TT], BF16)
        row1 = sb("c3_row1", [1, TT], F32)
        row2 = sb("c3_row2", [1, TT], F32)
        r = sb("c3_r", [128, 16 * TT], F32)
        rbf = sb("c3_rbf", [128, 16 * TT], BF16)
        sqf = sb("c3_sqf", [128, 16 * TT], BF16)
        mrow = sb("c3_mrow", [1, TT], F32)
        vtmp = sb("c3_vtmp", [1, TT], F32)
        rstd = sb("c3_rstd", [1, TT], F32)
        pmb = [ps(f"c3_pm{i}", [128, TT], F32) for i in range(4)]
        st_ps = [ps(f"c3_st{i}", [1, TT], F32) for i in range(2)]
        bc_ps = [ps(f"c3_bc{i}", [128, TT], F32) for i in range(2)]

        ld = Sem(ctx, nc, "c3_ld")
        lw = Sem(ctx, nc, "c3_lw")
        mm = Sem(ctx, nc, "c3_mm")
        nsum = Sem(ctx, nc, "c3_ns")
        nbc = Sem(ctx, nc, "c3_nb")
        sgd = Sem(ctx, nc, "c3_sgd")
        r1d = Sem(ctx, nc, "c3_r1")
        r3d = Sem(ctx, nc, "c3_r3")
        fin = Sem(ctx, nc, "c3_fin")
        sev = Sem(ctx, nc, "c3_sev")
        stq = Sem(ctx, nc, "c3_stq")

        with nc.Block() as blk:
            @blk.vector
            def _(vector):
                vector.memset(ones_col[:], 1.0)
                vector.memset(ones_f32[:], 1.0)

            @blk.sync
            def _(sync):
                sync.dma_start(out=onw[:], in_=io["onw"][:]).then_inc(ld.h, 16)
                pid = sync.partition_id()
                bqi = dynv(sync, pid // 4, 0, 1)
                qi = dynv(sync, pid % 4, 0, 3)
                ogv = io["og"][:, :, :].rearrange("(s q h) p u -> s q h p u", q=4, h=4)
                sync.dma_start(out=ogloc[:, :, :].rearrange("(hg hs) p u -> hg hs p u", hs=4),
                               in_=ogv[bass.ts(bqi, 4), bass.ts(qi, 1), :, :, :].rearrange(
                                   "hg one hs p u -> hg (one hs) p u")).then_inc(ld.h, 16)
                sync.wait_ge(ld.h, ld.n + 2 * 16)
                for kc in range(16):
                    sync.dma_start(out=ot[0][:, kc*TT:(kc+1)*TT],
                                   in_=ogloc[kc, :, 0:TT]).then_inc(ld.h, 16)
                ld.n += 18 * 16

        for tt in range(ntt):
            cur = ot[tt % 2]
            ld0 = ld.n
            with nc.Block() as blk:
                @blk.sync
                def _(sync, tt=tt):
                    if tt + 1 < ntt:
                        for kc in range(16):
                            sync.dma_start(out=ot[(tt+1) % 2][:, kc*TT:(kc+1)*TT],
                                           in_=ogloc[kc, :, (tt+1)*TT:(tt+2)*TT]).then_inc(ld.h, 16)
                        ld.n += 256
                    for kc in range(16):
                        sync.dma_start(out=gtile[:, kc*TT:(kc+1)*TT],
                                       in_=io["gg"][kc, :, tt*TT:(tt+1)*TT]).then_inc(ld.h, 16)
                    for kc in range(16):
                        sync.dma_start(out=s2t[:, kc*TT:(kc+1)*TT],
                                       in_=io["s2f"][kc, :, tt*TT:(tt+1)*TT]).then_inc(ld.h, 16)
                    ld.n += 512

                @blk.scalar
                def _(scalar, tt=tt, ld0=ld0):
                    gld = 256 if tt + 1 < ntt else 0
                    scalar.wait_ge(ld.h, ld0 + gld + 256)
                    for hc in range(16):
                        g = tt * 16 + hc
                        scalar.activation(out=sq[:], in_=cur[:, hc*TT:(hc+1)*TT],
                                          func=AF.Square).then_inc(sgd.h, 1)
                        scalar.wait_ge(r1d.h, g + 1)
                        scalar.activation(out=row2[:], in_=row1[:], func=AF.Sqrt).then_inc(sgd.h, 1)
                        scalar.activation(out=u1[:], in_=gtile[:, hc*TT:(hc+1)*TT],
                                          func=AF.Silu).then_inc(sgd.h, 1)
                        sgd.n += 3

                @blk.tensor
                def _(tensor, tt=tt):
                    for hc in range(16):
                        g = tt * 16 + hc
                        tensor.wait_ge(sgd.h, 3 * g + 1)
                        tensor.matmul(out=st_ps[0][0:1, :], lhsT=ones_col[:], rhs=sq[:],
                                      start=True, stop=True).then_inc(nsum.h, 1)
                        nsum.n += 1
                        tensor.wait_ge(r3d.h, g + 1)
                        tensor.matmul(out=bc_ps[0][:], lhsT=ones_f32[:, 0:128], rhs=row1[:],
                                      start=True, stop=True).then_inc(nbc.h, 1)
                        nbc.n += 1

                @blk.vector
                def _(vector, tt=tt):
                    for hc in range(16):
                        g = tt * 16 + hc
                        vector.wait_ge(nsum.h, g + 1)
                        vector.tensor_scalar(out=row1[:], in0=st_ps[0][0:1, :], scalar1=1.0/128.0,
                                             scalar2=1e-5, op0=ALU.mult, op1=ALU.add).then_inc(r1d.h, 1)
                        r1d.n += 1
                        vector.wait_ge(sgd.h, 3 * g + 2)
                        vector.reciprocal(out=row1[:], in_=row2[:]).then_inc(r3d.h, 1)
                        r3d.n += 1
                        vector.wait_ge(nbc.h, g + 1)
                        vector.wait_ge(sgd.h, 3 * g + 3)
                        vector.tensor_scalar(out=sq[:], in0=cur[:, hc*TT:(hc+1)*TT],
                                             scalar1=onw[:, 0:1], scalar2=None, op0=ALU.mult)
                        vector.tensor_tensor(out=sq[:], in0=sq[:], in1=bc_ps[0][:], op=ALU.mult)
                        vector.tensor_tensor(out=cur[:, hc*TT:(hc+1)*TT], in0=sq[:],
                                             in1=u1[:], op=ALU.mult).then_inc(fin.h, 1)
                        fin.n += 1

            mm0 = mm.n
            sev0 = sev.n
            with nc.Block() as blk:
                @blk.sync
                def _(sync, mm0=mm0):
                    for mt in range(16):
                        if mt >= 2:
                            sync.wait_ge(mm.h, mm0 + mt - 1)
                        sync.dma_start(out=lhsb[mt % 2][:].rearrange("p (kc m) -> p kc m", m=128),
                                       in_=io["ow_T"][:, mt*128:(mt+1)*128].rearrange(
                                           "(kc p) m -> p kc m", p=128)).then_inc(lw.h, 16)
                    lw.n += 256

                @blk.tensor
                def _(tensor, cur=cur, mm0=mm0, sev0=sev0):
                    lw0 = lw.n - 256
                    for mt in range(16):
                        tensor.wait_ge(lw.h, lw0 + 16 * (mt + 1))
                        if mt >= 4:
                            tensor.wait_ge(sev.h, sev0 + mt - 3)
                        for kc in range(16):
                            inst = tensor.matmul(out=pmb[mt % 4][:],
                                                 lhsT=lhsb[mt % 2][:, kc*128:(kc+1)*128],
                                                 rhs=cur[:, kc*TT:(kc+1)*TT],
                                                 start=(kc == 0), stop=(kc == 15))
                        inst.then_inc(mm.h, 1)
                    mm.n += 16

                @blk.vector
                def _(vector, mm0=mm0, sev0=sev0):
                    for mt in range(16):
                        vector.wait_ge(mm.h, mm0 + mt + 1)
                        vector.tensor_tensor(out=r[:, mt*TT:(mt+1)*TT], in0=pmb[mt % 4][:],
                                             in1=s2t[:, mt*TT:(mt+1)*TT], op=ALU.add).then_inc(sev.h, 1)
                        sev.n += 1

            lnt = {"r": r, "rbf": rbf, "sq": sqf, "out": rbf, "ones_col": ones_col,
                   "ones_f32": ones_f32, "st_ps": st_ps, "bc_ps": bc_ps,
                   "mrow": mrow, "vtmp": vtmp, "rstd": rstd}
            with nc.Block() as blk:
                ln = LNEmit(lnt, TT, mm, sev)

                @blk.tensor
                def _(tensor, ln=ln):
                    ln.tensor(tensor)

                @blk.scalar
                def _(scalar, ln=ln):
                    ln.scalar(scalar)

                @blk.vector
                def _(vector, ln=ln, tt=tt):
                    if tt >= 1:
                        vector.wait_ge(stq.h, 256 * tt)
                    ln.vector(vector)

                @blk.gpsimd
                def _(gp, tt=tt, ln=ln):
                    gp.wait_ge(sev.h, ln.sev0 + 50)
                    for kc in range(16):
                        gp.dma_start(out=io["s3"][kc, :, tt*TT:(tt+1)*TT],
                                     in_=rbf[:, kc*TT:(kc+1)*TT]).then_inc(stq.h, 16)
                    stq.n += 256

        with nc.Block() as blk:
            @blk.gpsimd
            def _(gp):
                gp.wait_ge(stq.h, stq.n)


def build_d1(nc, units):
    with ExitStack() as ctx:
        sb = lambda *a: ctx.enter_context(nc.sbuf_tensor(*a))
        ps = lambda *a: ctx.enter_context(nc.psum_tensor(*a))
        ident_b = sb("d1_idb", [128, 128], BF16)
        idint = sb("d1_idi", [128, 128], I32)
        ln2b = sb("d1_ln2b", [128, 1], F32)
        pk = [sb(f"d1_pk{i}", [128, 1024], I32) for i in range(2)]
        scl = [sb(f"d1_sc{i}", [128, 64], F32) for i in range(2)]
        sce = sb("d1_sce", [128, 2048], F32)
        idx = sb("d1_idx", [128, 2048], I32)
        mi = sb("d1_mi", [128, 2048], I32)
        s_i = sb("d1_si", [128, 2048], I32)
        ef = sb("d1_ef", [128, 2048], F32)
        m0f = sb("d1_m0", [128, 2048], F32)
        sf = sb("d1_sf", [128, 2048], F32)
        p2 = sb("d1_p2", [128, 2048], F32)
        v1 = sb("d1_v1", [128, 2048], F32)
        v0 = sb("d1_v0", [128, 2048], F32)
        cm = sb("d1_cm", [128, 2048], I32)
        vv = [sb(f"d1_vv{i}", [128, 2048], BF16) for i in range(2)]
        ev = sb("d1_ev", [128, 2048], BF16)
        psB = ps("d1_psB", [128, 2048], BF16)

        ld = Sem(ctx, nc, "d1_ld")
        dv = Sem(ctx, nc, "d1_dv")
        sc_s = Sem(ctx, nc, "d1_sc")
        pe = Sem(ctx, nc, "d1_pe")
        stq = Sem(ctx, nc, "d1_st")

        with nc.Block() as blk:
            @blk.gpsimd
            def _(gp):
                gp.iota(idint[:], pattern=[[1, 128]], base=0, channel_multiplier=-1)

            @blk.sync
            def _(sync):
                sync.dma_start(out=pk[0][:], in_=units[0]["pk"]).then_inc(ld.h, 16)
                sync.dma_start(out=scl[0][:], in_=units[0]["sc"]).then_inc(ld.h, 16)
                ld.n += 32

        with nc.Block() as blk:
            @blk.vector
            def _(vector):
                vector.tensor_single_scalar(out=ident_b[:], in_=idint[:], scalar=0, op=ALU.is_equal)
                vector.memset(ln2b[:], -0.6931471805599453)

        for u, unit in enumerate(units):
            qb = u % 2
            ld0 = ld.n
            dv0 = dv.n
            sc0 = sc_s.n
            pe0 = pe.n
            st0 = stq.n
            with nc.Block() as blk:
                @blk.sync
                def _(sync, u=u, unit=unit, qb=qb):
                    if u + 1 < len(units):
                        sync.dma_start(out=pk[(u+1) % 2][:], in_=units[u+1]["pk"]).then_inc(ld.h, 16)
                        sync.dma_start(out=scl[(u+1) % 2][:], in_=units[u+1]["sc"]).then_inc(ld.h, 16)
                        ld.n += 32

                @blk.vector
                def _(vector, u=u, qb=qb, dv0=dv0, sc0=sc0, st0=st0):
                    vector.wait_ge(ld.h, 32 * (u + 1))
                    iv = idx[:].rearrange("p (k two) -> p k two", two=2)
                    vector.tensor_single_scalar(out=iv[:, :, 0:1], in_=pk[qb][:], scalar=15,
                                                op=ALU.bitwise_and)
                    vector.tensor_single_scalar(out=iv[:, :, 1:2],
                                                in_=pk[qb][:], scalar=4, op=ALU.logical_shift_right)
                    vector.tensor_single_scalar(out=iv[:, :, 1:2], in_=iv[:, :, 1:2],
                                                scalar=15, op=ALU.bitwise_and)
                    vector.tensor_single_scalar(out=mi[:], in_=idx[:], scalar=7, op=ALU.bitwise_and)
                    vector.tensor_single_scalar(out=s_i[:], in_=idx[:], scalar=3, op=ALU.logical_shift_right)
                    vector.tensor_copy(out=sf[:], in_=s_i[:])
                    vector.tensor_single_scalar(out=idx[:], in_=mi[:], scalar=1, op=ALU.logical_shift_right)
                    vector.tensor_copy(out=ef[:], in_=idx[:])
                    vector.tensor_single_scalar(out=mi[:], in_=mi[:], scalar=1, op=ALU.bitwise_and)
                    vector.tensor_copy(out=m0f[:], in_=mi[:])
                    se = sce[:].rearrange("p (k r) -> p k r", r=32)
                    si = scl[qb][:].rearrange("p (k one) -> p k one", one=1)
                    for j in range(32):
                        vector.tensor_copy(out=se[:, :, j:j+1], in_=si)
                    inst = vector.tensor_scalar(out=v0[:], in0=m0f[:], scalar1=0.5,
                                                scalar2=None, op0=ALU.mult)
                    inst.then_inc(dv.h, 1)
                    dv.n += 1

                @blk.scalar
                def _(scalar, qb=qb, dv0=dv0):
                    scalar.wait_ge(dv.h, dv0 + 1)
                    LN2 = 0.6931471805599453
                    scalar.activation(out=p2[:], in_=ef[:], func=AF.Exp, scale=LN2,
                                      bias=ln2b[:, 0:1]).then_inc(sc_s.h, 1)
                    sc_s.n += 1

                @blk.vector
                def _(vector, qb=qb, sc0=sc0, dv0=dv0):
                    vector.wait_ge(sc_s.h, sc0 + 1)
                    vector.tensor_scalar(out=v1[:], in0=m0f[:], scalar1=0.5, scalar2=1.0,
                                         op0=ALU.mult, op1=ALU.add)
                    vector.tensor_tensor(out=v1[:], in0=v1[:], in1=p2[:], op=ALU.mult)
                    vector.tensor_single_scalar(out=cm[:], in_=idx[:], scalar=1, op=ALU.is_lt)
                    vector.tensor_copy(out=p2[:], in_=cm[:])
                    vector.tensor_tensor(out=v0[:], in0=v0[:], in1=v1[:], op=ALU.subtract)
                    vector.tensor_tensor(out=v0[:], in0=v0[:], in1=p2[:], op=ALU.mult)
                    vector.tensor_tensor(out=v1[:], in0=v1[:], in1=v0[:], op=ALU.add)
                    vector.tensor_scalar(out=sf[:], in0=sf[:], scalar1=-2.0, scalar2=1.0,
                                         op0=ALU.mult, op1=ALU.add)
                    vector.tensor_tensor(out=v1[:], in0=v1[:], in1=sf[:], op=ALU.mult)
                    vector.tensor_tensor(out=vv[qb][:], in0=v1[:], in1=sce[:],
                                         op=ALU.mult).then_inc(dv.h, 1)
                    dv.n += 1

                @blk.tensor
                def _(tensor, u=u, qb=qb, dv0=dv0, sc0=sc0):
                    tensor.wait_ge(dv.h, dv0 + 2)
                    if u >= 1:
                        tensor.wait_ge(sc_s.h, sc0)
                    for kb in range(16):
                        inst = tensor.transpose(out=psB[:, kb*128:(kb+1)*128],
                                                in_=vv[qb][:, kb*128:(kb+1)*128],
                                                identity=ident_b[:])
                    inst.then_inc(pe.h, 1)
                    pe.n += 1

                @blk.scalar
                def _(scalar, u=u, qb=qb, pe0=pe0, sc0=sc0, st0=st0):
                    scalar.wait_ge(pe.h, pe0 + 1)
                    if u >= 1:
                        scalar.wait_ge(stq.h, 256 * u)
                    for kb in range(16):
                        inst = scalar.copy(out=ev[:, kb*128:(kb+1)*128],
                                           in_=psB[:, kb*128:(kb+1)*128])
                    inst.then_inc(sc_s.h, 1)
                    sc_s.n += 1

                @blk.gpsimd
                def _(gp, unit=unit, sc0=sc0, st0=st0):
                    gp.wait_ge(sc_s.h, sc0 + 2)
                    for kb in range(16):
                        gp.dma_start(out=unit["dst"][kb],
                                     in_=ev[:, kb*128:(kb+1)*128]).then_inc(stq.h, 16)
                    stq.n += 256

        with nc.Block() as blk:
            @blk.gpsimd
            def _(gp):
                gp.wait_ge(stq.h, stq.n)

def build_d2(nc, io):
    TT = 512
    ntt = TLOC // TT
    NHB = 8
    C0 = 0.7978845608028654
    with ExitStack() as ctx:
        sb = lambda *a: ctx.enter_context(nc.sbuf_tensor(*a))
        ps = lambda *a: ctx.enter_context(nc.psum_tensor(*a))
        ones_col = sb("d2_onc", [128, 1], BF16)
        ones_f32 = sb("d2_onf", [1, 128], F32)
        s1t = [sb(f"d2_s1{i}", [128, 16 * TT], BF16) for i in range(2)]
        wgt = [sb(f"d2_wg{i}", [128, 16 * 128], BF16) for i in range(2)]
        wut = [sb(f"d2_wu{i}", [128, 16 * 128], BF16) for i in range(2)]
        wdt = [sb(f"d2_wd{i}", [128, 16 * 128], BF16) for i in range(2)]
        gsb = sb("d2_gsb", [128, 2 * TT], BF16)
        gtmp = sb("d2_gtmp", [128, TT], BF16)
        gtmp2 = sb("d2_gtmp2", [128, 2 * TT], BF16)
        inter = sb("d2_inter", [128, 16 * TT], BF16)
        acc = sb("d2_acc", [128, 16 * TT], F32)
        rbf = sb("d2_rbf", [128, 16 * TT], BF16)
        sq = sb("d2_sq", [128, 16 * TT], BF16)
        outf = sb("d2_outf", [128, 16 * TT], F32)
        mrow = sb("d2_mrow", [1, TT], F32)
        vtmp = sb("d2_vtmp", [1, TT], F32)
        rstd = sb("d2_rstd", [1, TT], F32)
        pmb = [ps(f"d2_pm{i}", [128, TT], F32) for i in range(4)]
        st_ps = [ps(f"d2_st{i}", [1, TT], F32) for i in range(2)]
        bc_ps = [ps(f"d2_bc{i}", [128, TT], F32) for i in range(2)]

        ld = Sem(ctx, nc, "d2_ld")
        lw = Sem(ctx, nc, "d2_lw")
        mm = Sem(ctx, nc, "d2_mm")
        sgel = Sem(ctx, nc, "d2_sgel")
        sev = Sem(ctx, nc, "d2_sev")
        st = Sem(ctx, nc, "d2_st")

        def wsl(wap, colbase, kbase, bw):
            blk_i = colbase // bw
            cb = colbase % bw
            return wap[blk_i, kbase:kbase + 2048, cb:cb + 128].rearrange(
                "(kc p) m -> p kc m", p=128)

        with nc.Block() as blk:
            @blk.vector
            def _(vector):
                vector.memset(ones_col[:], 1.0)
                vector.memset(ones_f32[:], 1.0)

            @blk.sync
            def _(sync):
                for kc in range(16):
                    sync.dma_start(out=s1t[0][:, kc*TT:(kc+1)*TT],
                                   in_=io["s3"][kc, :, 0:TT]).then_inc(ld.h, 16)
                ld.n += 256

        for tt in range(ntt):
            cur = s1t[tt % 2]
            for hb in range(NHB):
                lw0 = lw.n
                mm0 = mm.n
                sev0 = sev.n
                sg0 = sgel.n
                with nc.Block() as blk:
                    @blk.sync
                    def _(sync, tt=tt, hb=hb, mm0=mm0):
                        for hm in range(16):
                            if hm >= 2:
                                sync.wait_ge(mm.h, mm0 + hm - 1)
                            sync.dma_start(out=wgt[hm % 2][:].rearrange("p (kc m) -> p kc m", m=128),
                                           in_=wsl(io["wg1_T"], hb*2048 + hm*128, 0, 2048)).then_inc(lw.h, 16)
                            sync.dma_start(out=wut[hm % 2][:].rearrange("p (kc m) -> p kc m", m=128),
                                           in_=wsl(io["wu1_T"], hb*2048 + hm*128, 0, 2048)).then_inc(lw.h, 16)
                        for dm in range(16):
                            if dm >= 2:
                                sync.wait_ge(mm.h, mm0 + 16 + dm - 1)
                            sync.dma_start(out=wdt[dm % 2][:].rearrange("p (kc m) -> p kc m", m=128),
                                           in_=wsl(io["wd1_T"], dm*128, hb*2048, 256)).then_inc(lw.h, 16)
                        lw.n += 48 * 16

                    @blk.tensor
                    def _(tensor, tt=tt, hb=hb, cur=cur, lw0=lw0, mm0=mm0, sev0=sev0):
                        for hm in range(16):
                            tensor.wait_ge(lw.h, lw0 + 32 * (hm + 1))
                            if hm >= 2:
                                tensor.wait_ge(sev.h, sev0 + hm - 1)
                            ga = pmb[2 * (hm % 2)]
                            ua = pmb[2 * (hm % 2) + 1]
                            for kc in range(16):
                                tensor.matmul(out=ga[:], lhsT=wgt[hm % 2][:, kc*128:(kc+1)*128],
                                              rhs=cur[:, kc*TT:(kc+1)*TT],
                                              start=(kc == 0), stop=(kc == 15))
                            for kc in range(16):
                                inst = tensor.matmul(out=ua[:], lhsT=wut[hm % 2][:, kc*128:(kc+1)*128],
                                                     rhs=cur[:, kc*TT:(kc+1)*TT],
                                                     start=(kc == 0), stop=(kc == 15))
                            inst.then_inc(mm.h, 1)
                        mm.n += 16
                        for dm in range(16):
                            tensor.wait_ge(lw.h, lw0 + 32 * 16 + 16 * (dm + 1))
                            tensor.wait_ge(sev.h, sev0 + 16)
                            if dm >= 2:
                                tensor.wait_ge(sev.h, sev0 + 16 + dm - 1)
                            pa = pmb[dm % 2]
                            for kc in range(16):
                                inst = tensor.matmul(out=pa[:], lhsT=wdt[dm % 2][:, kc*128:(kc+1)*128],
                                                     rhs=inter[:, kc*TT:(kc+1)*TT],
                                                     start=(kc == 0), stop=(kc == 15))
                            inst.then_inc(mm.h, 1)
                        mm.n += 16

                    @blk.scalar
                    def _(scalar, mm0=mm0, sg0=sg0, sev0=sev0):
                        for hm in range(16):
                            scalar.wait_ge(mm.h, mm0 + hm + 1)
                            if hm >= 1:
                                scalar.wait_ge(sev.h, sev0 + hm)
                            ga = pmb[2 * (hm % 2)]
                            sl = slice((hm % 2)*TT, (hm % 2 + 1)*TT)
                            scalar.copy(out=gsb[:, sl], in_=ga[:])
                            scalar.activation(out=gtmp[:], in_=ga[:], func=AF.Square).then_inc(sgel.h, 1)
                            scalar.wait_ge(sgel.h, sg0 + 3 * hm + 2)
                            scalar.activation(out=gtmp[:], in_=gtmp2[:, sl], func=AF.Tanh,
                                              scale=C0).then_inc(sgel.h, 1)
                        sgel.n += 32

                    @blk.vector
                    def _(vector, hb=hb, mm0=mm0, sg0=sg0, sev0=sev0):
                        for hm in range(16):
                            vector.wait_ge(sgel.h, sg0 + 3 * hm + 1)
                            sl = slice((hm % 2)*TT, (hm % 2 + 1)*TT)
                            vector.tensor_scalar(out=gtmp2[:, sl], in0=gtmp[:], scalar1=0.044715,
                                                 scalar2=1.0, op0=ALU.mult, op1=ALU.add)
                            vector.tensor_tensor(out=gtmp2[:, sl], in0=gtmp2[:, sl],
                                                 in1=gsb[:, sl], op=ALU.mult).then_inc(sgel.h, 1)
                            vector.wait_ge(sgel.h, sg0 + 3 * hm + 3)
                            vector.tensor_scalar(out=gtmp2[:, sl], in0=gtmp[:], scalar1=0.5,
                                                 scalar2=0.5, op0=ALU.mult, op1=ALU.add)
                            vector.tensor_tensor(out=gtmp2[:, sl], in0=gtmp2[:, sl],
                                                 in1=gsb[:, sl], op=ALU.mult)
                            ua = pmb[2 * (hm % 2) + 1]
                            vector.tensor_tensor(out=inter[:, hm*TT:(hm+1)*TT], in0=gtmp2[:, sl],
                                                 in1=ua[:], op=ALU.mult).then_inc(sev.h, 1)
                        sev.n += 16
                        sgel.n += 16
                        for dm in range(16):
                            vector.wait_ge(mm.h, mm0 + 16 + dm + 1)
                            pa = pmb[dm % 2]
                            if hb == 0:
                                vector.tensor_copy(out=acc[:, dm*TT:(dm+1)*TT], in_=pa[:]).then_inc(sev.h, 1)
                            else:
                                vector.tensor_tensor(out=acc[:, dm*TT:(dm+1)*TT],
                                                     in0=acc[:, dm*TT:(dm+1)*TT],
                                                     in1=pa[:], op=ALU.add).then_inc(sev.h, 1)
                        sev.n += 16

            lnt = {"r": acc, "rbf": rbf, "sq": sq, "out": outf, "ones_col": ones_col,
                   "ones_f32": ones_f32, "st_ps": st_ps, "bc_ps": bc_ps,
                   "mrow": mrow, "vtmp": vtmp, "rstd": rstd}
            with nc.Block() as blk:
                res0 = sev.n
                sev.n += 16
                ln = LNEmit(lnt, TT, mm, sev)

                @blk.sync
                def _(sync, tt=tt):
                    if tt + 1 < ntt:
                        for kc in range(16):
                            sync.dma_start(out=s1t[(tt+1) % 2][:, kc*TT:(kc+1)*TT],
                                           in_=io["s3"][kc, :, (tt+1)*TT:(tt+2)*TT]).then_inc(ld.h, 16)
                        ld.n += 256

                @blk.vector
                def _(vector, tt=tt, cur=cur, ln=ln):
                    if tt >= 1:
                        vector.wait_ge(st.h, 256 * tt)
                    for kc in range(16):
                        vector.tensor_tensor(out=acc[:, kc*TT:(kc+1)*TT],
                                             in0=acc[:, kc*TT:(kc+1)*TT],
                                             in1=cur[:, kc*TT:(kc+1)*TT], op=ALU.add).then_inc(sev.h, 1)
                    ln.vector(vector)

                @blk.scalar
                def _(scalar, ln=ln):
                    ln.scalar(scalar)

                @blk.tensor
                def _(tensor, ln=ln):
                    ln.tensor(tensor)

                @blk.gpsimd
                def _(gp, tt=tt, ln=ln):
                    gp.wait_ge(sev.h, ln.sev0 + 50)
                    for kc in range(16):
                        gp.dma_start(out=io["outf"][kc, :, tt*TT:(tt+1)*TT],
                                     in_=outf[:, kc*TT:(kc+1)*TT]).then_inc(st.h, 16)
                    st.n += 256

        with nc.Block() as blk:
            @blk.gpsimd
            def _(gp):
                gp.wait_ge(st.h, st.n)

def _ct(a, dt=np.float32):
    return np.ascontiguousarray(a, dtype=dt)


def kernel(**inputs):
    d = inputs
    x = np.asarray(d["L0_attn_x"], np.float32)
    xT = host_xT(x)

    projc = np.concatenate([
        np.asarray(d["L1_attn_qkv_w"], np.float32).T,
        np.asarray(d["L1_attn_g_w"], np.float32).T,
        np.concatenate([np.asarray(d["L1_attn_b_w"], np.float32).T,
                        np.zeros((2048, 16), np.float32),
                        np.asarray(d["L1_attn_gk_w"], np.float32).T,
                        np.zeros((2048, 16), np.float32)], axis=1)], axis=1)
    Wf = {
        "xt": (xT, 2048, TLOC, 1),
        "outw": (np.asarray(d["L0_attn_out_W"], np.float32).T, 512, 2048, 0),
        "wg": (np.asarray(d["L0_ffn_w_gate"], np.float32).T, 256, 8192, 0),
        "wu": (np.asarray(d["L0_ffn_w_up"], np.float32).T, 256, 8192, 0),
        "wd": (np.asarray(d["L0_ffn_w_down"], np.float32).T, 1024, 2048, 0),
        "pj": (projc, 256, 5184, 0),
        "ow": (np.asarray(d["L1_attn_o_w"], np.float32).T, 256, 2048, 0),
    }
    nc = mknc()
    pairs = []
    for name, (arr, r, c, ax) in Wf.items():
        src = nc.dram_tensor("s_" + name, [r, c], F32, kind="ExternalInput").ap()
        dst = nc.dram_tensor("d_" + name, [r, c], BF16, kind="ExternalOutput").ap()
        pairs.append((src, dst, r, c))
    build_cast(nc, pairs)
    in_maps = []
    for cc in range(NCORE):
        m = {}
        for name, (arr, r, c, ax) in Wf.items():
            if ax == 1:
                m["s_" + name] = _ct(arr[:, cc*c:(cc+1)*c])
            else:
                m["s_" + name] = _ct(arr[cc*r:(cc+1)*r, :])
        in_maps.append(m)
    res = run_bass_kernel_spmd(nc, in_maps, core_ids=list(range(NCORE)))
    WB = {}
    for name, (arr, r, c, ax) in Wf.items():
        WB[name] = np.concatenate([np.asarray(rr["d_" + name]) for rr in res.results], axis=ax)
    xT_bf3 = np.stack([np.asarray(rr["d_xt"]).reshape(NKC, 128, TLOC) for rr in res.results], axis=0)

    nc = mknc()
    io = {
        "xT_bf": nc.dram_tensor("xT_bf", [NCORE, NKC, 128, TLOC], BF16, kind="ExternalInput").ap(),
        "w_in_T": nc.dram_tensor("w_in_T", [2048, 1024], F32, kind="ExternalInput").ap(),
        "wx_T": nc.dram_tensor("wx_T", [512, 33], F32, kind="ExternalInput").ap(),
        "conv_w": nc.dram_tensor("conv_wD", [512, 4], F32, kind="ExternalInput").ap(),
        "conv_b": nc.dram_tensor("conv_bD", [512, 1], F32, kind="ExternalInput").ap(),
        "x_conv": nc.dram_tensor("x_conv", [NPC, 128, T], BF16, kind="ExternalOutput").ap(),
        "sz": nc.dram_tensor("szD", [NPC, 128, T], BF16, kind="ExternalOutput").ap(),
        "p_part": nc.dram_tensor("p_part", [33, T], F32, kind="ExternalOutput").ap(),
    }
    build_a1(nc, io)
    W_in = np.asarray(d["L0_attn_W_in"], np.float32)
    W_x = np.asarray(d["L0_attn_W_x"], np.float32)
    in_maps = []
    for cc in range(NCORE):
        rows = list(range(cc*512, (cc+1)*512)) + list(range(DI + cc*512, DI + (cc+1)*512))
        in_maps.append({
            "xT_bf": xT_bf3,
            "w_in_T": _ct(W_in[rows, :].T),
            "wx_T": _ct(W_x[:, cc*512:(cc+1)*512].T),
            "conv_wD": _ct(np.asarray(d["L0_attn_conv_w"], np.float32)[cc*512:(cc+1)*512, 0, :]),
            "conv_bD": _ct(np.asarray(d["L0_attn_conv_b"], np.float32)[cc*512:(cc+1)*512, None]),
        })
    res = run_bass_kernel_spmd(nc, in_maps, core_ids=list(range(NCORE)))
    x_conv = np.concatenate([np.asarray(r["x_conv"]).reshape(512, T) for r in res.results], axis=0)
    szv = np.concatenate([np.asarray(r["szD"]).reshape(512, T) for r in res.results], axis=0)
    p80 = np.stack([np.asarray(r["p_part"]) for r in res.results], axis=0)

    nc = mknc()
    io = {
        "x_conv": nc.dram_tensor("x_conv", [NPC, 128, T], BF16, kind="ExternalInput").ap(),
        "sz": nc.dram_tensor("szD", [NPC, 128, T], BF16, kind="ExternalInput").ap(),
        "p_all": nc.dram_tensor("p_all", [8, 33, T], F32, kind="ExternalInput").ap(),
        "A_log": nc.dram_tensor("A_logD", [512, 16], F32, kind="ExternalInput").ap(),
        "Dvec": nc.dram_tensor("DvecD", [512, 1], F32, kind="ExternalInput").ap(),
        "dt_bias": nc.dram_tensor("dt_biasD", [1, 4096], F32, kind="ExternalInput").ap(),
        "y_pre": nc.dram_tensor("y_pre", [NPC, 128, T], BF16, kind="ExternalOutput").ap(),
    }
    build_a3(nc, io)
    dtb = _ct(np.asarray(d["L0_attn_dt_bias"], np.float32).reshape(1, 4096))
    in_maps = []
    for cc in range(NCORE):
        sl = slice(cc*512, (cc+1)*512)
        in_maps.append({
            "x_conv": _ct(x_conv[sl].reshape(NPC, 128, T), BF16NP),
            "szD": _ct(szv[sl].reshape(NPC, 128, T), BF16NP),
            "p_all": p80,
            "A_logD": _ct(np.asarray(d["L0_attn_A_log"], np.float32)[sl]),
            "DvecD": _ct(np.asarray(d["L0_attn_D"], np.float32)[sl, None]),
            "dt_biasD": dtb,
        })
    res = run_bass_kernel_spmd(nc, in_maps, core_ids=list(range(NCORE)))
    y_pre = np.concatenate([np.asarray(r["y_pre"]).reshape(512, T) for r in res.results], axis=0)

    nc = mknc()
    io = {
        "y_pre_all": nc.dram_tensor("y_pre_all", [32, 128, T], BF16, kind="ExternalInput").ap(),
        "xT_c": nc.dram_tensor("xT_c", [2048, TLOC], F32, kind="ExternalInput").ap(),
        "outw_T": nc.dram_tensor("outw_Td", [4096, 2048], BF16, kind="ExternalInput").ap(),
        "wg_T": nc.dram_tensor("wg_Td", [2048, 8192], BF16, kind="ExternalInput").ap(),
        "wu_T": nc.dram_tensor("wu_Td", [2048, 8192], BF16, kind="ExternalInput").ap(),
        "wd_T": nc.dram_tensor("wd_Td", [8192, 2048], BF16, kind="ExternalInput").ap(),
        "s1_d": nc.dram_tensor("s1_d", [16, 128, TLOC], BF16, kind="ExternalOutput").ap(),
        "s2": nc.dram_tensor("s2", [16, 128, TLOC], BF16, kind="ExternalOutput").ap(),
    }
    build_b(nc, io)
    ypre3 = y_pre.reshape(32, 128, T)
    in_maps = []
    for cc in range(NCORE):
        in_maps.append({
            "y_pre_all": _ct(ypre3, BF16NP),
            "xT_c": _ct(xT[:, cc*TLOC:(cc+1)*TLOC]),
            "outw_Td": WB["outw"], "wg_Td": WB["wg"],
            "wu_Td": WB["wu"], "wd_Td": WB["wd"],
        })
    res = run_bass_kernel_spmd(nc, in_maps, core_ids=list(range(NCORE)))
    s2 = np.concatenate([np.asarray(r["s2"]).reshape(2048, TLOC) for r in res.results], axis=1)

    nc = mknc()
    io = {
        "s2_h": nc.dram_tensor("s2_h", [16, 128, TLOC + 3], BF16, kind="ExternalInput").ap(),
        "projw_T": nc.dram_tensor("projw_T", [2048, 5184], BF16, kind="ExternalInput").ap(),
        "conv_w": nc.dram_tensor("conv1w", [3072, 4], F32, kind="ExternalInput").ap(),
        "qn": nc.dram_tensor("qn", [16, 128, TLOC], BF16, kind="ExternalOutput").ap(),
        "kn": nc.dram_tensor("kn", [4, 128, TLOC], BF16, kind="ExternalOutput").ap(),
        "vv": nc.dram_tensor("vvD", [4, 128, TLOC], BF16, kind="ExternalOutput").ap(),
        "glog": nc.dram_tensor("glogD", [16, TLOC], F32, kind="ExternalOutput").ap(),
        "beta": nc.dram_tensor("betaD", [16, TLOC], F32, kind="ExternalOutput").ap(),
        "gg": nc.dram_tensor("ggD", [16, 128, TLOC], BF16, kind="ExternalOutput").ap(),
    }
    build_c1(nc, io)
    s2_3 = s2.reshape(16, 128, T)
    cw1 = _ct(np.asarray(d["L1_attn_conv_w"], np.float32))
    in_maps = []
    for cc in range(NCORE):
        h = np.zeros((16, 128, TLOC + 3), BF16NP)
        h[:, :, 3:] = s2_3[:, :, cc*TLOC:(cc+1)*TLOC]
        if cc % 4 != 0:
            h[:, :, 0:3] = s2_3[:, :, cc*TLOC-3:cc*TLOC]
        in_maps.append({"s2_h": h, "projw_T": WB["pj"], "conv1w": cw1})
    res = run_bass_kernel_spmd(nc, in_maps, core_ids=list(range(NCORE)))
    qn = np.concatenate([np.asarray(r["qn"]) for r in res.results], axis=2)
    kn = np.concatenate([np.asarray(r["kn"]) for r in res.results], axis=2)
    vvv = np.concatenate([np.asarray(r["vvD"]) for r in res.results], axis=2)
    glog = np.concatenate([np.asarray(r["glogD"]) for r in res.results], axis=1)
    beta = np.concatenate([np.asarray(r["betaD"]) for r in res.results], axis=1)
    gg = np.concatenate([np.asarray(r["ggD"]) for r in res.results], axis=2)

    nc = mknc()
    io = {
        "qg": nc.dram_tensor("qg", [128, 128, TLOC], BF16, kind="ExternalInput").ap(),
        "kg": nc.dram_tensor("kg", [32, 128, TLOC], BF16, kind="ExternalInput").ap(),
        "vg": nc.dram_tensor("vg", [32, 128, TLOC], BF16, kind="ExternalInput").ap(),
        "glbt": nc.dram_tensor("glbt", [256, TLOC], F32, kind="ExternalInput").ap(),
        "ooq": nc.dram_tensor("ooq", [4, 4, 128, TLOC], BF16, kind="ExternalOutput").ap(),
    }
    build_c2(nc, io)
    qg_h = _ct(qn.reshape(16, 128, NCORE, TLOC).transpose(2, 0, 1, 3).reshape(128, 128, TLOC), BF16NP)
    kg_h = _ct(kn.reshape(4, 128, NCORE, TLOC).transpose(2, 0, 1, 3).reshape(32, 128, TLOC), BF16NP)
    vg_h = _ct(vvv.reshape(4, 128, NCORE, TLOC).transpose(2, 0, 1, 3).reshape(32, 128, TLOC), BF16NP)
    glbt_h = np.concatenate([glog.reshape(16, NCORE, TLOC), beta.reshape(16, NCORE, TLOC)],
                            axis=0)
    glbt_h = _ct(glbt_h.transpose(1, 0, 2).reshape(256, TLOC))
    in_maps = [{"qg": qg_h, "kg": kg_h, "vg": vg_h, "glbt": glbt_h} for _ in range(NCORE)]
    res = run_bass_kernel_spmd(nc, in_maps, core_ids=list(range(NCORE)))
    og_h = np.stack([np.asarray(r["ooq"]).reshape(16, 128, TLOC) for r in res.results],
                    axis=0).reshape(128, 128, TLOC)

    nc = mknc()
    io = {
        "og": nc.dram_tensor("ogD", [128, 128, TLOC], BF16, kind="ExternalInput").ap(),
        "gg": nc.dram_tensor("ggD", [16, 128, TLOC], BF16, kind="ExternalInput").ap(),
        "s2f": nc.dram_tensor("s2f", [16, 128, TLOC], BF16, kind="ExternalInput").ap(),
        "ow_T": nc.dram_tensor("ow_Td", [2048, 2048], BF16, kind="ExternalInput").ap(),
        "onw": nc.dram_tensor("onwD", [128, 1], F32, kind="ExternalInput").ap(),
        "s3": nc.dram_tensor("s3", [16, 128, TLOC], BF16, kind="ExternalOutput").ap(),
    }
    build_c3(nc, io)
    onw = _ct(np.asarray(d["L1_attn_out_norm_w"], np.float32)[:, None])
    in_maps = []
    for cc in range(NCORE):
        sl = slice(cc*TLOC, (cc+1)*TLOC)
        in_maps.append({
            "ogD": og_h,
            "ggD": _ct(gg[:, :, sl], BF16NP),
            "s2f": _ct(s2_3[:, :, sl], BF16NP),
            "ow_Td": WB["ow"],
            "onwD": onw,
        })
    res = run_bass_kernel_spmd(nc, in_maps, core_ids=list(range(NCORE)))
    s3 = np.concatenate([np.asarray(r["s3"]) for r in res.results], axis=2)

    nc = mknc()
    gpk = nc.dram_tensor("gpk", [2048, 1024], I32, kind="ExternalInput").ap()
    gsc = nc.dram_tensor("gsc", [2048, 64], F32, kind="ExternalInput").ap()
    upk = nc.dram_tensor("upk", [2048, 1024], I32, kind="ExternalInput").ap()
    usc = nc.dram_tensor("usc", [2048, 64], F32, kind="ExternalInput").ap()
    dpk = nc.dram_tensor("dpk", [256, 8192], I32, kind="ExternalInput").ap()
    dsc = nc.dram_tensor("dsc", [256, 512], F32, kind="ExternalInput").ap()
    wgs = nc.dram_tensor("wgs", [2048, 2048], BF16, kind="ExternalOutput").ap()
    wus = nc.dram_tensor("wus", [2048, 2048], BF16, kind="ExternalOutput").ap()
    wds = nc.dram_tensor("wds", [16384, 256], BF16, kind="ExternalOutput").ap()
    units = []
    for t in range(16):
        units.append({"pk": gpk[t*128:(t+1)*128, :], "sc": gsc[t*128:(t+1)*128, :],
                      "dst": [wgs[kb*128:(kb+1)*128, t*128:(t+1)*128] for kb in range(16)]})
    for t in range(16):
        units.append({"pk": upk[t*128:(t+1)*128, :], "sc": usc[t*128:(t+1)*128, :],
                      "dst": [wus[kb*128:(kb+1)*128, t*128:(t+1)*128] for kb in range(16)]})
    for t in range(2):
        for j in range(8):
            units.append({"pk": dpk[t*128:(t+1)*128, j*1024:(j+1)*1024],
                          "sc": dsc[t*128:(t+1)*128, j*64:(j+1)*64],
                          "dst": [wds[j*2048 + kb*128: j*2048 + (kb+1)*128,
                                      t*128:(t+1)*128] for kb in range(16)]})
    build_d1(nc, units)
    in_maps = []
    for cc in range(NCORE):
        in_maps.append({
            "gpk": _ct(np.asarray(d["L1_ffn_gate_W_packed"])[cc*2048:(cc+1)*2048], np.int32),
            "gsc": _ct(np.asarray(d["L1_ffn_gate_scales"], np.float32)[cc*2048:(cc+1)*2048]),
            "upk": _ct(np.asarray(d["L1_ffn_up_W_packed"])[cc*2048:(cc+1)*2048], np.int32),
            "usc": _ct(np.asarray(d["L1_ffn_up_scales"], np.float32)[cc*2048:(cc+1)*2048]),
            "dpk": _ct(np.asarray(d["L1_ffn_down_W_packed"])[cc*256:(cc+1)*256], np.int32),
            "dsc": _ct(np.asarray(d["L1_ffn_down_scales"], np.float32)[cc*256:(cc+1)*256]),
        })
    res = run_bass_kernel_spmd(nc, in_maps, core_ids=list(range(NCORE)))
    wg1 = np.stack([np.asarray(r["wgs"]) for r in res.results], axis=0)
    wu1 = np.stack([np.asarray(r["wus"]) for r in res.results], axis=0)
    wd1 = np.stack([np.asarray(r["wds"]) for r in res.results], axis=0)

    nc = mknc()
    io = {
        "s3": nc.dram_tensor("s3D", [16, 128, TLOC], BF16, kind="ExternalInput").ap(),
        "wg1_T": nc.dram_tensor("wg1_T", [8, 2048, 2048], BF16, kind="ExternalInput").ap(),
        "wu1_T": nc.dram_tensor("wu1_T", [8, 2048, 2048], BF16, kind="ExternalInput").ap(),
        "wd1_T": nc.dram_tensor("wd1_T", [8, 16384, 256], BF16, kind="ExternalInput").ap(),
        "outf": nc.dram_tensor("outf", [16, 128, TLOC], F32, kind="ExternalOutput").ap(),
    }
    build_d2(nc, io)
    in_maps = []
    for cc in range(NCORE):
        sl = slice(cc*TLOC, (cc+1)*TLOC)
        in_maps.append({
            "s3D": _ct(s3[:, :, sl], BF16NP),
            "wg1_T": wg1, "wu1_T": wu1, "wd1_T": wd1,
        })
    res = run_bass_kernel_spmd(nc, in_maps, core_ids=list(range(NCORE)))
    outf = np.concatenate([np.asarray(r["outf"]) for r in res.results], axis=2)
    full = outf.reshape(2048, T)
    out = full.reshape(2048, B, S).transpose(1, 2, 0)
    return _ct(out)
